# version 03
# baseline (speedup 1.0000x reference)
import numpy as np
import ml_dtypes
from contextlib import ExitStack

import concourse.bass as bass
import concourse.mybir as mybir
from concourse.bass_utils import run_bass_kernel_spmd

F32 = mybir.dt.float32
BF16 = mybir.dt.bfloat16
Alu = mybir.AluOpType
Act = mybir.ActivationFunctionType
AX = mybir.AxisListType
BF = ml_dtypes.bfloat16

S = 2048
D = 1024
NT = S // 128
ND = D // 128
NH = 16
BIAS_S = 32
NEG = -3.0e38

STAGES = ("l0a", "l0f", "l1a", "l1f")


def _split_const(a):
    hi = a.astype(BF)
    lo = (a - hi.astype(np.float32)).astype(BF)
    return hi, lo


def _interp_matrix(n, s):
    pos = np.arange(n, dtype=np.float32) * np.float32((s - 1) / (n - 1))
    i0 = np.floor(pos).astype(np.int32)
    i1 = np.minimum(i0 + 1, s - 1)
    frac = (pos - i0.astype(np.float32)).astype(np.float32)
    M = np.zeros((n, s), dtype=np.float32)
    np.add.at(M, (np.arange(n), i0), (1.0 - frac))
    np.add.at(M, (np.arange(n), i1), frac)
    return M


class Prog:
    ENGS = ("ten", "vec", "scl", "gps", "syn")
    SEMS = ("T", "V", "S", "G", "DIO", "DWT", "DGH", "DAU", "DSL", "DS2", "DSS", "DGR", "DRR", "DB4", "DO")
    DMA_SEMS = frozenset(SEMS[4:])

    def __init__(self):
        self.ops = {e: [] for e in self.ENGS}
        self.ctr = {s: 0 for s in self.SEMS}
        self.waited = {e: {s: 0 for s in self.SEMS} for e in self.ENGS}
        self.own = {"ten": "T", "vec": "V", "scl": "S", "gps": "G", "syn": None}
        self.sem = {}
        self.dirty = {e: False for e in self.ENGS}
        self.last_real = {e: None for e in self.ENGS}
        self.last_mark = {e: None for e in self.ENGS}
        self.chain = {"vec": True, "scl": True, "gps": True}
        self.chain_val = {e: 0 for e in self.ENGS}

    def add(self, eng, fn):
        if self.chain.get(eng):
            sname = self.own[eng]
            prevv = self.chain_val[eng]
            if prevv:
                self.ops[eng].append(lambda e, sn=sname, v=prevv: e.wait_ge(self.sem[sn], v))
            self.ctr[sname] += 1
            val = self.ctr[sname]
            self.ops[eng].append(lambda e, fn=fn, sn=sname: fn(e).then_inc(self.sem[sn], 1))
            self.chain_val[eng] = val
            self.last_mark[eng] = (sname, val)
            self.dirty[eng] = False
            return
        self.ops[eng].append(fn)
        self.last_real[eng] = len(self.ops[eng]) - 1
        self.dirty[eng] = True

    def mark(self, eng):
        if self.chain.get(eng):
            return self.last_mark[eng]
        if not self.dirty[eng]:
            return self.last_mark[eng]
        sname = self.own[eng]
        self.ctr[sname] += 1
        val = self.ctr[sname]
        i = self.last_real[eng]
        prev = self.ops[eng][i]
        self.ops[eng][i] = lambda e, prev=prev, sn=sname: prev(e).then_inc(self.sem[sn], 1)
        self.dirty[eng] = False
        self.last_mark[eng] = (sname, val)
        return (sname, val)

    def wait(self, eng, dep):
        if dep is None:
            return
        sname, val = dep
        if sname in self.DMA_SEMS and val < self.ctr[sname]:
            val = self.ctr[sname]
        if self.own[eng] == sname:
            return
        if self.waited[eng][sname] >= val:
            return
        self.waited[eng][sname] = val
        self.ops[eng].append(lambda e, sn=sname, v=val: e.wait_ge(self.sem[sn], v))

    def waits(self, eng, deps):
        for d in deps:
            self.wait(eng, d)

    def wait_force(self, eng, dep):
        if dep is None:
            return
        sname, val = dep
        self.ops[eng].append(lambda e, sn=sname, v=val: e.wait_ge(self.sem[sn], v))

    def dma(self, eng, out, in_, q=None, slow=False):
        assert q is not None, "specify a DMA stream"
        qname = q
        self.ctr[qname] += 16
        val = self.ctr[qname]
        self.ops[eng].append(
            lambda e, out=out, in_=in_, qn=qname, slow=slow: e.dma_start(
                out=out, in_=in_, allow_slow_non_contiguous=slow).then_inc(self.sem[qn], 16))
        return (qname, val)

    def barrier(self):
        marks = [self.mark(e) for e in ("ten", "vec", "scl", "gps")]
        for e in self.ENGS:
            for m in marks:
                self.wait(e, m)

    def emit(self, block):
        def mk(name):
            def body(eng):
                for fn in self.ops[name]:
                    fn(eng)
            return body
        block.sync(mk("syn"))
        block.tensor(mk("ten"))
        block.vector(mk("vec"))
        block.scalar(mk("scl"))
        block.gpsimd(mk("gps"))


class Builder:
    def __init__(self, stages):
        self.stages = stages
        nc = bass.Bass("TRN2", target_bir_lowering=False)
        self.nc = nc
        self.p = Prog()
        self.ctx = ExitStack()

        self.xin = nc.dram_tensor("x_in", [S, D], F32, kind="ExternalInput").ap()
        self.xout = nc.dram_tensor("x_out", [S, D], F32, kind="ExternalOutput").ap()
        WSHAPES = {
            "L0_attn_Wq": (D, D), "L0_attn_bq": (D,), "L0_attn_Wk": (D, D), "L0_attn_bk": (D,),
            "L0_attn_Wv": (D, D), "L0_attn_bv": (D,), "L0_attn_Wo": (D, D), "L0_attn_bo": (D,),
            "L0_attn_gamma": (D,), "L0_attn_beta": (D,), "L0_attn_rel_bias": (NH, BIAS_S, BIAS_S),
            "L0_ffn_w_up": (4096, D), "L0_ffn_w_down": (D, 4096),
            "L1_attn_in_proj_w": (4096, D), "L1_attn_conv_w": (2048, 256, 4),
            "L1_attn_conv_b": (2048,), "L1_attn_ln_g": (2048,), "L1_attn_ln_b": (2048,),
            "L1_attn_fc_w": (D, 2048), "L1_attn_fc_b": (D,), "L1_attn_cproj_w": (2048, D),
            "L1_attn_cproj_b": (2048,), "L1_attn_out_w": (D, 2048),
            "L1_ffn_g1": (D,), "L1_ffn_w_qkv": (D, 3 * D), "L1_ffn_w_o": (D, D),
            "L1_ffn_g2": (D,), "L1_ffn_w_router": (D, 8),
            "L1_ffn_w1": (8, D, 2048), "L1_ffn_w2": (8, 2048, D),
        }
        self.W = {k: nc.dram_tensor(k, list(v), F32, kind="ExternalInput").ap()
                  for k, v in WSHAPES.items()}

        self.ident_bf_c = nc.inline_tensor(np.eye(128, dtype=np.float32).astype(BF), "c_identbf")
        self.ident_f_c = nc.inline_tensor(np.eye(128, dtype=np.float32), "c_identf")
        self.ones_c = nc.inline_tensor(np.ones((1, 128), np.float32), "c_ones")
        self.onesbf_c = nc.inline_tensor(np.ones((1, 128), np.float32).astype(BF), "c_onesbf")
        self.onescol_c = nc.inline_tensor(np.ones((128, 1), np.float32), "c_onescol")
        self.mask_c = nc.inline_tensor(np.triu(np.full((128, 128), NEG, np.float32), k=1), "c_mask")
        M = _interp_matrix(S, BIAS_S)
        Mq = M.copy()
        Mq[-1, :] = 0.0
        mqh, mql = _split_const(np.ascontiguousarray(Mq.T))
        self.mqt_h_c = nc.inline_tensor(mqh, "c_mqt_h")
        self.mqt_l_c = nc.inline_tensor(mql, "c_mqt_l")
        self.mt_c = nc.inline_tensor(np.ascontiguousarray(M.T), "c_mt")
        theta = (10000.0 ** (-2.0 * (np.arange(32) / 64.0))).astype(np.float64)
        tpos = np.arange(S, dtype=np.float64)
        fr = tpos[None, :] * theta[:, None]
        cosd = np.cos(fr).astype(np.float32)
        sind = np.sin(fr).astype(np.float32)
        C1 = np.zeros((128, S), np.float32)
        C2 = np.zeros((128, S), np.float32)
        for p_ in range(128):
            f = (p_ % 64) // 2
            C1[p_] = cosd[f]
            C2[p_] = (-sind[f]) if (p_ % 2 == 0) else sind[f]
        self.rope1_c = nc.inline_tensor(C1, "c_rope1")
        self.rope2_c = nc.inline_tensor(C2, "c_rope2")
        self.irev_c = nc.inline_tensor(
            np.broadcast_to((8.0 - np.arange(8, dtype=np.float32)), (128, 8)).copy(), "c_irev")
        self.ecol_c = nc.inline_tensor((np.arange(128, dtype=np.float32) // 16)[:, None].copy(), "c_ecol")
        G8 = np.zeros((128, 8), np.float32)
        for p_ in range(128):
            G8[p_, p_ // 16] = 1.0
        self.g8_c = nc.inline_tensor(G8, "c_g8")
        self.g8t_c = nc.inline_tensor(np.ascontiguousarray(G8.T), "c_g8t")

        e = self.ctx.enter_context
        sb = lambda nm, shp, dt=BF16: e(nc.sbuf_tensor(nm, shp, dt))
        self.x_sb = sb("x_sb", [128, NT * D], F32)
        self.xsl_h = sb("xsl_h", [128, ND * 512])
        self.xsl_l = sb("xsl_l", [128, ND * 512])
        self.xnT_scr_h = nc.dram_tensor("xnT_scr_h", [128, ND * S], BF16, kind="Internal").ap()
        self.xnT_scr_l = nc.dram_tensor("xnT_scr_l", [128, ND * S], BF16, kind="Internal").ap()
        self.wupT_scr = nc.dram_tensor("wupT_scr", [128, 32 * 1024], BF16, kind="Internal").ap()
        self.wdownT_scr = nc.dram_tensor("wdownT_scr", [128, 32 * 1024], BF16, kind="Internal").ap()
        self.x_scr = nc.dram_tensor("x_scr", [128, NT * D], F32, kind="Internal").ap()
        self.m_loc = nc.dram_tensor("m_loc", [2, S], F32, kind="Internal").ap()
        self.e_loc = nc.dram_tensor("e_loc", [2, S], F32, kind="Internal").ap()
        self.m_all = nc.dram_tensor("m_all", [16, S], F32, kind="Internal").ap()
        self.e_all = nc.dram_tensor("e_all", [16, S], F32, kind="Internal").ap()
        self.u32 = sb("u32", [128, D], F32)
        self.sth = sb("sth", [128, D])
        self.stl = sb("stl", [128, D])
        for nm in ("sm", "sq", "mu", "rho", "mb", "tt1", "tt2", "r1s", "mb1s"):
            setattr(self, nm, sb(nm, [128, NT], F32))
        self.wld = sb("wld", [128, 1024], F32)
        self.wld_h = sb("wld_h", [128, 1024])
        self.wld_l = sb("wld_l", [128, 1024])
        self.wT_h = sb("wT_h", [128, 3 * 1024])
        self.wT_l = sb("wT_l", [128, 3 * 1024])
        self.g32 = sb("g32", [128, 512], F32)
        self.relbT = sb("relbT", [32, 128], F32)
        self.rb_st = sb("rb_st", [32, 128], F32)
        self.mneg = sb("mneg", [128, 1], F32)
        self.lsum = sb("lsum", [128, 1], F32)
        self.linv = sb("linv", [128, 1], F32)
        self.AR = sb("arena", [128, 43008])
        self.qt_h = self._av(0, 4096)
        self.qt_l = self._av(4096, 4096)
        self.kt_h = self._av(8192, 4096)
        self.kt_l = self._av(12288, 4096)
        self.vt_h = self._av(16384, 4096)
        self.vt_l = self._av(20480, 4096)
        self.qaug_h = self._av(24576, 4096, parts=96)
        self.qaug_l = self._av(28672, 4096, parts=96)
        self.kaug_h = self._av(32768, 4096, parts=96)
        self.kaug_l = self._av(36864, 4096, parts=96)
        self.b4_h = self._av(40960, 4096)
        self.b4_l = self._av(45056, 4096)
        self.p32 = self._av(49152, 8192, F32)
        self.gam_rep = self._av(49152, 4096, F32)
        self.bet_rep = self._av(53248, 4096, F32)
        self.ph = self._av(57344, 4096)
        self.pl = self._av(61440, 4096)
        self.pth = self._av(65536, 2048)
        self.ptl = self._av(67584, 2048)
        self.ogh = self._av(69632, 4096)
        self.ogl = self._av(73728, 4096)
        self.brow_h = self._av(77824, 4096, parts=1)
        self.brow_l = self._av(81920, 4096, parts=1)
        self.ident_sb = sb("ident_sb", [128, 128])
        self.identf_sb = sb("identf_sb", [128, 128], F32)
        self.ones_sb = sb("ones_sb", [1, 128], F32)
        self.onescol_sb = sb("onescol_sb", [128, 1], F32)
        self.onesb_sb = sb("onesb_sb", [1, 128])
        self.mask_sb = sb("mask_sb", [128, 128], F32)
        self.bias_fm = sb("bias_fm", [128, 4 * ND], F32)
        self.bias2 = sb("bias2", [128, 72], F32)
        self.irev_sb = sb("irev_sb", [128, 8], F32)
        self.ecol_sb = sb("ecol_sb", [128, 1], F32)
        self.g8_sb = sb("g8_sb", [128, 8], F32)
        self.g8t_sb = sb("g8t_sb", [8, 128], F32)
        self.grow = sb("grow", [1, 512], F32)

        self.PS_A = e(nc.psum_tensor("PS_A", [128, 2048], F32))
        self.T_PS = e(nc.psum_tensor("T_PS", [128, 2048], BF16))
        self.PS_C = e(nc.psum_tensor("PS_C", [128, 512], F32))
        self.PS_D = e(nc.psum_tensor("PS_D", [128, 512], F32))

        self.pfree = {k: None for k in ("A0", "A1", "A2", "A3", "T", "C", "D")}
        self.sfree = {}

        self.xh_v = self.xsl_h[:].rearrange("p (c s) -> p c s", c=ND)
        self.xl_v = self.xsl_l[:].rearrange("p (c s) -> p c s", c=ND)

    DBG = False

    def dbg_dump(self, name, ap, dep=None):
        if not self.DBG:
            return
        self.p.waits("syn", [dep, ("DB4", self.p.ctr["DB4"])])
        t = self.nc.dram_tensor("dbg_" + name, list(ap.shape), ap.dtype, kind="ExternalOutput").ap()
        dd = self.p.dma("syn", out=t, in_=ap, q="DB4")
        for eng in ("vec", "scl", "ten"):
            self.p.wait(eng, dd)

    def _av(self, off_bytes, nbytes, dt=None, parts=128, p0=0):
        v = self.AR[p0:p0 + parts, off_bytes // 2:(off_bytes + nbytes) // 2]
        if dt is F32:
            v = v.bitcast(F32)
        return v

    def init_loads(self):
        p = self.p
        for t in range(NT):
            p.dma("syn", out=self.x_sb[:, t * D:(t + 1) * D], in_=self.xin[t * 128:(t + 1) * 128, :], q="DIO")
        p.dma("syn", out=self.ident_sb[:], q="DIO", in_=self.ident_bf_c.ap())
        p.dma("syn", out=self.identf_sb[:], q="DIO", in_=self.ident_f_c.ap())
        p.dma("syn", out=self.ones_sb[:], q="DIO", in_=self.ones_c.ap())
        p.dma("syn", out=self.onescol_sb[:], q="DIO", in_=self.onescol_c.ap())
        p.dma("syn", out=self.onesb_sb[:], q="DIO", in_=self.onesbf_c.ap())
        p.dma("syn", out=self.mask_sb[:], q="DIO", in_=self.mask_c.ap())
        p.dma("syn", out=self.qaug_h[64:96, :], in_=self.mqt_h_c.ap(), q="DIO")
        dep = p.dma("syn", out=self.qaug_l[64:96, :], in_=self.mqt_l_c.ap(), q="DIO")
        for eng in ("ten", "vec", "scl", "gps"):
            p.wait(eng, dep)

    def replicate_row(self, dram_vec, dst, n):
        p = self.p
        for c0 in range(0, n, 512):
            w_ = min(512, n - c0)
            p.waits("syn", [self.sfree.get("grow")])
            dd = p.dma("syn", out=self.grow[:, 0:w_],
                       in_=dram_vec[c0:c0 + w_].rearrange("d -> () d"), q="DGR")
            p.waits("ten", [self.pfree["C"], dd])
            p.add("ten", lambda t, w_=w_: t.matmul(
                self.PS_C[:, 0:w_], self.ones_sb[:, :], self.grow[:, 0:w_],
                start=True, stop=True))
            d = p.mark("ten")
            self.sfree["grow"] = d
            p.wait("vec", d)
            p.add("vec", lambda v, c0=c0, w_=w_: v.tensor_copy(dst[:, c0:c0 + w_], self.PS_C[:, 0:w_]))
            self.pfree["C"] = p.mark("vec")

    def split_row(self, dram_vec, n):
        p = self.p
        d = None
        for c0 in range(0, n, 512):
            w_ = min(512, n - c0)
            p.waits("syn", [self.sfree.get("grow")])
            dd = p.dma("syn", out=self.grow[:, 0:w_],
                       in_=dram_vec[c0:c0 + w_].rearrange("d -> () d"), q="DGR")
            p.waits("vec", [dd])
            p.add("vec", lambda v, c0=c0, w_=w_: v.tensor_copy(
                self.brow_h[:, c0:c0 + w_], self.grow[:, 0:w_]))
            p.add("vec", lambda v, c0=c0, w_=w_: v.tensor_tensor(
                self.brow_l[:, c0:c0 + w_], self.grow[:, 0:w_], self.brow_h[:, c0:c0 + w_],
                Alu.subtract))
            d = p.mark("vec")
            self.sfree["grow"] = d
        return d

    def stats_pass(self):
        p = self.p
        sd = None
        for t in range(NT):
            xs = self.x_sb[:, t * D:(t + 1) * D]
            p.add("vec", lambda v, xs=xs, t=t: v.reduce_sum(self.sm[:, t:t + 1], xs, axis=AX.X))
            p.wait_force("scl", sd)
            p.add("scl", lambda s, xs=xs, t=t: s.activation(
                self.u32[:], xs, Act.Square, accum_out=self.sq[:, t:t + 1]))
            sd = p.mark("scl")
        self._stats_dep = sd
        return (p.mark("vec"), sd)

    def ln_scalars(self, mode, eps=1e-5, eps2=1e-5):
        p = self.p
        p.wait("vec", getattr(self, "_stats_dep", None))
        if mode == "rms":
            p.add("vec", lambda v: v.tensor_scalar_mul(self.tt1[:], self.sq[:], 1.0 / D))
            p.add("vec", lambda v: v.tensor_scalar_add(self.tt1[:], self.tt1[:], eps))
            d = p.mark("vec")
            p.wait("scl", d)
            p.add("scl", lambda s: s.activation(self.tt1[:], self.tt1[:], Act.Sqrt))
            d = p.mark("scl")
            p.wait("vec", d)
            p.add("vec", lambda v: v.reciprocal(self.rho[:], self.tt1[:]))
            p.add("vec", lambda v: v.tensor_scalar_mul(self.mb[:], self.rho[:], 0.0))
            self._ln_dep = p.mark("vec")
            return self._ln_dep
        p.add("vec", lambda v: v.tensor_scalar_mul(self.mu[:], self.sm[:], 1.0 / D))
        p.add("vec", lambda v: v.tensor_scalar_mul(self.tt1[:], self.sq[:], 1.0 / D))
        p.add("vec", lambda v: v.tensor_tensor(self.tt2[:], self.mu[:], self.mu[:], Alu.mult))
        p.add("vec", lambda v: v.tensor_tensor(self.tt1[:], self.tt1[:], self.tt2[:], Alu.subtract))
        p.add("vec", lambda v: v.tensor_scalar_add(self.tt2[:], self.tt1[:], eps))
        d = p.mark("vec")
        p.wait("scl", d)
        p.add("scl", lambda s: s.activation(self.tt2[:], self.tt2[:], Act.Sqrt))
        d = p.mark("scl")
        p.wait("vec", d)
        p.add("vec", lambda v: v.reciprocal(self.tt2[:], self.tt2[:]))
        p.add("vec", lambda v: v.tensor_copy(self.r1s[:], self.tt2[:]))
        p.add("vec", lambda v: v.tensor_tensor(self.mb1s[:], self.mu[:], self.r1s[:], Alu.mult))
        p.add("vec", lambda v: v.tensor_scalar_mul(self.mb1s[:], self.mb1s[:], -1.0))
        if mode == "lnln":
            p.add("vec", lambda v: v.tensor_tensor(self.rho[:], self.tt2[:], self.tt2[:], Alu.mult))
            p.add("vec", lambda v: v.tensor_tensor(self.rho[:], self.rho[:], self.tt1[:], Alu.mult))
            p.add("vec", lambda v: v.tensor_scalar_add(self.rho[:], self.rho[:], eps2))
            d = p.mark("vec")
            p.wait("scl", d)
            p.add("scl", lambda s: s.activation(self.rho[:], self.rho[:], Act.Sqrt))
            d = p.mark("scl")
            p.wait("vec", d)
            p.add("vec", lambda v: v.reciprocal(self.rho[:], self.rho[:]))
            p.add("vec", lambda v: v.tensor_tensor(self.rho[:], self.rho[:], self.tt2[:], Alu.mult))
        elif mode == "ln_rms":
            d = p.mark("vec")
            p.wait("scl", d)
            p.add("scl", lambda s: s.activation(self.tt1[:], self.tt1[:], Act.Sqrt))
            d = p.mark("scl")
            p.wait("vec", d)
            p.add("vec", lambda v: v.tensor_tensor(self.tt1[:], self.tt1[:], self.tt2[:], Alu.mult))
            p.add("vec", lambda v: v.tensor_scalar_add(self.tt1[:], self.tt1[:], 1e-8))
            p.add("vec", lambda v: v.reciprocal(self.tt1[:], self.tt1[:]))
            p.add("vec", lambda v: v.tensor_tensor(self.rho[:], self.tt1[:], self.tt2[:], Alu.mult))
        else:
            p.add("vec", lambda v: v.tensor_copy(self.rho[:], self.tt2[:]))
        p.add("vec", lambda v: v.tensor_tensor(self.mb[:], self.mu[:], self.rho[:], Alu.mult))
        p.add("vec", lambda v: v.tensor_scalar_mul(self.mb[:], self.mb[:], -1.0))
        self._ln_dep = p.mark("vec")
        return self._ln_dep

    def split_transpose_x(self, dst_h, dst_l, gamma, tiles=None, span=S, nchunk=ND,
                          to_sbuf=False):
        p = self.p
        if tiles is None:
            tiles = range(NT)
        tiles = list(tiles)
        t0 = tiles[0]
        last = None
        for t in tiles:
            xs = self.x_sb[:, t * D:(t + 1) * D]
            p.waits("scl", [self.sfree.get("u32"), getattr(self, "_ln_dep", None)])
            p.add("scl", lambda s, xs=xs, t=t: s.activation(
                self.u32[:], xs, Act.Identity, scale=self.rho[:, t:t + 1], bias=self.mb[:, t:t + 1]))
            d = p.mark("scl")
            p.wait("vec", d)
            src = self.u32
            if gamma:
                p.add("vec", lambda v: v.tensor_tensor(self.u32[:], self.u32[:], self.gam_rep[:], Alu.mult))
                p.add("vec", lambda v: v.tensor_tensor(self.u32[:], self.u32[:], self.bet_rep[:], Alu.add))
            p.waits("vec", [self.sfree.get("st")])
            p.add("vec", lambda v, src=src: v.tensor_copy(self.sth[:], src[:]))
            p.add("vec", lambda v, src=src: v.tensor_tensor(self.stl[:], src[:], self.sth[:], Alu.subtract))
            d = p.mark("vec")
            self.sfree["u32"] = d
            p.wait("ten", d)
            p.waits("ten", [self.pfree["T"]])
            for c in range(ND):
                p.add("ten", lambda tn, c=c: tn.transpose(
                    self.T_PS[:, c * 128:(c + 1) * 128], self.sth[:, c * 128:(c + 1) * 128], self.ident_sb[:]))
            for c in range(ND):
                p.add("ten", lambda tn, c=c: tn.transpose(
                    self.T_PS[:, 1024 + c * 128:1024 + (c + 1) * 128],
                    self.stl[:, c * 128:(c + 1) * 128], self.ident_sb[:]))
            d = p.mark("ten")
            self.sfree["st"] = d
            p.wait("vec", d)
            col = 128 * (t - t0)
            nc_cols = nchunk * 128
            if to_sbuf:
                dh_ = dst_h.rearrange("p (c s) -> p c s", c=nchunk)[:, :, col:col + 128]
                dl_ = dst_l.rearrange("p (c s) -> p c s", c=nchunk)[:, :, col:col + 128]
                p.waits("vec", [self.sfree.get("xsl")])
                p.add("vec", lambda v, dh_=dh_: v.tensor_copy(
                    dh_, self.T_PS[:, 0:nc_cols].rearrange("p (c k) -> p c k", c=nchunk)))
                p.add("vec", lambda v, dl_=dl_: v.tensor_copy(
                    dl_, self.T_PS[:, 1024:1024 + nc_cols].rearrange("p (c k) -> p c k", c=nchunk)))
                last = p.mark("vec")
                self.pfree["T"] = last
                continue
            p.waits("vec", [self.sfree.get("xsl")])
            p.add("vec", lambda v: v.tensor_copy(self.xsl_h[:, 0:1024], self.T_PS[:, 0:1024]))
            p.add("vec", lambda v: v.tensor_copy(self.xsl_l[:, 0:1024], self.T_PS[:, 1024:2048]))
            last = p.mark("vec")
            self.pfree["T"] = last
            p.waits("syn", [last])
            dh_ = dst_h.rearrange("p (c s) -> p c s", c=nchunk)[:, :, col:col + 128]
            dl_ = dst_l.rearrange("p (c s) -> p c s", c=nchunk)[:, :, col:col + 128]
            p.dma("syn", out=dh_, in_=self.xsl_h[:, 0:1024].rearrange("p (c k) -> p c k", c=nchunk), q="DSS")
            sd = p.dma("syn", out=dl_, in_=self.xsl_l[:, 0:1024].rearrange("p (c k) -> p c k", c=nchunk), q="DSS")
            self.sfree["xsl"] = sd
            last = sd
        return last

    def load_xslab(self, scr_h, scr_l, tc, span=S, nchunk=ND):
        p = self.p
        p.waits("syn", [self.sfree.get("xsl")])
        sh = scr_h.rearrange("p (c s) -> p c s", c=nchunk)[:, :, tc * 512:(tc + 1) * 512]
        sl = scr_l.rearrange("p (c s) -> p c s", c=nchunk)[:, :, tc * 512:(tc + 1) * 512]
        p.dma("syn", out=self.xsl_h[:].rearrange("p (c s) -> p c s", c=ND), in_=sh, q="DSL")
        d = p.dma("syn", out=self.xsl_l[:].rearrange("p (c s) -> p c s", c=ND), in_=sl, q="DSL")
        return d

    def load_weight_rows(self, dram_rows, ncols):
        p = self.p
        p.waits("syn", [self.sfree.get("wld")])
        p.waits("gps", [self.sfree.get("wld")])
        d1 = p.dma("syn", out=self.wld[:, 0:ncols], in_=dram_rows, q="DWT")
        d2 = p.dma("gps", out=self.wld_h[:, 0:ncols], in_=dram_rows, q="DGH")
        p.waits("vec", [d1, d2])
        p.add("vec", lambda v: v.tensor_tensor(
            self.wld_l[:, 0:ncols], self.wld[:, 0:ncols], self.wld_h[:, 0:ncols], Alu.subtract))
        return p.mark("vec")

    def load_weight_cols(self, dram, rows, c0, ncols_src):
        p = self.p
        p.waits("syn", [self.sfree.get("wld")])
        p.waits("gps", [self.sfree.get("wld")])
        nrc = rows // 128
        d1 = d2 = None
        for r in range(nrc):
            src = dram[r * 128:(r + 1) * 128, c0:c0 + ncols_src]
            d1 = p.dma("syn", out=self.wld[:, r * ncols_src:(r + 1) * ncols_src], in_=src, q="DWT")
            d2 = p.dma("gps", out=self.wld_h[:, r * ncols_src:(r + 1) * ncols_src], in_=src, q="DGH")
        p.waits("vec", [d1, d2])
        n = nrc * ncols_src
        p.add("vec", lambda v, n=n: v.tensor_tensor(
            self.wld_l[:, 0:n], self.wld[:, 0:n], self.wld_h[:, 0:n], Alu.subtract))
        return p.mark("vec")

    def transpose_weight(self, dst_off, ncols, dep, extra=()):
        p = self.p
        p.wait("ten", dep)
        p.waits("ten", list(extra))
        p.waits("vec", list(extra))
        p.waits("ten", [self.pfree["T"]])
        nch = ncols // 128
        assert nch <= 8
        for c in range(nch):
            p.add("ten", lambda tn, c=c: tn.transpose(
                self.T_PS[:, c * 128:(c + 1) * 128], self.wld_h[:, c * 128:(c + 1) * 128], self.ident_sb[:]))
        for c in range(nch):
            p.add("ten", lambda tn, c=c: tn.transpose(
                self.T_PS[:, 1024 + c * 128:1024 + (c + 1) * 128],
                self.wld_l[:, c * 128:(c + 1) * 128], self.ident_sb[:]))
        dT = p.mark("ten")
        self.sfree["wld"] = dT
        p.wait("vec", dT)
        p.add("vec", lambda v: v.tensor_copy(self.wT_h[:, dst_off:dst_off + ncols], self.T_PS[:, 0:ncols]))
        p.add("vec", lambda v: v.tensor_copy(self.wT_l[:, dst_off:dst_off + ncols],
                                             self.T_PS[:, 1024:1024 + ncols]))
        d = p.mark("vec")
        self.pfree["T"] = d
        return d

    def mm3(self, bank_ap, lhsT_h, lhsT_l, rhs_h, rhs_l, first, last, passes=None):
        p = self.p
        if passes is None:
            passes = [(lhsT_h, rhs_h), (lhsT_l, rhs_h), (lhsT_h, rhs_l)]
        for i, (a_, b_) in enumerate(passes):
            st = first and i == 0
            sp = last and i == len(passes) - 1
            p.add("ten", lambda tn, a_=a_, b_=b_, st=st, sp=sp: tn.matmul(
                bank_ap, a_, b_, start=st, stop=sp))

    def proj_fm(self, woff, dst_h, dst_l, scale, bias_col, extra=None):
        p = self.p
        deps = []
        for tc in range(4):
            bank = tc % 4
            p.waits("ten", [self.pfree[f"A{bank}"]])
            ps = self.PS_A[:, bank * 512:(bank + 1) * 512]
            for c in range(ND):
                self.mm3(ps, self.wT_h[:, woff + c * 128:woff + (c + 1) * 128],
                         self.wT_l[:, woff + c * 128:woff + (c + 1) * 128],
                         self.xh_v[:, c, tc * 512:(tc + 1) * 512],
                         self.xl_v[:, c, tc * 512:(tc + 1) * 512],
                         first=(c == 0), last=(c == ND - 1))
            d = p.mark("ten")
            p.wait("scl", d)
            p.waits("scl", [self.sfree.get("g32")])
            p.add("scl", lambda s, ps=ps: s.activation(
                self.g32[:], ps, Act.Identity, scale=scale, bias=bias_col))
            d = p.mark("scl")
            self.pfree[f"A{bank}"] = d
            p.wait("vec", d)
            p.add("vec", lambda v, tc=tc: v.tensor_copy(dst_h[:, tc * 512:(tc + 1) * 512], self.g32[:]))
            p.add("vec", lambda v, tc=tc: v.tensor_tensor(
                dst_l[:, tc * 512:(tc + 1) * 512], self.g32[:], dst_h[:, tc * 512:(tc + 1) * 512],
                Alu.subtract))
            d = p.mark("vec")
            self.sfree["g32"] = d
            deps.append(d)
        return deps[-1]

    def l0_attn(self):
        p = self.p
        W = self.W
        self.replicate_row(W["L0_attn_gamma"], self.gam_rep, D)
        self.replicate_row(W["L0_attn_beta"], self.bet_rep, D)
        bd = None
        for i, nm in enumerate(("bq", "bk", "bv", "bo")):
            bd = p.dma("syn", out=self.bias_fm[:, i * ND:(i + 1) * ND],
                       in_=W[f"L0_attn_{nm}"].rearrange("(c p) -> p c", p=128), slow=True, q="DRR")
        p.wait("vec", bd)
        p.add("vec", lambda v: v.tensor_scalar_mul(self.bias_fm[:, 0:ND], self.bias_fm[:, 0:ND], 0.125))
        bqs_dep = p.mark("vec")
        bo_dep = self.split_row(W["L0_attn_bo"], D)

        self.stats_pass()
        self.ln_scalars("lnln")
        xn_dep = self.split_transpose_x(self.xnT_scr_h, self.xnT_scr_l, gamma=True)


        b4_dep = None

        def compute_b4(bb):
            mt = self.p32[0:32, :]
            p.waits("syn", [self.sfree.get("p32"), self.sfree.get("rb_st")])
            mt_dep = p.dma("syn", out=mt, in_=self.mt_c.ap(), q="DB4")
            rb_dep = p.dma("syn", out=self.rb_st[:].rearrange("p (a j) -> p a j", a=4),
                           in_=self.W["L0_attn_rel_bias"].rearrange("h i j -> i h j")[
                               :, bb * 4:(bb + 1) * 4, :], q="DB4")
            p.waits("ten", [self.pfree["C"], mt_dep, rb_dep])
            for a in range(4):
                p.add("ten", lambda tn, a=a: tn.transpose(
                    self.PS_C[0:32, a * 32:(a + 1) * 32], self.rb_st[:, a * 32:(a + 1) * 32],
                    self.identf_sb[0:32, 0:32]))
            d = p.mark("ten")
            self.sfree["rb_st"] = d
            p.wait("vec", d)
            p.waits("vec", [self.sfree.get("relbT")])
            p.add("vec", lambda v: v.tensor_copy(self.relbT[:, :], self.PS_C[0:32, 0:128]))
            d = p.mark("vec")
            self.pfree["C"] = d
            p.wait("ten", d)
            for tc in range(4):
                bank = tc
                p.waits("ten", [self.pfree[f"A{bank}"]])
                p.add("ten", lambda tn, tc=tc, bank=bank, mt=mt: tn.matmul(
                    self.PS_A[:, bank * 512:(bank + 1) * 512], self.relbT[:, :],
                    mt[:, tc * 512:(tc + 1) * 512], start=True, stop=True))
            dT = p.mark("ten")
            self.sfree["relbT"] = dT
            p.wait("vec", dT)
            p.add("vec", lambda v: v.tensor_copy(self.b4_h[:, :], self.PS_A[:, :]))
            p.add("vec", lambda v: v.tensor_tensor(self.b4_l[:, :], self.PS_A[:, :], self.b4_h[:, :],
                                                   Alu.subtract))
            d = p.mark("vec")
            for k in ("A0", "A1", "A2", "A3"):
                self.pfree[k] = d
            return d

        for g in range(8):
            if g % 2 == 0:
                b4_dep = compute_b4(g // 2)
            free = self.sfree.get("qkv_free")
            dd = self.load_weight_rows(W["L0_attn_Wq"][g * 128:(g + 1) * 128, :], 1024)
            wq_dep = self.transpose_weight(0, 1024, dd, extra=(free,))
            dd = self.load_weight_rows(W["L0_attn_Wk"][g * 128:(g + 1) * 128, :], 1024)
            wk_dep = self.transpose_weight(1024, 1024, dd, extra=(free,))
            dd = self.load_weight_rows(W["L0_attn_Wv"][g * 128:(g + 1) * 128, :], 1024)
            wv_dep = self.transpose_weight(2 * 1024, 1024, dd, extra=(free,))

            p.waits("vec", [self.sfree.get("qkv_free")])
            qmm_last = None
            for tc in range(4):
                slab_dep = self.load_xslab(self.xnT_scr_h, self.xnT_scr_l, tc)
                p.waits("ten", [slab_dep, wq_dep, wk_dep, wv_dep, bqs_dep, b4_dep])
                for wi in range(3):
                    bank = wi
                    p.waits("ten", [self.pfree[f"A{bank}"]])
                    ps = self.PS_A[:, bank * 512:(bank + 1) * 512]
                    for c in range(ND):
                        self.mm3(ps, self.wT_h[:, wi * 1024 + c * 128:wi * 1024 + (c + 1) * 128],
                                 self.wT_l[:, wi * 1024 + c * 128:wi * 1024 + (c + 1) * 128],
                                 self.xh_v[:, c, 0:512], self.xl_v[:, c, 0:512],
                                 first=(c == 0), last=(c == ND - 1))
                    d = p.mark("ten")
                    if wi == 0:
                        qmm_last = d
                    if wi == 2:
                        self.sfree["xsl"] = d
                    p.wait("scl", d)
                    p.waits("scl", [self.sfree.get("g32")])
                    scale = 0.125 if wi == 0 else 1.0
                    bcol = self.bias_fm[:, wi * ND + g:wi * ND + g + 1]
                    p.add("scl", lambda s, ps=ps, scale=scale, bcol=bcol: s.activation(
                        self.g32[:], ps, Act.Identity, scale=scale, bias=bcol))
                    d = p.mark("scl")
                    self.pfree[f"A{bank}"] = d
                    p.wait("vec", d)
                    if wi < 2:
                        dst_h, dst_l = (self.qt_h, self.qt_l) if wi == 0 else (self.kt_h, self.kt_l)
                        p.add("vec", lambda v, tc=tc, dst_h=dst_h: v.tensor_copy(
                            dst_h[:, tc * 512:(tc + 1) * 512], self.g32[:]))
                        p.add("vec", lambda v, tc=tc, dst_h=dst_h, dst_l=dst_l: v.tensor_tensor(
                            dst_l[:, tc * 512:(tc + 1) * 512], self.g32[:],
                            dst_h[:, tc * 512:(tc + 1) * 512], Alu.subtract))
                        d = p.mark("vec")
                        self.sfree["g32"] = d
                    else:
                        p.waits("vec", [self.sfree.get("st")])
                        p.add("vec", lambda v: v.tensor_copy(self.sth[:, 0:512], self.g32[:]))
                        p.add("vec", lambda v: v.tensor_tensor(self.stl[:, 0:512], self.g32[:],
                                                               self.sth[:, 0:512], Alu.subtract))
                        d = p.mark("vec")
                        self.sfree["g32"] = d
                        p.wait("ten", d)
                        p.waits("ten", [self.pfree["T"]])
                        for i in range(4):
                            p.add("ten", lambda tn, i=i: tn.transpose(
                                self.T_PS[:, i * 128:(i + 1) * 128],
                                self.sth[:, i * 128:(i + 1) * 128], self.ident_sb[:]))
                            p.add("ten", lambda tn, i=i: tn.transpose(
                                self.T_PS[:, 1024 + i * 128:1024 + (i + 1) * 128],
                                self.stl[:, i * 128:(i + 1) * 128], self.ident_sb[:]))
                        dT = p.mark("ten")
                        self.sfree["st"] = dT
                        p.wait("vec", dT)
                        p.add("vec", lambda v, tc=tc: v.tensor_copy(
                            self.vt_h[:, tc * 512:(tc + 1) * 512], self.T_PS[:, 0:512]))
                        p.add("vec", lambda v, tc=tc: v.tensor_copy(
                            self.vt_l[:, tc * 512:(tc + 1) * 512], self.T_PS[:, 1024:1536]))
                        d = p.mark("vec")
                        self.pfree["T"] = d
            qkv_dep = p.mark("vec")
            dd = self.load_weight_cols(W["L0_attn_Wo"], D, g * 128, 128)
            wo_dep = self.transpose_weight(0, 1024, dd, extra=(qmm_last,))

            for hh in range(2):
                h = 2 * g + hh
                p.waits("syn", [qkv_dep, self.sfree.get("aug")])
                a1 = p.dma("syn", out=self.qaug_h[0:64, :], in_=self.qt_h[64 * hh:64 * hh + 64, :], q="DAU")
                a2 = p.dma("syn", out=self.qaug_l[0:64, :], in_=self.qt_l[64 * hh:64 * hh + 64, :], q="DAU")
                a3 = p.dma("syn", out=self.kaug_h[0:64, :], in_=self.kt_h[64 * hh:64 * hh + 64, :], q="DAU")
                a4 = p.dma("syn", out=self.kaug_l[0:64, :], in_=self.kt_l[64 * hh:64 * hh + 64, :], q="DAU")
                hq = h % 4
                p.waits("syn", [b4_dep])
                a5 = p.dma("syn", out=self.kaug_h[64:96, :], in_=self.b4_h[32 * hq:32 * hq + 32, :], q="DAU")
                a6 = p.dma("syn", out=self.kaug_l[64:96, :], in_=self.b4_l[32 * hq:32 * hq + 32, :], q="DAU")
                p.waits("ten", [a1, a2, a3, a4, a5, a6, self.pfree["C"]])
                self.attn_head(hh, self.qaug_h, self.qaug_l, self.kaug_h, self.kaug_l)
            self.pfree["C"] = self._attn_sc
            sc = self._attn_sc
            self.sfree["og"] = sc
            p.waits("ten", [sc, wo_dep, bo_dep])
            for tq in range(NT):
                for oc in range(2):
                    bank = (2 * tq + oc) % 2
                    p.waits("ten", [self.pfree[f"A{bank}"]])
                    ps = self.PS_A[:, bank * 512:(bank + 1) * 512]
                    ogh_t = self.ogh[:, tq * 128:(tq + 1) * 128]
                    ogl_t = self.ogl[:, tq * 128:(tq + 1) * 128]
                    woh = self.wT_h[:, oc * 512:(oc + 1) * 512]
                    wol = self.wT_l[:, oc * 512:(oc + 1) * 512]
                    self.mm3(ps, ogh_t, ogl_t, woh, wol, first=True, last=(g != 0))
                    if g == 0:
                        p.add("ten", lambda tn, ps=ps, oc=oc: tn.matmul(
                            ps, self.onesb_sb[:, :], self.brow_h[:, oc * 512:(oc + 1) * 512],
                            start=False, stop=False))
                        p.add("ten", lambda tn, ps=ps, oc=oc: tn.matmul(
                            ps, self.onesb_sb[:, :], self.brow_l[:, oc * 512:(oc + 1) * 512],
                            start=False, stop=True))
                    d = p.mark("ten")
                    p.wait("vec", d)
                    xsl = self.x_sb[:, tq * D + oc * 512: tq * D + (oc + 1) * 512]
                    p.add("vec", lambda v, xsl=xsl, ps=ps: v.tensor_tensor(xsl, xsl, ps, Alu.add))
                    self.pfree[f"A{bank}"] = p.mark("vec")
            self.sfree["qkv_free"] = p.mark("ten")


    def attn_head(self, hh, lq_h, lq_l, kk_h, kk_l):
        p = self.p
        if True:
            if True:
                for tq in range(NT):
                    klen = (tq + 1) * 128
                    nkc = (klen + 511) // 512
                    p.waits("ten", [self.pfree["A0"], self.pfree["A1"], self.pfree["A2"],
                                    self.pfree["A3"]])
                    for kc in range(nkc):
                        w_ = min(512, klen - kc * 512)
                        ps = self.PS_A[:, kc * 512:kc * 512 + w_]
                        self.mm3(ps, lq_h[:, tq * 128:(tq + 1) * 128],
                                 lq_l[:, tq * 128:(tq + 1) * 128],
                                 kk_h[:, kc * 512:kc * 512 + w_],
                                 kk_l[:, kc * 512:kc * 512 + w_],
                                 first=True, last=True)
                    d = p.mark("ten")
                    p.wait("vec", d)
                    p.add("vec", lambda v, tq=tq: v.tensor_tensor(
                        self.PS_A[:, tq * 128:(tq + 1) * 128], self.PS_A[:, tq * 128:(tq + 1) * 128],
                        self.mask_sb[:], Alu.add))
                    p.add("vec", lambda v, klen=klen: v.reduce_max(
                        self.mneg[:, 0:1], self.PS_A[:, 0:klen], axis=AX.X, negate=True))
                    d = p.mark("vec")
                    p.wait("scl", d)
                    p.waits("scl", [self.sfree.get("p32")])
                    p.add("scl", lambda s, klen=klen: s.activation(
                        self.p32[:, 0:klen], self.PS_A[:, 0:klen], Act.Exp,
                        bias=self.mneg[:, 0:1], accum_out=self.lsum[:, 0:1]))
                    d = p.mark("scl")
                    for k in ("A0", "A1", "A2", "A3"):
                        self.pfree[k] = d
                    p.wait("vec", d)
                    p.add("vec", lambda v: v.reciprocal(self.linv[:, 0:1], self.lsum[:, 0:1]))
                    d = p.mark("vec")
                    p.wait("scl", d)
                    p.waits("scl", [self.sfree.get("ph")])
                    p.add("scl", lambda s, klen=klen: s.mul(self.ph[:, 0:klen], self.p32[:, 0:klen],
                                                            self.linv[:, 0:1]))
                    d = p.mark("scl")
                    p.wait("vec", d)
                    p.add("vec", lambda v, klen=klen: v.scalar_tensor_tensor(
                        self.pl[:, 0:klen], self.p32[:, 0:klen], self.linv[:, 0:1],
                        self.ph[:, 0:klen], Alu.mult, Alu.subtract))
                    d = p.mark("vec")
                    self.sfree["p32"] = d
                    p.wait("ten", d)
                    oq = self.PS_C[64 * hh:64 * hh + 64, 256 * (tq % 2):256 * (tq % 2) + 128]
                    ntile = tq + 1
                    first_pv = True
                    for w0 in range(0, ntile, 4):
                        nw = min(4, ntile - w0)
                        half = (w0 // 4) % 2
                        p.waits("ten", [self.pfree["T"]])
                        for i in range(nw):
                            p.add("ten", lambda tn, i=i, w0=w0, half=half: tn.transpose(
                                self.T_PS[:, half * 1024 + i * 128:half * 1024 + (i + 1) * 128],
                                self.ph[:, (w0 + i) * 128:(w0 + i + 1) * 128], self.ident_sb[:]))
                            p.add("ten", lambda tn, i=i, w0=w0, half=half: tn.transpose(
                                self.T_PS[:, half * 1024 + 512 + i * 128:half * 1024 + 512 + (i + 1) * 128],
                                self.pl[:, (w0 + i) * 128:(w0 + i + 1) * 128], self.ident_sb[:]))
                        d = p.mark("ten")
                        p.wait("vec", d)
                        p.waits("vec", [self.sfree.get("pt")])
                        pcol = (w0 % 8) * 128
                        p.add("vec", lambda v, pcol=pcol, nw=nw, half=half: v.tensor_copy(
                            self.pth[:, pcol:pcol + nw * 128],
                            self.T_PS[:, half * 1024:half * 1024 + nw * 128]))
                        p.add("vec", lambda v, pcol=pcol, nw=nw, half=half: v.tensor_copy(
                            self.ptl[:, pcol:pcol + nw * 128],
                            self.T_PS[:, half * 1024 + 512:half * 1024 + 512 + nw * 128]))
                        d = p.mark("vec")
                        self.pfree["T"] = d
                        p.wait("ten", d)
                        for i in range(nw):
                            c = w0 + i
                            cc = (c % 8) * 128
                            vh_ = self.vt_h[:, c * 128 + 64 * hh:c * 128 + 64 * hh + 64]
                            vl_ = self.vt_l[:, c * 128 + 64 * hh:c * 128 + 64 * hh + 64]
                            self.mm3(oq, vh_, vl_,
                                     self.pth[:, cc:cc + 128],
                                     self.ptl[:, cc:cc + 128],
                                     first=first_pv and i == 0,
                                     last=(w0 + nw == ntile) and (i == nw - 1))
                        first_pv = False
                    d = p.mark("ten")
                    self.sfree["ph"] = d
                    self.sfree["pt"] = d
                    p.wait("scl", d)
                    p.waits("scl", [self.sfree.get("og")])
                    p.add("scl", lambda s, hh=hh, tq=tq: s.copy(
                        self.ogh[64 * hh:64 * hh + 64, tq * 128:(tq + 1) * 128],
                        self.PS_C[64 * hh:64 * hh + 64, 256 * (tq % 2):256 * (tq % 2) + 128]))
                    d = p.mark("scl")
                    p.wait("vec", d)
                    p.add("vec", lambda v, hh=hh, tq=tq: v.tensor_tensor(
                        self.ogl[64 * hh:64 * hh + 64, tq * 128:(tq + 1) * 128],
                        self.PS_C[64 * hh:64 * hh + 64, 256 * (tq % 2):256 * (tq % 2) + 128],
                        self.ogh[64 * hh:64 * hh + 64, tq * 128:(tq + 1) * 128], Alu.subtract))
                    sc = p.mark("vec")
                    if tq >= 1:
                        p.wait("ten", sc)
                self.sfree["aug"] = p.mark("ten")
                self._attn_sc = sc

    def _ternarize_stream(self, dram, nrows, ncols, tcol_neg, tcol, scr, scr_map):
        p = self.p
        for r in range(nrows // 128):
            for q in range(ncols // 1024):
                sl = dram[r * 128:(r + 1) * 128, q * 1024:(q + 1) * 1024]
                p.waits("syn", [self.sfree.get("wld")])
                d1 = p.dma("syn", out=self.wld[:, 0:1024], in_=sl, q="DWT")
                p.waits("scl", [d1, self.sfree.get("u32")])
                p.add("scl", lambda s: s.activation(self.u32[:], self.wld[:, 0:1024], Act.Abs))
                d = p.mark("scl")
                p.wait("vec", d)
                p.waits("vec", [self.sfree.get("st")])
                p.add("vec", lambda v: v.tensor_scalar(
                    self.sth[:, 0:1024], self.u32[:], tcol, None, Alu.is_gt))
                p.add("vec", lambda v: v.tensor_tensor(
                    self.u32[:], self.wld[:, 0:1024], self.sth[:, 0:1024], Alu.mult))
                d = p.mark("vec")
                self.sfree["wld"] = d
                p.wait("scl", d)
                p.add("scl", lambda s: s.activation(self.wld_h[:, 0:1024], self.u32[:], Act.Sign))
                d = p.mark("scl")
                self.sfree["u32"] = d
                p.wait("ten", d)
                p.waits("ten", [self.pfree["T"]])
                for c in range(8):
                    p.add("ten", lambda tn, c=c: tn.transpose(
                        self.T_PS[:, c * 128:(c + 1) * 128], self.wld_h[:, c * 128:(c + 1) * 128],
                        self.ident_sb[:]))
                dT = p.mark("ten")
                p.wait("vec", dT)
                p.add("vec", lambda v: v.tensor_copy(self.wld_l[:, 0:1024], self.T_PS[:, 0:1024]))
                d = p.mark("vec")
                self.pfree["T"] = d
                p.waits("syn", [d])
                out_ap, in_ap = scr_map(r, q)
                sd = p.dma("syn", out=out_ap, in_=in_ap, q="DSS")
                self.sfree["wld"] = sd

    def l0_ffn(self):
        p = self.p
        W = self.W
        hb_h = self._av(0, 32768)
        hb_l = self._av(32768, 32768)
        hsq = self._av(65536, 2048, F32)
        srow = self._av(67584 + 4096, 2048, F32, parts=1)
        s2col = self._av(73728, 64, F32)
        acell = self._av(73856, 64, F32)
        tcol_up = self._av(74240, 4, F32)
        tcol_dn = self._av(74368, 4, F32)

        self.stats_pass()
        self.ln_scalars("ln_rms")

        def alpha_pass(dram, nchunks, tcol):
            acc = acell[:, 0:1]
            tmp = acell[:, 1:2]
            first = True
            dview = dram.rearrange("(r p) c -> p r c", p=128)
            ncq = dview.shape[2] // 1024
            for i in range(nchunks):
                r, q = i // ncq, i % ncq
                p.waits("syn", [self.sfree.get("wld")])
                d1 = p.dma("syn", out=self.wld[:, 0:1024],
                           in_=dview[:, r, q * 1024:(q + 1) * 1024],
                           q="DWT")
                p.waits("vec", [d1])
                p.add("vec", lambda v, first=first: v.tensor_reduce(
                    tmp if not first else acc, self.wld[:, 0:1024], axis=AX.X, op=Alu.add,
                    apply_absolute_value=True))
                if not first:
                    p.add("vec", lambda v: v.tensor_tensor(acc, acc, tmp, Alu.add))
                d = p.mark("vec")
                self.sfree["wld"] = d
                first = False
            p.waits("ten", [d, self.pfree["C"]])
            p.add("ten", lambda tn: tn.matmul(self.PS_C[0:1, 0:1], acc, self.onescol_sb[:, :],
                                              start=True, stop=True))
            d = p.mark("ten")
            p.wait("vec", d)
            p.add("vec", lambda v: v.tensor_scalar_mul(acell[0:1, 2:3], self.PS_C[0:1, 0:1],
                                                       0.5 / 4194304.0))
            d = p.mark("vec")
            self.pfree["C"] = d
            p.waits("ten", [d, self.pfree["C"]])
            p.add("ten", lambda tn: tn.matmul(self.PS_C[:, 0:1], self.ones_sb[:, :],
                                              acell[0:1, 2:3], start=True, stop=True))
            d = p.mark("ten")
            p.wait("vec", d)
            p.add("vec", lambda v, tcol=tcol: v.tensor_copy(tcol, self.PS_C[:, 0:1]))
            d = p.mark("vec")
            self.pfree["C"] = d
            return d

        dup = alpha_pass(W["L0_ffn_w_up"], 32, tcol_up)
        ddn = alpha_pass(W["L0_ffn_w_down"], 32, tcol_dn)

        wup_v = self.wupT_scr
        self._ternarize_stream(
            W["L0_ffn_w_up"], 4096, 1024, None, tcol_up, self.wupT_scr,
            lambda r, q: (self.wupT_scr[:, r * 1024:(r + 1) * 1024], self.wld_l[:, 0:1024]))
        dn_view = self.wdownT_scr.rearrange("p (fc rr k) -> p fc rr k", fc=32, rr=8)
        self._ternarize_stream(
            W["L0_ffn_w_down"], 1024, 4096, None, tcol_dn, self.wdownT_scr,
            lambda r, q: (dn_view[:, 8 * q:8 * q + 8, r, :],
                          self.wld_l[:, 0:1024].rearrange("p (j k) -> p j k", j=8)))

        xb_h = self.xsl_h[:].rearrange("p (c s) -> p c s", c=ND)
        xb_l = self.xsl_l[:].rearrange("p (c s) -> p c s", c=ND)
        self.sfree["wsl0"] = self.sfree["wsl1"] = self.sfree.get("wld")
        for b in range(4):
            xd = self.split_transpose_x(self.xsl_h[:], self.xsl_l[:], gamma=False,
                                        tiles=range(4 * b, 4 * b + 4), span=512, to_sbuf=True)
            p.waits("ten", [xd])
            stats_started = False
            for fc in range(32):
                wbuf = self.wld_h if fc % 2 == 0 else self.wld_l
                p.waits("syn", [self.sfree.get(f"wsl{fc % 2}")])
                wd = p.dma("syn", out=wbuf[:, 0:1024],
                           in_=self.wupT_scr[:, fc * 1024:(fc + 1) * 1024],
                           q="DSL" if fc % 2 == 0 else "DS2")
                bank = fc % 4
                p.waits("ten", [wd, self.pfree[f"A{bank}"]])
                ps = self.PS_A[:, bank * 512:(bank + 1) * 512]
                for c in range(ND):
                    self.mm3(ps, None, None, None, None, first=(c == 0), last=(c == ND - 1),
                             passes=[(wbuf[:, c * 128:(c + 1) * 128], xb_h[:, c, :]),
                                     (wbuf[:, c * 128:(c + 1) * 128], xb_l[:, c, :])])
                d = p.mark("ten")
                self.sfree[f"wsl{fc % 2}"] = d
                p.wait("scl", d)
                p.waits("scl", [self.sfree.get("g32")])
                p.add("scl", lambda s, ps=ps: s.activation(self.g32[:], ps, Act.Gelu))
                d = p.mark("scl")
                self.pfree[f"A{bank}"] = d
                p.wait("vec", d)
                p.add("vec", lambda v, fc=fc: v.tensor_copy(hb_h[:, fc * 512:(fc + 1) * 512], self.g32[:]))
                p.add("vec", lambda v, fc=fc: v.tensor_tensor(
                    hb_l[:, fc * 512:(fc + 1) * 512], self.g32[:], hb_h[:, fc * 512:(fc + 1) * 512],
                    Alu.subtract))
                p.add("vec", lambda v: v.tensor_tensor(hsq[:, :], self.g32[:], self.g32[:], Alu.mult))
                d = p.mark("vec")
                self.sfree["g32"] = d
                p.waits("ten", [d] + ([self.pfree["D"]] if not stats_started else []))
                p.add("ten", lambda tn, fc=fc: tn.matmul(
                    self.PS_D[0:1, 0:512], self.onescol_sb[:, :],
                    hsq[:, :], start=(fc == 0), stop=(fc == 31)))
                stats_started = True
                p.waits("vec", [p.mark("ten")])
            d = p.mark("ten")
            p.wait("vec", d)
            p.add("vec", lambda v: v.tensor_copy(srow[:, :], self.PS_D[0:1, 0:512]))
            d = p.mark("vec")
            self.pfree["D"] = d
            p.waits("ten", [d, self.pfree["C"]])
            for tt in range(4):
                p.add("ten", lambda tn, tt=tt: tn.transpose(
                    self.PS_C[:, tt:tt + 1], srow[0:1, tt * 128:(tt + 1) * 128],
                    self.identf_sb[0:1, 0:1]))
            d = p.mark("ten")
            p.wait("vec", d)
            p.add("vec", lambda v: v.tensor_copy(s2col[:, 0:4], self.PS_C[:, 0:4]))
            d = p.mark("vec")
            self.pfree["C"] = d
            p.wait("scl", d)
            p.add("scl", lambda s: s.activation(s2col[:, 0:4], s2col[:, 0:4], Act.Sqrt))
            d = p.mark("scl")
            p.wait("vec", d)
            p.add("vec", lambda v: v.tensor_scalar(s2col[:, 0:4], s2col[:, 0:4], 1.0 / 64.0, 1e-8,
                                                   Alu.mult, Alu.add))
            p.add("vec", lambda v: v.reciprocal(s2col[:, 0:4], s2col[:, 0:4]))
            s2dep = p.mark("vec")

            banks = [self.PS_A[:, i * 512:(i + 1) * 512] for i in range(4)] +                     [self.T_PS[:].bitcast(F32)[:, 0:512], self.T_PS[:].bitcast(F32)[:, 512:1024],
                     self.PS_C[:, 0:512], self.PS_D[:, 0:512]]
            p.waits("ten", [self.pfree["A0"], self.pfree["A1"], self.pfree["A2"], self.pfree["A3"],
                            self.pfree["T"], self.pfree["C"], self.pfree["D"]])
            for fc in range(32):
                wbuf = self.wld_h if fc % 2 == 0 else self.wld_l
                p.waits("syn", [self.sfree.get(f"wsl{fc % 2}")])
                wd = p.dma("syn", out=wbuf[:, 0:1024],
                           in_=self.wdownT_scr[:, fc * 1024:(fc + 1) * 1024],
                           q="DSL" if fc % 2 == 0 else "DS2")
                p.waits("ten", [wd])
                for tt in range(4):
                    hh = hb_h[:, fc * 512 + tt * 128:fc * 512 + (tt + 1) * 128]
                    hl = hb_l[:, fc * 512 + tt * 128:fc * 512 + (tt + 1) * 128]
                    for oc in range(2):
                        self.mm3(banks[tt * 2 + oc], None, None, None, None,
                                 first=(fc == 0), last=(fc == 31),
                                 passes=[(hh, wbuf[:, oc * 512:(oc + 1) * 512]),
                                         (hl, wbuf[:, oc * 512:(oc + 1) * 512])])
                d = p.mark("ten")
                self.sfree[f"wsl{fc % 2}"] = d
            d = p.mark("ten")
            p.wait("vec", d)
            p.waits("vec", [s2dep])
            for tt in range(4):
                t_g = 4 * b + tt
                for oc in range(2):
                    xsl_ = self.x_sb[:, t_g * D + oc * 512:t_g * D + (oc + 1) * 512]
                    p.add("vec", lambda v, xsl_=xsl_, tt=tt, oc=oc: v.scalar_tensor_tensor(
                        xsl_, banks[tt * 2 + oc], s2col[:, tt:tt + 1], xsl_, Alu.mult, Alu.add))
            d = p.mark("vec")
            for k in ("A0", "A1", "A2", "A3", "T", "C", "D"):
                self.pfree[k] = d


    def l1_attn(self):
        p = self.p
        W = self.W
        I = 2048
        NC = 16
        BT = 512
        h_in_h = self._av(0, 16512)
        h_in_l = self._av(16512, 16512)
        u_cm = self._av(0, 32768, F32)
        h2_h = self._av(0, 16384)
        h2_l = self._av(16384, 16384)
        h32 = self._av(33024, 32768, F32)
        gate32 = self._av(65792, 2048, F32)
        m1_h = self._av(67840, 8192)
        m1_l = self._av(76032, 8192)
        murep = self._av(67840, 2048, F32)
        rrep = self._av(69888, 2048, F32)
        srow2 = self._av(65792, 2048, F32, parts=1)
        srow2b = self._av(71936, 2048, F32, parts=1)
        halo_h = self._av(84224, 96)
        halo_l = self._av(84320, 96)

        hiv = lambda buf: buf.rearrange("p (c s) -> p c s", c=NC)
        h32v = h32.rearrange("p (c s) -> p c s", c=NC)
        xb_h = self.xsl_h[:].rearrange("p (c s) -> p c s", c=ND)
        xb_l = self.xsl_l[:].rearrange("p (c s) -> p c s", c=ND)

        for off, nm in ((0, "ln_g"), (16, "ln_b"), (32, "conv_b"), (48, "cproj_b")):
            bd = p.dma("syn", out=self.bias2[:, off:off + 16],
                       in_=W[f"L1_attn_{nm}"].rearrange("(c p) -> p c", p=128), slow=True, q="DRR")
        bd = p.dma("syn", out=self.bias2[:, 64:72],
                   in_=W["L1_attn_fc_b"].rearrange("(c p) -> p c", p=128), slow=True, q="DRR")
        for eng in ("vec", "scl"):
            p.wait(eng, bd)

        self.dbg_dump("xl1a", self.x_sb[:])
        vd_, sd_ = self.stats_pass()
        self.dbg_dump("sq1a", self.sq[:], sd_)
        self.dbg_dump("sm1a", self.sm[:], vd_)
        d_ = self.ln_scalars("ln")
        self.dbg_dump("rho1a", self.rho[:], d_)

        p.add("vec", lambda v: v.memset(hiv(h_in_h)[:, :, 0:4], 0.0))
        p.add("vec", lambda v: v.memset(hiv(h_in_l)[:, :, 0:4], 0.0))

        for b in range(4):
            xd = self.split_transpose_x(self.xsl_h[:], self.xsl_l[:], gamma=False,
                                        tiles=range(4 * b, 4 * b + 4), span=BT, to_sbuf=True)
            if b == 3:
                self.dbg_dump("xsl_h", self.xsl_h[:], xd)
                self.dbg_dump("xsl_l", self.xsl_l[:], xd)
            if b > 0:
                p.add("vec", lambda v: v.tensor_copy(hiv(h_in_h)[:, :, 1:4], halo_h[:].rearrange("p (c k) -> p c k", c=NC)))
                p.add("vec", lambda v: v.tensor_copy(hiv(h_in_l)[:, :, 1:4], halo_l[:].rearrange("p (c k) -> p c k", c=NC)))
            p.waits("ten", [xd])

            for c in range(NC):
                slot = c % 2
                free = self.sfree.get("wT")
                dd = self.load_weight_rows(W["L1_attn_in_proj_w"][c * 128:(c + 1) * 128, :], 1024)
                wdep = self.transpose_weight(slot * 1024, 1024, dd, extra=(free,))
                bank = c % 4
                p.waits("ten", [wdep, self.pfree[f"A{bank}"]])
                ps = self.PS_A[:, bank * 512:(bank + 1) * 512]
                for dc in range(ND):
                    self.mm3(ps, self.wT_h[:, slot * 1024 + dc * 128:slot * 1024 + (dc + 1) * 128],
                             self.wT_l[:, slot * 1024 + dc * 128:slot * 1024 + (dc + 1) * 128],
                             xb_h[:, dc, :], xb_l[:, dc, :], first=(dc == 0), last=(dc == ND - 1))
                d = p.mark("ten")
                self.sfree["wT"] = d
                p.wait("scl", d)
                p.waits("scl", [self.sfree.get("g32")])
                p.add("scl", lambda s, ps=ps: s.activation(self.g32[:], ps, Act.Identity))
                d = p.mark("scl")
                self.pfree[f"A{bank}"] = d
                p.wait("vec", d)
                p.waits("vec", [self.sfree.get("hreg")])
                p.add("vec", lambda v, c=c: v.tensor_copy(hiv(h_in_h)[:, c, 4:516], self.g32[:]))
                p.add("vec", lambda v, c=c: v.tensor_tensor(
                    hiv(h_in_l)[:, c, 4:516], self.g32[:], hiv(h_in_h)[:, c, 4:516], Alu.subtract))
                d = p.mark("vec")
                self.sfree["g32"] = d
            hin_dep = p.mark("vec")
            if b == 3:
                self.dbg_dump("hin_h", h_in_h, hin_dep)
                self.dbg_dump("hin_l", h_in_l, hin_dep)

            for g in range(8):
                free = self.sfree.get("wT")
                for rr in range(2):
                    dd = self.load_weight_rows(W["L1_attn_conv_w"].rearrange("o c k -> o (c k)")[
                        g * 256 + rr * 128:g * 256 + rr * 128 + 128, :], 1024)
                    p.wait("ten", dd)
                    p.waits("ten", [free, self.pfree["T"]])
                    p.waits("vec", [free])
                    for ci in range(2):
                        for k in range(4):
                            src_h = self.wld_h[:, 0:1024].rearrange("p (ci2 k2) -> p k2 ci2", k2=4)[:, k, ci * 128:(ci + 1) * 128]
                            src_l = self.wld_l[:, 0:1024].rearrange("p (ci2 k2) -> p k2 ci2", k2=4)[:, k, ci * 128:(ci + 1) * 128]
                            ti = (ci * 4 + k)
                            p.add("ten", lambda tn, src_h=src_h, ti=ti, rr=rr: tn.transpose(
                                self.T_PS[:, ti * 128:(ti + 1) * 128], src_h, self.ident_sb[:]))
                            p.add("ten", lambda tn, src_l=src_l, ti=ti, rr=rr: tn.transpose(
                                self.T_PS[:, 1024 + ti * 128:1024 + (ti + 1) * 128], src_l, self.ident_sb[:]))
                    dT = p.mark("ten")
                    self.sfree["wld"] = dT
                    p.wait("vec", dT)
                    p.add("vec", lambda v, rr=rr: v.tensor_copy(
                        self.wT_h[:, rr * 1024:(rr + 1) * 1024], self.T_PS[:, 0:1024]))
                    p.add("vec", lambda v, rr=rr: v.tensor_copy(
                        self.wT_l[:, rr * 1024:(rr + 1) * 1024], self.T_PS[:, 1024:2048]))
                    d = p.mark("vec")
                    self.pfree["T"] = d
                conv_w_dep = p.mark("vec")
                for co in range(2):
                    cog = 2 * g + co
                    free = self.sfree.get("wT2")
                    dd = self.load_weight_rows(
                        W["L1_attn_in_proj_w"][I + cog * 128:I + (cog + 1) * 128, :], 1024)
                    gdep = self.transpose_weight(2 * 1024, 1024, dd, extra=(free,))
                    p.waits("ten", [gdep, self.pfree["C"]])
                    for dc in range(ND):
                        self.mm3(self.PS_C[:, 0:512],
                                 self.wT_h[:, 2 * 1024 + dc * 128:2 * 1024 + (dc + 1) * 128],
                                 self.wT_l[:, 2 * 1024 + dc * 128:2 * 1024 + (dc + 1) * 128],
                                 xb_h[:, dc, :], xb_l[:, dc, :], first=(dc == 0), last=(dc == ND - 1))
                    d = p.mark("ten")
                    self.sfree["wT2"] = d
                    p.wait("scl", d)
                    p.waits("scl", [self.sfree.get("gate32")])
                    p.add("scl", lambda s: s.activation(gate32[:, :], self.PS_C[:, 0:512], Act.Silu))
                    sil_dep = p.mark("scl")
                    self.pfree["C"] = sil_dep
                    bank = co % 2
                    p.waits("ten", [conv_w_dep, hin_dep, self.pfree[f"A{bank}"]])
                    ps = self.PS_A[:, bank * 512:(bank + 1) * 512]
                    first = True
                    for ci in range(2):
                        cig = 2 * g + ci
                        for k in range(4):
                            ti = (co * 8 + ci * 4 + k) * 128
                            rh = hiv(h_in_h)[:, cig, 1 + k:1 + k + 512]
                            rl = hiv(h_in_l)[:, cig, 1 + k:1 + k + 512]
                            self.mm3(ps, self.wT_h[:, ti:ti + 128], self.wT_l[:, ti:ti + 128],
                                     rh, rl, first=first, last=(ci == 1 and k == 3))
                            first = False
                    d = p.mark("ten")
                    p.waits("vec", [d, sil_dep])
                    p.add("vec", lambda v, cog=cog, ps=ps: v.scalar_tensor_tensor(
                        h32v[:, cog, :], ps, self.bias2[:, 32 + cog:33 + cog],
                        gate32[:, :], Alu.add, Alu.mult))
                    d = p.mark("vec")
                    self.pfree[f"A{bank}"] = d
                    self.sfree["gate32"] = d
            conv_done = p.mark("vec")
            if b == 3:
                self.dbg_dump("h32", h32, conv_done)
            self.sfree["xsl"] = p.mark("ten")

            p.add("vec", lambda v: v.tensor_copy(
                halo_h[:].rearrange("p (c k) -> p c k", c=NC), hiv(h_in_h)[:, :, 513:516]))
            p.add("vec", lambda v: v.tensor_copy(
                halo_l[:].rearrange("p (c k) -> p c k", c=NC), hiv(h_in_l)[:, :, 513:516]))
            halo_dep = p.mark("vec")

            p.waits("ten", [conv_done, self.pfree["C"], self.pfree["D"]])
            for c in range(NC):
                p.add("ten", lambda tn, c=c: tn.matmul(
                    self.PS_D[0:1, 0:512], self.onescol_sb[:, :], h32v[:, c, :],
                    start=(c == 0), stop=(c == NC - 1)))
            for c in range(NC):
                p.waits("vec", [self.sfree.get("gate32")])
                p.add("vec", lambda v, c=c: v.tensor_tensor(gate32[:, :], h32v[:, c, :], h32v[:, c, :],
                                                            Alu.mult))
                d = p.mark("vec")
                p.waits("ten", [d])
                p.add("ten", lambda tn, c=c: tn.matmul(
                    self.PS_C[0:1, 0:512], self.onescol_sb[:, :], gate32[:, :],
                    start=(c == 0), stop=(c == NC - 1)))
                self.sfree["gate32"] = p.mark("ten")
            d = p.mark("ten")
            p.wait("vec", d)
            p.add("vec", lambda v: v.tensor_copy(srow2[:, :], self.PS_D[0:1, 0:512]))
            p.add("vec", lambda v: v.tensor_copy(srow2b[:, :], self.PS_C[0:1, 0:512]))
            d = p.mark("vec")
            self.pfree["C"] = d
            self.pfree["D"] = d
            p.waits("ten", [d, self.pfree["A0"], self.pfree["A1"]])
            p.add("ten", lambda tn: tn.matmul(self.PS_A[:, 0:512], self.ones_sb[:, :], srow2[:, :],
                                              start=True, stop=True))
            p.add("ten", lambda tn: tn.matmul(self.PS_A[:, 512:1024], self.ones_sb[:, :], srow2b[:, :],
                                              start=True, stop=True))
            d = p.mark("ten")
            p.wait("vec", d)
            p.add("vec", lambda v: v.tensor_scalar_mul(murep[:, :], self.PS_A[:, 0:512], 1.0 / I))
            p.add("vec", lambda v: v.tensor_scalar_mul(rrep[:, :], self.PS_A[:, 512:1024], 1.0 / I))
            d = p.mark("vec")
            self.pfree["A0"] = d
            self.pfree["A1"] = d
            p.add("vec", lambda v: v.tensor_tensor(gate32[:, :], murep[:, :], murep[:, :], Alu.mult))
            p.add("vec", lambda v: v.tensor_tensor(rrep[:, :], rrep[:, :], gate32[:, :], Alu.subtract))
            p.add("vec", lambda v: v.tensor_scalar_add(rrep[:, :], rrep[:, :], 1e-5))
            d = p.mark("vec")
            p.wait("scl", d)
            p.add("scl", lambda s: s.activation(rrep[:, :], rrep[:, :], Act.Sqrt))
            d = p.mark("scl")
            p.wait("vec", d)
            p.add("vec", lambda v: v.reciprocal(rrep[:, :], rrep[:, :]))
            stat_dep = p.mark("vec")

            u_cmv = u_cm.rearrange("p (c s) -> p c s", c=NC)
            p.waits("vec", [halo_dep, stat_dep])
            for c in range(NC):
                p.add("vec", lambda v, c=c: v.tensor_tensor(u_cmv[:, c, :], h32v[:, c, :],
                                                            murep[:, :], Alu.subtract))
                p.add("vec", lambda v, c=c: v.tensor_tensor(u_cmv[:, c, :], u_cmv[:, c, :],
                                                            rrep[:, :], Alu.mult))
            ucm_dep = p.mark("vec")

            for half in range(2):
                started = [False] * 4
                for i in range(NC):
                    p.waits("scl", [ucm_dep, self.sfree.get("g32")])
                    p.add("scl", lambda s, i=i: s.activation(
                        self.g32[:], u_cmv[:, i, :], Act.Identity,
                        scale=self.bias2[:, i:i + 1],
                        bias=self.bias2[:, 16 + i:17 + i]))
                    d = p.mark("scl")
                    p.wait("vec", d)
                    p.waits("vec", [self.sfree.get("st")])
                    p.add("vec", lambda v: v.tensor_copy(self.sth[:, 0:512], self.g32[:]))
                    p.add("vec", lambda v: v.tensor_tensor(self.stl[:, 0:512], self.g32[:],
                                                           self.sth[:, 0:512], Alu.subtract))
                    hn_dep = p.mark("vec")
                    self.sfree["g32"] = hn_dep
                    free = self.sfree.get("wT")
                    dd = self.load_weight_cols(W["L1_attn_fc_w"][half * 512:(half + 1) * 512, :],
                                               512, i * 128, 128)
                    wdep = self.transpose_weight(0, 512, dd, extra=(free,))
                    p.waits("ten", [wdep, hn_dep])
                    if i == 0:
                        p.waits("ten", [self.pfree["A0"], self.pfree["A1"], self.pfree["A2"],
                                        self.pfree["A3"]])
                    for oc in range(4):
                        ps = self.PS_A[:, oc * 512:(oc + 1) * 512]
                        self.mm3(ps, None, None, None, None,
                                 first=(i == 0), last=(i == NC - 1),
                                 passes=[(self.wT_h[:, oc * 128:(oc + 1) * 128], self.sth[:, 0:512]),
                                         (self.wT_l[:, oc * 128:(oc + 1) * 128], self.sth[:, 0:512]),
                                         (self.wT_h[:, oc * 128:(oc + 1) * 128], self.stl[:, 0:512])])
                    d = p.mark("ten")
                    self.sfree["wT"] = d
                    self.sfree["st"] = d
                d = p.mark("ten")
                p.wait("scl", d)
                for oc in range(4):
                    og = half * 4 + oc
                    p.waits("scl", [self.sfree.get("g32")])
                    p.add("scl", lambda s, oc=oc, og=og: s.activation(
                        self.g32[:], self.PS_A[:, oc * 512:(oc + 1) * 512], Act.Silu,
                        bias=self.bias2[:, 64 + og:65 + og]))
                    d2 = p.mark("scl")
                    p.wait("vec", d2)
                    p.waits("vec", [self.sfree.get("m1")])
                    p.add("vec", lambda v, og=og: v.tensor_copy(
                        m1_h[:, og * 512:(og + 1) * 512], self.g32[:]))
                    p.add("vec", lambda v, og=og: v.tensor_tensor(
                        m1_l[:, og * 512:(og + 1) * 512], self.g32[:], m1_h[:, og * 512:(og + 1) * 512],
                        Alu.subtract))
                    d2 = p.mark("vec")
                    self.sfree["g32"] = d2
                    self.pfree[f"A{oc}"] = d2
            m1_dep = p.mark("vec")

            for c in range(NC):
                slot = c % 2
                free = self.sfree.get("wT")
                dd = self.load_weight_rows(W["L1_attn_cproj_w"][c * 128:(c + 1) * 128, :], 1024)
                wdep = self.transpose_weight(slot * 1024, 1024, dd, extra=(free,))
                bank = c % 4
                p.waits("ten", [wdep, m1_dep, self.pfree[f"A{bank}"]])
                ps = self.PS_A[:, bank * 512:(bank + 1) * 512]
                for oc in range(ND):
                    self.mm3(ps, self.wT_h[:, slot * 1024 + oc * 128:slot * 1024 + (oc + 1) * 128],
                             self.wT_l[:, slot * 1024 + oc * 128:slot * 1024 + (oc + 1) * 128],
                             m1_h[:, oc * 512:(oc + 1) * 512], m1_l[:, oc * 512:(oc + 1) * 512],
                             first=(oc == 0), last=(oc == ND - 1))
                d = p.mark("ten")
                self.sfree["wT"] = d
                p.wait("vec", d)
                p.waits("vec", [self.sfree.get("g32")])
                p.add("vec", lambda v, c=c, ps=ps: v.scalar_tensor_tensor(
                    self.g32[:], ps, self.bias2[:, 48 + c:49 + c], h32v[:, c, :],
                    Alu.add, Alu.add))
                h2v_h = h2_h.rearrange("p (c s) -> p c s", c=NC)
                h2v_l = h2_l.rearrange("p (c s) -> p c s", c=NC)
                p.add("vec", lambda v, c=c, h2v_h=h2v_h: v.tensor_copy(h2v_h[:, c, :], self.g32[:]))
                p.add("vec", lambda v, c=c, h2v_h=h2v_h, h2v_l=h2v_l: v.tensor_tensor(
                    h2v_l[:, c, :], self.g32[:], h2v_h[:, c, :], Alu.subtract))
                d = p.mark("vec")
                self.sfree["g32"] = d
                self.pfree[f"A{bank}"] = d
            h2_dep = p.mark("vec")
            h2v_h = h2_h.rearrange("p (c s) -> p c s", c=NC)
            h2v_l = h2_l.rearrange("p (c s) -> p c s", c=NC)

            for half in range(2):
                started = False
                for c in range(NC):
                    free = self.sfree.get("wT")
                    dd = self.load_weight_cols(W["L1_attn_out_w"][half * 512:(half + 1) * 512, :],
                                               512, c * 128, 128)
                    wdep = self.transpose_weight(0, 512, dd, extra=(free,))
                    p.waits("ten", [wdep, h2_dep])
                    if not started:
                        p.waits("ten", [self.pfree["A0"], self.pfree["A1"], self.pfree["A2"],
                                        self.pfree["A3"]])
                        started = True
                    for tt in range(4):
                        ps = self.PS_A[:, tt * 512:(tt + 1) * 512]
                        self.mm3(ps, h2v_h[:, c, tt * 128:(tt + 1) * 128],
                                 h2v_l[:, c, tt * 128:(tt + 1) * 128],
                                 self.wT_h[:, 0:512], self.wT_l[:, 0:512],
                                 first=(c == 0), last=(c == NC - 1))
                    d = p.mark("ten")
                    self.sfree["wT"] = d
                d = p.mark("ten")
                p.wait("vec", d)
                for tt in range(4):
                    t_g = 4 * b + tt
                    xsl_ = self.x_sb[:, t_g * D + half * 512:t_g * D + (half + 1) * 512]
                    p.add("vec", lambda v, xsl_=xsl_, tt=tt: v.tensor_tensor(
                        xsl_, xsl_, self.PS_A[:, tt * 512:(tt + 1) * 512], Alu.add))
                d = p.mark("vec")
                for k in ("A0", "A1", "A2", "A3"):
                    self.pfree[k] = d
            self.sfree["hreg"] = p.mark("ten")
            self.sfree["m1"] = self.sfree["hreg"]


    def l1_ffn(self):
        p = self.p
        W = self.W
        cd = None
        for t_, c_ in ((self.irev_sb, self.irev_c), (self.ecol_sb, self.ecol_c),
                       (self.g8_sb, self.g8_c), (self.g8t_sb, self.g8t_c)):
            cd = p.dma("syn", out=t_[:], in_=c_.ap(), q="DRR")

        spill = None
        for t in range(NT):
            spill = p.dma("syn", out=self.x_scr[:, t * D:(t + 1) * D],
                          in_=self.x_sb[:, t * D:(t + 1) * D], q="DSS")
        self.replicate_row(W["L1_ffn_g1"], self.gam_rep, D)
        p.add("vec", lambda v: v.memset(self.bet_rep[:], 0.0))
        self.stats_pass()
        self.ln_scalars("lnln", eps=1e-5, eps2=1e-6)
        xn_dep = self.split_transpose_x(self.xnT_scr_h, self.xnT_scr_l, gamma=True)
        p.waits("scl", [spill, xn_dep])
        zt = None
        for t in range(NT):
            xs = self.x_sb[:, t * D:(t + 1) * D]
            p.add("scl", lambda s, xs=xs, t=t: s.activation(
                xs, xs, Act.Identity, scale=self.r1s[:, t:t + 1], bias=self.mb1s[:, t:t + 1]))
        zt = p.mark("scl")
        p.waits("vec", [zt])
        p.waits("ten", [zt])

        qkv = W["L1_ffn_w_qkv"]
        for g in range(8):
            for wi, coff in ((0, 0), (1, D), (2, 2 * D)):
                p.waits("syn", [self.sfree.get("wld")])
                p.waits("gps", [self.sfree.get("wld")])
                d1 = d2 = None
                for r in range(ND):
                    slc = qkv[r * 128:(r + 1) * 128, coff + g * 128:coff + (g + 1) * 128]
                    d1 = p.dma("syn", out=self.wld[:, r * 128:(r + 1) * 128], in_=slc, q="DWT")
                    d2 = p.dma("gps", out=self.wld_h[:, r * 128:(r + 1) * 128], in_=slc, q="DGH")
                p.waits("vec", [d1, d2])
                p.add("vec", lambda v: v.tensor_tensor(
                    self.wld_l[:, 0:1024], self.wld[:, 0:1024], self.wld_h[:, 0:1024], Alu.subtract))
                wdep = p.mark("vec")
                for tc in range(4):
                    slab_dep = self.load_xslab(self.xnT_scr_h, self.xnT_scr_l, tc)
                    p.waits("ten", [slab_dep, wdep, xn_dep])
                    bank = tc % 4
                    p.waits("ten", [self.pfree[f"A{bank}"]])
                    ps = self.PS_A[:, bank * 512:(bank + 1) * 512]
                    for c in range(ND):
                        self.mm3(ps, self.wld_h[:, c * 128:(c + 1) * 128],
                                 self.wld_l[:, c * 128:(c + 1) * 128],
                                 self.xh_v[:, c, 0:512], self.xl_v[:, c, 0:512],
                                 first=(c == 0), last=(c == ND - 1))
                    d = p.mark("ten")
                    if wi == 2 and tc == 3:
                        self.sfree["wld"] = d
                    self.sfree["xsl"] = d
                    p.wait("scl", d)
                    p.waits("scl", [self.sfree.get("g32")])
                    scale = 0.125 if wi == 0 else 1.0
                    p.add("scl", lambda s, ps=ps, scale=scale: s.activation(
                        self.g32[:], ps, Act.Identity, scale=scale))
                    d = p.mark("scl")
                    self.pfree[f"A{bank}"] = d
                    if wi < 2:
                        qs_buf = self._av(77824, 2048, F32)
                        qr_buf = self._av(79872, 2048, F32)
                        p.waits("syn", [self.sfree.get("u32"), ("DB4", p.ctr["DB4"])])
                        c1d = p.dma("syn", out=self.u32[:, 0:512],
                                    in_=self.rope1_c.ap()[:, tc * 512:(tc + 1) * 512], q="DB4")
                        c2d = p.dma("syn", out=self.u32[:, 512:1024],
                                    in_=self.rope2_c.ap()[:, tc * 512:(tc + 1) * 512], q="DB4")
                        p.waits("vec", [d, c1d, c2d])
                        mask = []
                        for i_ in range(32):
                            mask.append(i_ + 1 if i_ % 2 == 0 else i_ - 1)
                        p.add("vec", lambda v, qs_buf=qs_buf, mask=mask: v.stream_shuffle(
                            qs_buf[:, :], self.g32[:], mask))
                        p.add("vec", lambda v, qr_buf=qr_buf: v.tensor_tensor(
                            qr_buf[:, :], self.g32[:], self.u32[:, 0:512], Alu.mult))
                        p.add("vec", lambda v, qs_buf=qs_buf: v.tensor_tensor(
                            qs_buf[:, :], qs_buf[:, :], self.u32[:, 512:1024], Alu.mult))
                        p.add("vec", lambda v, qr_buf=qr_buf, qs_buf=qs_buf: v.tensor_tensor(
                            qr_buf[:, :], qr_buf[:, :], qs_buf[:, :], Alu.add))
                        ud = p.mark("vec")
                        self.sfree["u32"] = ud
                        dst_h, dst_l = (self.qt_h, self.qt_l) if wi == 0 else (self.kt_h, self.kt_l)
                        p.add("vec", lambda v, tc=tc, dst_h=dst_h, qr_buf=qr_buf: v.tensor_copy(
                            dst_h[:, tc * 512:(tc + 1) * 512], qr_buf[:, :]))
                        p.add("vec", lambda v, tc=tc, dst_h=dst_h, dst_l=dst_l, qr_buf=qr_buf:
                              v.tensor_tensor(dst_l[:, tc * 512:(tc + 1) * 512], qr_buf[:, :],
                                              dst_h[:, tc * 512:(tc + 1) * 512], Alu.subtract))
                        d = p.mark("vec")
                        self.sfree["g32"] = d
                    else:
                        p.wait("vec", d)
                        p.waits("vec", [self.sfree.get("st")])
                        p.add("vec", lambda v: v.tensor_copy(self.sth[:, 0:512], self.g32[:]))
                        p.add("vec", lambda v: v.tensor_tensor(self.stl[:, 0:512], self.g32[:],
                                                               self.sth[:, 0:512], Alu.subtract))
                        d = p.mark("vec")
                        self.sfree["g32"] = d
                        p.wait("ten", d)
                        p.waits("ten", [self.pfree["T"]])
                        for i_ in range(4):
                            p.add("ten", lambda tn, i_=i_: tn.transpose(
                                self.T_PS[:, i_ * 128:(i_ + 1) * 128],
                                self.sth[:, i_ * 128:(i_ + 1) * 128], self.ident_sb[:]))
                            p.add("ten", lambda tn, i_=i_: tn.transpose(
                                self.T_PS[:, 1024 + i_ * 128:1024 + (i_ + 1) * 128],
                                self.stl[:, i_ * 128:(i_ + 1) * 128], self.ident_sb[:]))
                        dT = p.mark("ten")
                        self.sfree["st"] = dT
                        p.wait("vec", dT)
                        p.add("vec", lambda v, tc=tc: v.tensor_copy(
                            self.vt_h[:, tc * 512:(tc + 1) * 512], self.T_PS[:, 0:512]))
                        p.add("vec", lambda v, tc=tc: v.tensor_copy(
                            self.vt_l[:, tc * 512:(tc + 1) * 512], self.T_PS[:, 1024:1536]))
                        d = p.mark("vec")
                        self.pfree["T"] = d
            qkv_dep = p.mark("vec")

            for hh in range(2):
                p.waits("ten", [qkv_dep, self.pfree["C"]])
                self.attn_head(hh, self.qt_h[64 * hh:64 * hh + 64, :],
                               self.qt_l[64 * hh:64 * hh + 64, :],
                               self.kt_h[64 * hh:64 * hh + 64, :],
                               self.kt_l[64 * hh:64 * hh + 64, :])
            self.pfree["C"] = self._attn_sc

            sc = self._attn_sc
            self.sfree["og"] = sc
            p.waits("syn", [self.sfree.get("wld")])
            p.waits("gps", [self.sfree.get("wld")])
            wo = W["L1_ffn_w_o"]
            d1 = p.dma("syn", out=self.wld[:, 0:1024], in_=wo[g * 128:(g + 1) * 128, :], q="DWT")
            d2 = p.dma("gps", out=self.wld_h[:, 0:1024], in_=wo[g * 128:(g + 1) * 128, :], q="DGH")
            p.waits("vec", [d1, d2, sc])
            p.add("vec", lambda v: v.tensor_tensor(
                self.wld_l[:, 0:1024], self.wld[:, 0:1024], self.wld_h[:, 0:1024], Alu.subtract))
            wo_dep = p.mark("vec")
            p.waits("ten", [sc, wo_dep])
            for tq in range(NT):
                for oc in range(2):
                    bank = (2 * tq + oc) % 2
                    p.waits("ten", [self.pfree[f"A{bank}"]])
                    ps = self.PS_A[:, bank * 512:(bank + 1) * 512]
                    self.mm3(ps, self.ogh[:, tq * 128:(tq + 1) * 128],
                             self.ogl[:, tq * 128:(tq + 1) * 128],
                             self.wld_h[:, oc * 512:(oc + 1) * 512],
                             self.wld_l[:, oc * 512:(oc + 1) * 512], first=True, last=True)
                    d = p.mark("ten")
                    if tq == NT - 1 and oc == 1:
                        self.sfree["wld"] = d
                    p.wait("vec", d)
                    xsl_ = self.x_sb[:, tq * D + oc * 512:tq * D + (oc + 1) * 512]
                    p.add("vec", lambda v, xsl_=xsl_, ps=ps: v.tensor_tensor(xsl_, xsl_, ps, Alu.add))
                    self.pfree[f"A{bank}"] = p.mark("vec")
            self.sfree["qkv_free"] = p.mark("ten")

        p.barrier()
        self.dbg_dump("xattn", self.x_sb[:])
        hb_h = self._av(0, 16384)
        hb_l = self._av(16384, 16384)
        gtile = self._av(32768, 8192, F32)
        etile = self._av(40960, 8192, F32)
        lgT = self._av(49152, 8192, F32, parts=8)
        probs = self._av(57344, 512, F32)
        m1c = self._av(57856, 64, F32)
        m2c = self._av(57920, 64, F32)
        e1c = self._av(57984, 64, F32)
        e2c = self._av(58048, 64, F32)
        smax = self._av(58112, 4, F32)
        ssum = self._av(58116, 4, F32)
        locol = self._av(58120, 4, F32, parts=8)
        hicol = self._av(58124, 4, F32, parts=8)
        midcol = self._av(58128, 4, F32, parts=8)
        gecol = self._av(58132, 4, F32, parts=8)
        cntcol = self._av(58136, 4, F32)
        thcol = self._av(58140, 4, F32)
        throw = self._av(58144, 32, F32, parts=1)
        threp = self._av(58176, 32, F32)
        thsel1 = self._av(58240, 64, F32)
        thsel2 = self._av(58304, 64, F32)
        mke1 = self._av(58368, 64, F32)
        mke2 = self._av(58432, 64, F32)
        ce_all = self._av(58496, 512, F32)
        bis_scr = self._av(0, 4096)

        self.replicate_row(W["L1_ffn_g2"], self.gam_rep, D)
        p.add("vec", lambda v: v.memset(self.bet_rep[:], 0.0))
        self.stats_pass()
        self.ln_scalars("rms", eps=1e-6)
        xn_dep = self.split_transpose_x(self.xnT_scr_h, self.xnT_scr_l, gamma=True)

        p.waits("syn", [self.sfree.get("wld")])
        rd = p.dma("syn", out=self.wld[:, 0:64].rearrange("p (c o) -> p c o", c=8),
                   in_=W["L1_ffn_w_router"].rearrange("(c p) o -> p c o", p=128), q="DWT")
        d2 = p.dma("gps", out=self.wld_h[:, 0:64].rearrange("p (c o) -> p c o", c=8),
                   in_=W["L1_ffn_w_router"].rearrange("(c p) o -> p c o", p=128), q="DGH")
        p.waits("vec", [rd, d2])
        p.add("vec", lambda v: v.tensor_tensor(self.wld_l[:, 0:64], self.wld[:, 0:64],
                                               self.wld_h[:, 0:64], Alu.subtract))
        rdep = p.mark("vec")
        for tc in range(4):
            slab_dep = self.load_xslab(self.xnT_scr_h, self.xnT_scr_l, tc)
            p.waits("ten", [slab_dep, rdep, xn_dep, self.pfree["C"]])
            for c in range(ND):
                self.mm3(self.PS_C[0:8, 0:512], self.wld_h[:, c * 8:(c + 1) * 8],
                         self.wld_l[:, c * 8:(c + 1) * 8],
                         self.xh_v[:, c, 0:512], self.xl_v[:, c, 0:512],
                         first=(c == 0), last=(c == ND - 1))
            d = p.mark("ten")
            self.sfree["xsl"] = d
            p.wait("vec", d)
            p.add("vec", lambda v, tc=tc: v.tensor_copy(lgT[:, tc * 512:(tc + 1) * 512],
                                                        self.PS_C[0:8, 0:512]))
            self.pfree["C"] = p.mark("vec")
        lg_dep = self.pfree["C"]

        p.waits("ten", [lg_dep, self.pfree["T"]])
        TPSF = self.T_PS[:].bitcast(F32)
        prev_d = None
        for t in range(NT):
            p.waits("ten", [prev_d])
            p.add("ten", lambda tn, t=t: tn.transpose(
                TPSF[:, (t % 8) * 8:(t % 8) * 8 + 8],
                lgT[:, t * 128:(t + 1) * 128], self.identf_sb[0:8, 0:8]))
            d = p.mark("ten")
            p.wait("vec", d)
            P8 = probs[:, t * 8:(t + 1) * 8]
            p.add("vec", lambda v, t=t, P8=P8: v.tensor_copy(
                P8, TPSF[:, (t % 8) * 8:(t % 8) * 8 + 8]))
            p.add("vec", lambda v, P8=P8: v.reduce_max(smax[:, 0:1], P8, axis=AX.X, negate=True))
            d = p.mark("vec")
            p.wait("scl", d)
            p.add("scl", lambda s, P8=P8: s.activation(P8, P8, Act.Exp, bias=smax[:, 0:1],
                                                       accum_out=ssum[:, 0:1]))
            d = p.mark("scl")
            p.wait("vec", d)
            p.add("vec", lambda v: v.reciprocal(ssum[:, 0:1], ssum[:, 0:1]))
            p.add("vec", lambda v, P8=P8: v.tensor_scalar_mul(P8, P8, ssum[:, 0:1]))
            u8a = self.u32[:, 0:8]
            u8b = self.u32[:, 8:16]
            u1 = self.u32[:, 16:17]
            p.add("vec", lambda v, P8=P8, t=t: v.reduce_max(m1c[:, t:t + 1], P8, axis=AX.X))
            p.add("vec", lambda v, P8=P8, t=t, u8a=u8a: v.tensor_scalar(
                u8a, P8, m1c[:, t:t + 1], None, Alu.is_equal))
            p.add("vec", lambda v, u8a=u8a, u8b=u8b: v.tensor_tensor(
                u8b, u8a, self.irev_sb[:], Alu.mult))
            p.add("vec", lambda v, u8b=u8b, u1=u1: v.reduce_max(u1, u8b, axis=AX.X))
            p.add("vec", lambda v, t=t, u1=u1: v.tensor_scalar(
                e1c[:, t:t + 1], u1, -1.0, 8.0, Alu.mult, Alu.add))
            p.add("vec", lambda v, u8a=u8a, u1=u1: v.tensor_scalar(
                u8a, self.irev_sb[:], u1, None, Alu.is_equal))
            p.add("vec", lambda v, P8=P8, u8a=u8a, u8b=u8b: v.scalar_tensor_tensor(
                u8b, u8a, -1.0e30, P8, Alu.mult, Alu.add))
            p.add("vec", lambda v, u8b=u8b, t=t: v.reduce_max(m2c[:, t:t + 1], u8b, axis=AX.X))
            p.add("vec", lambda v, u8a=u8a, u8b=u8b, t=t: v.tensor_scalar(
                u8a, u8b, m2c[:, t:t + 1], None, Alu.is_equal))
            p.add("vec", lambda v, u8a=u8a: v.tensor_tensor(
                u8a, u8a, self.irev_sb[:], Alu.mult))
            p.add("vec", lambda v, u8a=u8a, u1=u1: v.reduce_max(u1, u8a, axis=AX.X))
            p.add("vec", lambda v, t=t, u1=u1: v.tensor_scalar(
                e2c[:, t:t + 1], u1, -1.0, 8.0, Alu.mult, Alu.add))
            prev_d = p.mark("vec")
            self.sfree["u32"] = prev_d
        top2_dep = p.mark("vec")

        for k, colsrc in ((0, m1c), (1, m2c)):
            p.dma("syn", out=self.m_loc[k].rearrange("(t q) -> q t", q=128), in_=colsrc[:, 0:16],
                  q="DB4", slow=True)
        for k, colsrc in ((0, e1c), (1, e2c)):
            cd2 = p.dma("syn", out=self.e_loc[k].rearrange("(t q) -> q t", q=128), in_=colsrc[:, 0:16],
                        q="DB4", slow=True)
        p.waits("syn", [top2_dep])
        p.waits("gps", [cd2])
        p.add("gps", lambda g_: g_.collective_compute(
            "AllGather", Alu.bypass, [[0, 1, 2, 3, 4, 5, 6, 7]],
            [self.m_loc[:, :]], [self.m_all[:, :]]))
        agd = p.mark("gps")
        p.add("gps", lambda g_: g_.collective_compute(
            "AllGather", Alu.bypass, [[0, 1, 2, 3, 4, 5, 6, 7]],
            [self.e_loc[:, :]], [self.e_all[:, :]]))
        agd = p.mark("gps")

        p.waits("syn", [agd])
        gd = ed = None
        for e_ in range(8):
            gd = p.dma("syn", out=gtile[16 * e_:16 * e_ + 16, :], in_=self.m_all[:, :], q="DSS")
            ed = p.dma("syn", out=etile[16 * e_:16 * e_ + 16, :], in_=self.e_all[:, :], q="DSS")
        p.waits("vec", [gd, ed])
        p.add("vec", lambda v: v.tensor_scalar(etile[:, :], etile[:, :], self.ecol_sb[:, 0:1],
                                               None, Alu.is_equal))
        p.add("vec", lambda v: v.tensor_tensor(gtile[:, :], gtile[:, :], etile[:, :], Alu.mult))
        p.add("vec", lambda v: v.tensor_scalar(etile[:, :], etile[:, :], 1.0, 1.0e30,
                                               Alu.subtract, Alu.mult))
        p.add("vec", lambda v: v.tensor_tensor(gtile[:, :], gtile[:, :], etile[:, :], Alu.add))
        p.add("vec", lambda v: v.memset(locol[:, 0:1], 0.0))
        p.add("vec", lambda v: v.memset(hicol[:, 0:1], 1.0))
        gt_dep = p.mark("vec")
        for it in range(40):
            p.add("vec", lambda v: v.tensor_tensor(midcol[:, 0:1], locol[:, 0:1], hicol[:, 0:1],
                                                   Alu.add))
            p.add("vec", lambda v: v.tensor_scalar_mul(midcol[:, 0:1], midcol[:, 0:1], 0.5))
            d = p.mark("vec")
            p.waits("ten", [d, self.pfree["C"]])
            p.add("ten", lambda tn: tn.matmul(self.PS_C[:, 0:1], self.g8t_sb[:, :],
                                              midcol[:, 0:1], start=True, stop=True))
            d = p.mark("ten")
            p.wait("vec", d)
            p.add("vec", lambda v: v.tensor_copy(thcol[:, 0:1], self.PS_C[:, 0:1]))
            d = p.mark("vec")
            self.pfree["C"] = d
            p.add("vec", lambda v: v.tensor_scalar(
                bis_scr[:, 0:2048], gtile[:, :], thcol[:, 0:1], 0.0, Alu.is_ge,
                op1=Alu.add, accum_out=cntcol[:, 0:1]))
            d = p.mark("vec")
            p.waits("ten", [d, self.pfree["D"]])
            p.add("ten", lambda tn: tn.matmul(self.PS_D[0:8, 0:1], self.g8_sb[:, :],
                                              cntcol[:, 0:1], start=True, stop=True))
            d = p.mark("ten")
            p.wait("vec", d)
            p.add("vec", lambda v: v.tensor_copy(gecol[:, 0:1], self.PS_D[0:8, 0:1]))
            d = p.mark("vec")
            self.pfree["D"] = d
            p.add("vec", lambda v: v.tensor_single_scalar(gecol[:, 0:1], gecol[:, 0:1], 1280.0,
                                                          Alu.is_ge))
            gnot = self.u32[:, 20:21]
            t1 = self.u32[:, 21:22]
            p.add("vec", lambda v, gnot=gnot: v.tensor_scalar(gnot[0:8, :], gecol[:, 0:1], 1.0, -1.0,
                                                              Alu.subtract, Alu.mult))
            p.add("vec", lambda v, t1=t1: v.tensor_tensor(t1[0:8, :], gecol[:, 0:1], midcol[:, 0:1],
                                                          Alu.mult))
            p.add("vec", lambda v, gnot=gnot: v.tensor_tensor(locol[:, 0:1], gnot[0:8, :],
                                                              locol[:, 0:1], Alu.mult))
            p.add("vec", lambda v, t1=t1: v.tensor_tensor(locol[:, 0:1], locol[:, 0:1], t1[0:8, :],
                                                          Alu.add))
            p.add("vec", lambda v, t1=t1: v.tensor_tensor(t1[0:8, :], gecol[:, 0:1], hicol[:, 0:1],
                                                          Alu.mult))
            p.add("vec", lambda v, gnot=gnot: v.tensor_tensor(hicol[:, 0:1], gnot[0:8, :],
                                                              midcol[:, 0:1], Alu.mult))
            p.add("vec", lambda v, t1=t1: v.tensor_tensor(hicol[:, 0:1], hicol[:, 0:1], t1[0:8, :],
                                                          Alu.add))
            d = p.mark("vec")
        p.waits("ten", [d, self.pfree["C"]])
        p.add("ten", lambda tn: tn.transpose(self.PS_C[0:1, 0:8], locol[:, 0:1],
                                             self.identf_sb[0:8, 0:8]))
        d = p.mark("ten")
        p.wait("vec", d)
        p.add("vec", lambda v: v.tensor_copy(throw[:, 0:8], self.PS_C[0:1, 0:8]))
        d = p.mark("vec")
        self.pfree["C"] = d
        p.waits("ten", [d, self.pfree["D"]])
        p.add("ten", lambda tn: tn.matmul(self.PS_D[:, 0:8], self.ones_sb[:, :], throw[:, 0:8],
                                          start=True, stop=True))
        d = p.mark("ten")
        p.wait("vec", d)
        p.add("vec", lambda v: v.tensor_copy(threp[:, 0:8], self.PS_D[:, 0:8]))
        d = p.mark("vec")
        self.pfree["D"] = d
        for k, (ec, ts_) in enumerate(((e1c, thsel1), (e2c, thsel2))):
            p.add("vec", lambda v, ts_=ts_: v.memset(ts_[:, 0:16], 0.0))
            for e_ in range(8):
                p.add("vec", lambda v, ec=ec, e_=e_: v.tensor_single_scalar(
                    self.u32[:, 0:16], ec[:, 0:16], float(e_), Alu.is_equal))
                p.add("vec", lambda v, ts_=ts_, e_=e_: v.scalar_tensor_tensor(
                    ts_[:, 0:16], self.u32[:, 0:16], threp[:, e_:e_ + 1], ts_[:, 0:16],
                    Alu.mult, Alu.add))
        p.add("vec", lambda v: v.tensor_tensor(mke1[:, 0:16], m1c[:, 0:16], thsel1[:, 0:16],
                                               Alu.is_ge))
        p.add("vec", lambda v: v.tensor_tensor(mke1[:, 0:16], mke1[:, 0:16], m1c[:, 0:16],
                                               Alu.mult))
        p.add("vec", lambda v: v.tensor_tensor(mke2[:, 0:16], m2c[:, 0:16], thsel2[:, 0:16],
                                               Alu.is_ge))
        p.add("vec", lambda v: v.tensor_tensor(mke2[:, 0:16], mke2[:, 0:16], m2c[:, 0:16],
                                               Alu.mult))
        for e_ in range(8):
            ce = ce_all[:, 16 * e_:16 * e_ + 16]
            p.add("vec", lambda v, ce=ce: v.memset(ce, 0.0))
            for ec, mke in ((e1c, mke1), (e2c, mke2)):
                p.add("vec", lambda v, ec=ec, e_=e_: v.tensor_single_scalar(
                    self.u32[:, 0:16], ec[:, 0:16], float(e_), Alu.is_equal))
                p.add("vec", lambda v, mke=mke: v.tensor_tensor(
                    self.u32[:, 16:32], self.u32[:, 0:16], mke[:, 0:16], Alu.mult))
                p.add("vec", lambda v, ce=ce: v.tensor_tensor(ce, ce, self.u32[:, 16:32], Alu.add))
        coeff_dep = p.mark("vec")
        self.sfree["u32"] = coeff_dep
        self.dbg_dump("m1c", m1c[:, 0:16], coeff_dep)
        self.dbg_dump("m2c", m2c[:, 0:16], coeff_dep)
        self.dbg_dump("e1c", e1c[:, 0:16], coeff_dep)
        self.dbg_dump("e2c", e2c[:, 0:16], coeff_dep)
        self.dbg_dump("threp", threp[:, 0:8], coeff_dep)
        self.dbg_dump("ce", ce_all[:, 0:128], coeff_dep)
        self.dbg_dump("probs", probs[:, 0:128], coeff_dep)

        for e_ in range(8):
            for b in range(4):
                slab_dep = self.load_xslab(self.xnT_scr_h, self.xnT_scr_l, b)
                for fc in range(16):
                    p.waits("syn", [self.sfree.get("wld")])
                    p.waits("gps", [self.sfree.get("wld")])
                    d1 = d2 = None
                    for r in range(ND):
                        slc = W["L1_ffn_w1"][e_, r * 128:(r + 1) * 128, fc * 128:(fc + 1) * 128]
                        d1 = p.dma("syn", out=self.wld[:, r * 128:(r + 1) * 128], in_=slc, q="DWT")
                        d2 = p.dma("gps", out=self.wld_h[:, r * 128:(r + 1) * 128], in_=slc, q="DGH")
                    p.waits("vec", [d1, d2])
                    p.add("vec", lambda v: v.tensor_tensor(
                        self.wld_l[:, 0:1024], self.wld[:, 0:1024], self.wld_h[:, 0:1024],
                        Alu.subtract))
                    wdep = p.mark("vec")
                    bank = fc % 4
                    p.waits("ten", [slab_dep, wdep, self.pfree[f"A{bank}"]])
                    ps = self.PS_A[:, bank * 512:(bank + 1) * 512]
                    for c in range(ND):
                        self.mm3(ps, self.wld_h[:, c * 128:(c + 1) * 128],
                                 self.wld_l[:, c * 128:(c + 1) * 128],
                                 self.xh_v[:, c, 0:512], self.xl_v[:, c, 0:512],
                                 first=(c == 0), last=(c == ND - 1))
                    d = p.mark("ten")
                    self.sfree["wld"] = d
                    self.sfree["xsl"] = d
                    p.wait("scl", d)
                    p.waits("scl", [self.sfree.get("g32")])
                    p.add("scl", lambda s, ps=ps: s.activation(self.g32[:], ps, Act.Relu))
                    d = p.mark("scl")
                    self.pfree[f"A{bank}"] = d
                    p.wait("vec", d)
                    col = fc * 512
                    p.add("vec", lambda v, col=col: v.tensor_copy(hb_h[:, col:col + 512], self.g32[:]))
                    p.add("vec", lambda v, col=col: v.tensor_tensor(
                        hb_l[:, col:col + 512], self.g32[:], hb_h[:, col:col + 512], Alu.subtract))
                    d = p.mark("vec")
                    self.sfree["g32"] = d
                hb_dep = p.mark("vec")
                banks = [self.PS_A[:, i * 512:(i + 1) * 512] for i in range(4)] + \
                        [self.T_PS[:].bitcast(F32)[:, 0:512], self.T_PS[:].bitcast(F32)[:, 512:1024],
                         self.PS_C[:, 0:512], self.PS_D[:, 0:512]]
                p.waits("ten", [hb_dep, self.pfree["A0"], self.pfree["A1"], self.pfree["A2"],
                                self.pfree["A3"], self.pfree["T"], self.pfree["C"], self.pfree["D"]])
                for fc in range(16):
                    p.waits("syn", [self.sfree.get("wld")])
                    p.waits("gps", [self.sfree.get("wld")])
                    slc = W["L1_ffn_w2"][e_, fc * 128:(fc + 1) * 128, :]
                    d1 = p.dma("syn", out=self.wld[:, 0:1024], in_=slc, q="DWT")
                    d2 = p.dma("gps", out=self.wld_h[:, 0:1024], in_=slc, q="DGH")
                    p.waits("vec", [d1, d2])
                    p.add("vec", lambda v: v.tensor_tensor(
                        self.wld_l[:, 0:1024], self.wld[:, 0:1024], self.wld_h[:, 0:1024],
                        Alu.subtract))
                    wdep = p.mark("vec")
                    p.waits("ten", [wdep])
                    for tt in range(4):
                        hh_ = hb_h[:, fc * 512 + tt * 128:fc * 512 + (tt + 1) * 128]
                        hl_ = hb_l[:, fc * 512 + tt * 128:fc * 512 + (tt + 1) * 128]
                        for oc in range(2):
                            self.mm3(banks[tt * 2 + oc], None, None, None, None,
                                     first=(fc == 0), last=(fc == 15),
                                     passes=[(hh_, self.wld_h[:, oc * 512:(oc + 1) * 512]),
                                             (hl_, self.wld_h[:, oc * 512:(oc + 1) * 512]),
                                             (hh_, self.wld_l[:, oc * 512:(oc + 1) * 512])])
                    d = p.mark("ten")
                    self.sfree["wld"] = d
                d = p.mark("ten")
                p.wait("vec", d)
                p.waits("vec", [coeff_dep])
                for tt in range(4):
                    t_g = 4 * b + tt
                    for oc in range(2):
                        xsl_ = self.x_sb[:, t_g * D + oc * 512:t_g * D + (oc + 1) * 512]
                        p.add("vec", lambda v, xsl_=xsl_, tt=tt, oc=oc, e_=e_, t_g=t_g:
                              v.scalar_tensor_tensor(
                                  xsl_, banks[tt * 2 + oc],
                                  ce_all[:, 16 * e_ + t_g:16 * e_ + t_g + 1],
                                  xsl_, Alu.mult, Alu.add))
                d = p.mark("vec")
                for kk in ("A0", "A1", "A2", "A3", "T", "C", "D"):
                    self.pfree[kk] = d
        for t in range(NT):
            p.waits("syn", [self.sfree.get("u32")])
            ld = p.dma("syn", out=self.u32[:], in_=self.x_scr[:, t * D:(t + 1) * D], q="DSL")
            p.waits("vec", [ld])
            xs = self.x_sb[:, t * D:(t + 1) * D]
            p.add("vec", lambda v, xs=xs: v.tensor_tensor(xs, xs, self.u32[:], Alu.add))
            self.sfree["u32"] = p.mark("vec")

    def finalize(self):
        p = self.p
        p.barrier()
        for t in range(NT):
            p.dma("syn", out=self.xout[t * 128:(t + 1) * 128, :],
                  in_=self.x_sb[:, t * D:(t + 1) * D], q="DO")
        p.ops["syn"].append(lambda e: e.wait_ge(p.sem["DO"], p.ctr["DO"]))

        with ExitStack() as sctx:
            for s in Prog.SEMS:
                p.sem[s] = sctx.enter_context(self.nc.semaphore("sem_" + s))
            with self.nc.Block() as block:
                p.emit(block)
        self.ctx.close()
        return self.nc

    def build(self):
        self.init_loads()
        if "l0a" in self.stages:
            self.l0_attn()
        if "l0f" in self.stages:
            self.p.barrier()
            self.l0_ffn()
        if "l1a" in self.stages:
            self.p.barrier()
            self.l1_attn()
        if "l1f" in self.stages:
            self.p.barrier()
            self.l1_ffn()
        return self.finalize()


_NC_CACHE = {}


def _get_program(stages):
    key = tuple(stages)
    if key not in _NC_CACHE:
        _NC_CACHE[key] = Builder(stages).build()
    return _NC_CACHE[key]


def kernel(**inputs):
    nc = _get_program(STAGES)
    x = np.ascontiguousarray(np.asarray(inputs["L0_attn_x"], dtype=np.float32))
    wmap = {k: np.ascontiguousarray(np.asarray(v, dtype=np.float32))
            for k, v in inputs.items() if k != "L0_attn_x"}
    in_maps = []
    for c in range(8):
        m = {"x_in": x[c]}
        m.update(wmap)
        in_maps.append(m)
    res = run_bass_kernel_spmd(nc, in_maps, core_ids=list(range(8)))
    return np.stack([r["x_out"] for r in res.results], axis=0)


# version 04
# speedup vs baseline: 1.0081x; 1.0081x over previous
import numpy as np
import ml_dtypes
from contextlib import ExitStack

import concourse.bass as bass
import concourse.mybir as mybir
from concourse.bass_utils import run_bass_kernel_spmd

F32 = mybir.dt.float32
BF16 = mybir.dt.bfloat16
Alu = mybir.AluOpType
Act = mybir.ActivationFunctionType
AX = mybir.AxisListType
BF = ml_dtypes.bfloat16

S = 2048
D = 1024
NT = S // 128
ND = D // 128
NH = 16
BIAS_S = 32
NEG = -3.0e38

STAGES = ("l0a", "l0f", "l1a", "l1f")


def _split_const(a):
    hi = a.astype(BF)
    lo = (a - hi.astype(np.float32)).astype(BF)
    return hi, lo


def _interp_matrix(n, s):
    pos = np.arange(n, dtype=np.float32) * np.float32((s - 1) / (n - 1))
    i0 = np.floor(pos).astype(np.int32)
    i1 = np.minimum(i0 + 1, s - 1)
    frac = (pos - i0.astype(np.float32)).astype(np.float32)
    M = np.zeros((n, s), dtype=np.float32)
    np.add.at(M, (np.arange(n), i0), (1.0 - frac))
    np.add.at(M, (np.arange(n), i1), frac)
    return M


class Prog:
    ENGS = ("ten", "vec", "scl", "gps", "syn")
    SEMS = ("T", "V", "S", "G", "DIO", "DWT", "DGH", "DAU", "DSL", "DS2", "DSS", "DGR", "DRR", "DB4", "DO")
    DMA_SEMS = frozenset(SEMS[4:])

    def __init__(self):
        self.ops = {e: [] for e in self.ENGS}
        self.ctr = {s: 0 for s in self.SEMS}
        self.waited = {e: {s: 0 for s in self.SEMS} for e in self.ENGS}
        self.own = {"ten": "T", "vec": "V", "scl": "S", "gps": "G", "syn": None}
        self.sem = {}
        self.dirty = {e: False for e in self.ENGS}
        self.last_real = {e: None for e in self.ENGS}
        self.last_mark = {e: None for e in self.ENGS}
        self.chain = {"vec": True, "scl": True, "gps": True}
        self.chain_val = {e: 0 for e in self.ENGS}

    def add(self, eng, fn):
        if self.chain.get(eng):
            sname = self.own[eng]
            prevv = self.chain_val[eng]
            if prevv:
                self.ops[eng].append(lambda e, sn=sname, v=prevv: e.wait_ge(self.sem[sn], v))
            self.ctr[sname] += 1
            val = self.ctr[sname]
            self.ops[eng].append(lambda e, fn=fn, sn=sname: fn(e).then_inc(self.sem[sn], 1))
            self.chain_val[eng] = val
            self.last_mark[eng] = (sname, val)
            self.dirty[eng] = False
            return
        self.ops[eng].append(fn)
        self.last_real[eng] = len(self.ops[eng]) - 1
        self.dirty[eng] = True

    def mark(self, eng):
        if self.chain.get(eng):
            return self.last_mark[eng]
        if not self.dirty[eng]:
            return self.last_mark[eng]
        sname = self.own[eng]
        self.ctr[sname] += 1
        val = self.ctr[sname]
        i = self.last_real[eng]
        prev = self.ops[eng][i]
        self.ops[eng][i] = lambda e, prev=prev, sn=sname: prev(e).then_inc(self.sem[sn], 1)
        self.dirty[eng] = False
        self.last_mark[eng] = (sname, val)
        return (sname, val)

    def wait(self, eng, dep):
        if dep is None:
            return
        sname, val = dep
        if sname in self.DMA_SEMS and val < self.ctr[sname]:
            val = self.ctr[sname]
        if self.own[eng] == sname:
            return
        if self.waited[eng][sname] >= val:
            return
        self.waited[eng][sname] = val
        self.ops[eng].append(lambda e, sn=sname, v=val: e.wait_ge(self.sem[sn], v))

    def waits(self, eng, deps):
        for d in deps:
            self.wait(eng, d)

    def wait_force(self, eng, dep):
        if dep is None:
            return
        sname, val = dep
        self.ops[eng].append(lambda e, sn=sname, v=val: e.wait_ge(self.sem[sn], v))

    def dma(self, eng, out, in_, q=None, slow=False):
        assert q is not None, "specify a DMA stream"
        qname = q
        self.ctr[qname] += 16
        val = self.ctr[qname]
        self.ops[eng].append(
            lambda e, out=out, in_=in_, qn=qname, slow=slow: e.dma_start(
                out=out, in_=in_, allow_slow_non_contiguous=slow).then_inc(self.sem[qn], 16))
        return (qname, val)

    def barrier(self):
        marks = [self.mark(e) for e in ("ten", "vec", "scl", "gps")]
        for e in self.ENGS:
            for m in marks:
                self.wait(e, m)

    def emit(self, block):
        def mk(name):
            def body(eng):
                for fn in self.ops[name]:
                    fn(eng)
            return body
        block.sync(mk("syn"))
        block.tensor(mk("ten"))
        block.vector(mk("vec"))
        block.scalar(mk("scl"))
        block.gpsimd(mk("gps"))


class Builder:
    def __init__(self, stages):
        self.stages = stages
        nc = bass.Bass("TRN2", target_bir_lowering=False)
        self.nc = nc
        self.p = Prog()
        self.ctx = ExitStack()

        self.xin = nc.dram_tensor("x_in", [S, D], F32, kind="ExternalInput").ap()
        self.xout = nc.dram_tensor("x_out", [S, D], F32, kind="ExternalOutput").ap()
        WSHAPES = {
            "L0_attn_Wq": (D, D), "L0_attn_bq": (D,), "L0_attn_Wk": (D, D), "L0_attn_bk": (D,),
            "L0_attn_Wv": (D, D), "L0_attn_bv": (D,), "L0_attn_Wo": (D, D), "L0_attn_bo": (D,),
            "L0_attn_gamma": (D,), "L0_attn_beta": (D,), "L0_attn_rel_bias": (NH, BIAS_S, BIAS_S),
            "L0_ffn_w_up": (4096, D), "L0_ffn_w_down": (D, 4096),
            "L1_attn_in_proj_w": (4096, D), "L1_attn_conv_w": (2048, 256, 4),
            "L1_attn_conv_b": (2048,), "L1_attn_ln_g": (2048,), "L1_attn_ln_b": (2048,),
            "L1_attn_fc_w": (D, 2048), "L1_attn_fc_b": (D,), "L1_attn_cproj_w": (2048, D),
            "L1_attn_cproj_b": (2048,), "L1_attn_out_w": (D, 2048),
            "L1_ffn_g1": (D,), "L1_ffn_w_qkv": (D, 3 * D), "L1_ffn_w_o": (D, D),
            "L1_ffn_g2": (D,), "L1_ffn_w_router": (D, 8),
            "L1_ffn_w1": (8, D, 2048), "L1_ffn_w2": (8, 2048, D),
        }
        self.W = {k: nc.dram_tensor(k, list(v), F32, kind="ExternalInput").ap()
                  for k, v in WSHAPES.items()}

        self.ident_bf_c = nc.inline_tensor(np.eye(128, dtype=np.float32).astype(BF), "c_identbf")
        self.ident_f_c = nc.inline_tensor(np.eye(128, dtype=np.float32), "c_identf")
        self.ones_c = nc.inline_tensor(np.ones((1, 128), np.float32), "c_ones")
        self.onesbf_c = nc.inline_tensor(np.ones((1, 128), np.float32).astype(BF), "c_onesbf")
        self.onescol_c = nc.inline_tensor(np.ones((128, 1), np.float32), "c_onescol")
        self.mask_c = nc.inline_tensor(np.triu(np.full((128, 128), NEG, np.float32), k=1), "c_mask")
        M = _interp_matrix(S, BIAS_S)
        Mq = M.copy()
        Mq[-1, :] = 0.0
        mqh, mql = _split_const(np.ascontiguousarray(Mq.T))
        self.mqt_h_c = nc.inline_tensor(mqh, "c_mqt_h")
        self.mqt_l_c = nc.inline_tensor(mql, "c_mqt_l")
        self.mt_c = nc.inline_tensor(np.ascontiguousarray(M.T), "c_mt")
        theta = (10000.0 ** (-2.0 * (np.arange(32) / 64.0))).astype(np.float64)
        tpos = np.arange(S, dtype=np.float64)
        fr = tpos[None, :] * theta[:, None]
        cosd = np.cos(fr).astype(np.float32)
        sind = np.sin(fr).astype(np.float32)
        C1 = np.zeros((128, S), np.float32)
        C2 = np.zeros((128, S), np.float32)
        for p_ in range(128):
            f = (p_ % 64) // 2
            C1[p_] = cosd[f]
            C2[p_] = (-sind[f]) if (p_ % 2 == 0) else sind[f]
        self.rope1_c = nc.inline_tensor(C1, "c_rope1")
        self.rope2_c = nc.inline_tensor(C2, "c_rope2")
        self.irev_c = nc.inline_tensor(
            np.broadcast_to((8.0 - np.arange(8, dtype=np.float32)), (128, 8)).copy(), "c_irev")
        self.ecol_c = nc.inline_tensor((np.arange(128, dtype=np.float32) // 16)[:, None].copy(), "c_ecol")
        G8 = np.zeros((128, 8), np.float32)
        for p_ in range(128):
            G8[p_, p_ // 16] = 1.0
        self.g8_c = nc.inline_tensor(G8, "c_g8")
        self.g8t_c = nc.inline_tensor(np.ascontiguousarray(G8.T), "c_g8t")

        e = self.ctx.enter_context
        sb = lambda nm, shp, dt=BF16: e(nc.sbuf_tensor(nm, shp, dt))
        self.x_sb = sb("x_sb", [128, NT * D], F32)
        self.xsl_h = sb("xsl_h", [128, ND * 512])
        self.xsl_l = sb("xsl_l", [128, ND * 512])
        self.xnT_scr_h = nc.dram_tensor("xnT_scr_h", [128, ND * S], BF16, kind="Internal").ap()
        self.xnT_scr_l = nc.dram_tensor("xnT_scr_l", [128, ND * S], BF16, kind="Internal").ap()
        self.xnT32_scr = nc.dram_tensor("xnT32_scr", [128, ND * S], F32, kind="Internal").ap()
        self.wupT_scr = nc.dram_tensor("wupT_scr", [128, 32 * 1024], BF16, kind="Internal").ap()
        self.wdownT_scr = nc.dram_tensor("wdownT_scr", [128, 32 * 1024], BF16, kind="Internal").ap()
        self.x_scr = nc.dram_tensor("x_scr", [128, NT * D], F32, kind="Internal").ap()
        self.m_loc = nc.dram_tensor("m_loc", [2, S], F32, kind="Internal").ap()
        self.e_loc = nc.dram_tensor("e_loc", [2, S], F32, kind="Internal").ap()
        self.m_all = nc.dram_tensor("m_all", [16, S], F32, kind="Internal").ap()
        self.e_all = nc.dram_tensor("e_all", [16, S], F32, kind="Internal").ap()
        self.u32 = sb("u32", [128, D], F32)
        self.sth = sb("sth", [128, D])
        self.stl = sb("stl", [128, D])
        for nm in ("sm", "sq", "mu", "rho", "mb", "tt1", "tt2", "r1s", "mb1s"):
            setattr(self, nm, sb(nm, [128, NT], F32))
        self.wld = sb("wld", [128, 1024], F32)
        self.wld_h = sb("wld_h", [128, 1024])
        self.wld_l = sb("wld_l", [128, 1024])
        self.wT_h = sb("wT_h", [128, 3 * 1024])
        self.wT_l = sb("wT_l", [128, 3 * 1024])
        self.g32 = sb("g32", [128, 512], F32)
        self.relbT = sb("relbT", [32, 128], F32)
        self.rb_st = sb("rb_st", [32, 128], F32)
        self.mneg = sb("mneg", [128, 1], F32)
        self.lsum = sb("lsum", [128, 1], F32)
        self.linv = sb("linv", [128, 1], F32)
        self.AR = sb("arena", [128, 43008])
        self.qt_h = self._av(0, 4096)
        self.qt_l = self._av(4096, 4096)
        self.kt_h = self._av(8192, 4096)
        self.kt_l = self._av(12288, 4096)
        self.vt_h = self._av(16384, 4096)
        self.vt_l = self._av(20480, 4096)
        self.qaug_h = self._av(24576, 4096, parts=96)
        self.qaug_l = self._av(28672, 4096, parts=96)
        self.kaug_h = self._av(32768, 4096, parts=96)
        self.kaug_l = self._av(36864, 4096, parts=96)
        self.b4_h = self._av(40960, 4096)
        self.b4_l = self._av(45056, 4096)
        self.p32 = self._av(49152, 8192, F32)
        self.gam_rep = self._av(49152, 4096, F32)
        self.bet_rep = self._av(53248, 4096, F32)
        self.ph = self._av(57344, 4096)
        self.pl = self._av(61440, 4096)
        self.pth = self._av(65536, 2048)
        self.ptl = self._av(67584, 2048)
        self.ogh = self._av(69632, 4096)
        self.ogl = self._av(73728, 4096)
        self.brow_h = self._av(77824, 4096, parts=1)
        self.brow_l = self._av(81920, 4096, parts=1)
        self.xsl32 = self._av(59392, 16384, F32)
        self.ident_sb = sb("ident_sb", [128, 128])
        self.identf_sb = sb("identf_sb", [128, 128], F32)
        self.ones_sb = sb("ones_sb", [1, 128], F32)
        self.onescol_sb = sb("onescol_sb", [128, 1], F32)
        self.onesb_sb = sb("onesb_sb", [1, 128])
        self.mask_sb = sb("mask_sb", [128, 128], F32)
        self.bias_fm = sb("bias_fm", [128, 4 * ND], F32)
        self.bias2 = sb("bias2", [128, 72], F32)
        self.irev_sb = sb("irev_sb", [128, 8], F32)
        self.ecol_sb = sb("ecol_sb", [128, 1], F32)
        self.g8_sb = sb("g8_sb", [128, 8], F32)
        self.g8t_sb = sb("g8t_sb", [8, 128], F32)
        self.grow = sb("grow", [1, 512], F32)

        self.PS_A = e(nc.psum_tensor("PS_A", [128, 2048], F32))
        self.T_PS = e(nc.psum_tensor("T_PS", [128, 2048], BF16))
        self.PS_C = e(nc.psum_tensor("PS_C", [128, 512], F32))
        self.PS_D = e(nc.psum_tensor("PS_D", [128, 512], F32))

        self.pfree = {k: None for k in ("A0", "A1", "A2", "A3", "T", "C", "D")}
        self.sfree = {}

        self.xh_v = self.xsl_h[:].rearrange("p (c s) -> p c s", c=ND)
        self.xl_v = self.xsl_l[:].rearrange("p (c s) -> p c s", c=ND)

    DBG = False

    def dbg_dump(self, name, ap, dep=None):
        if not self.DBG:
            return
        self.p.waits("syn", [dep, ("DB4", self.p.ctr["DB4"])])
        t = self.nc.dram_tensor("dbg_" + name, list(ap.shape), ap.dtype, kind="ExternalOutput").ap()
        dd = self.p.dma("syn", out=t, in_=ap, q="DB4")
        for eng in ("vec", "scl", "ten"):
            self.p.wait(eng, dd)

    def _av(self, off_bytes, nbytes, dt=None, parts=128, p0=0):
        v = self.AR[p0:p0 + parts, off_bytes // 2:(off_bytes + nbytes) // 2]
        if dt is F32:
            v = v.bitcast(F32)
        return v

    def init_loads(self):
        p = self.p
        for t in range(NT):
            p.dma("syn", out=self.x_sb[:, t * D:(t + 1) * D], in_=self.xin[t * 128:(t + 1) * 128, :], q="DIO")
        p.dma("syn", out=self.ident_sb[:], q="DIO", in_=self.ident_bf_c.ap())
        p.dma("syn", out=self.identf_sb[:], q="DIO", in_=self.ident_f_c.ap())
        p.dma("syn", out=self.ones_sb[:], q="DIO", in_=self.ones_c.ap())
        p.dma("syn", out=self.onescol_sb[:], q="DIO", in_=self.onescol_c.ap())
        p.dma("syn", out=self.onesb_sb[:], q="DIO", in_=self.onesbf_c.ap())
        p.dma("syn", out=self.mask_sb[:], q="DIO", in_=self.mask_c.ap())
        p.dma("syn", out=self.qaug_h[64:96, :], in_=self.mqt_h_c.ap(), q="DIO")
        dep = p.dma("syn", out=self.qaug_l[64:96, :], in_=self.mqt_l_c.ap(), q="DIO")
        for eng in ("ten", "vec", "scl", "gps"):
            p.wait(eng, dep)

    def replicate_row(self, dram_vec, dst, n):
        p = self.p
        for c0 in range(0, n, 512):
            w_ = min(512, n - c0)
            p.waits("syn", [self.sfree.get("grow")])
            dd = p.dma("syn", out=self.grow[:, 0:w_],
                       in_=dram_vec[c0:c0 + w_].rearrange("d -> () d"), q="DGR")
            p.waits("ten", [self.pfree["C"], dd])
            p.add("ten", lambda t, w_=w_: t.matmul(
                self.PS_C[:, 0:w_], self.ones_sb[:, :], self.grow[:, 0:w_],
                start=True, stop=True))
            d = p.mark("ten")
            self.sfree["grow"] = d
            p.wait("vec", d)
            p.add("vec", lambda v, c0=c0, w_=w_: v.tensor_copy(dst[:, c0:c0 + w_], self.PS_C[:, 0:w_]))
            self.pfree["C"] = p.mark("vec")

    def split_row(self, dram_vec, n):
        p = self.p
        d = None
        for c0 in range(0, n, 512):
            w_ = min(512, n - c0)
            p.waits("syn", [self.sfree.get("grow")])
            dd = p.dma("syn", out=self.grow[:, 0:w_],
                       in_=dram_vec[c0:c0 + w_].rearrange("d -> () d"), q="DGR")
            p.waits("vec", [dd])
            p.add("vec", lambda v, c0=c0, w_=w_: v.tensor_copy(
                self.brow_h[:, c0:c0 + w_], self.grow[:, 0:w_]))
            p.add("vec", lambda v, c0=c0, w_=w_: v.tensor_tensor(
                self.brow_l[:, c0:c0 + w_], self.grow[:, 0:w_], self.brow_h[:, c0:c0 + w_],
                Alu.subtract))
            d = p.mark("vec")
            self.sfree["grow"] = d
        return d

    def stats_pass(self):
        p = self.p
        sd = None
        for t in range(NT):
            xs = self.x_sb[:, t * D:(t + 1) * D]
            p.add("vec", lambda v, xs=xs, t=t: v.reduce_sum(self.sm[:, t:t + 1], xs, axis=AX.X))
            p.wait_force("scl", sd)
            p.add("scl", lambda s, xs=xs, t=t: s.activation(
                self.u32[:], xs, Act.Square, accum_out=self.sq[:, t:t + 1]))
            sd = p.mark("scl")
        self._stats_dep = sd
        return (p.mark("vec"), sd)

    def ln_scalars(self, mode, eps=1e-5, eps2=1e-5):
        p = self.p
        p.wait("vec", getattr(self, "_stats_dep", None))
        if mode == "rms":
            p.add("vec", lambda v: v.tensor_scalar_mul(self.tt1[:], self.sq[:], 1.0 / D))
            p.add("vec", lambda v: v.tensor_scalar_add(self.tt1[:], self.tt1[:], eps))
            d = p.mark("vec")
            p.wait("scl", d)
            p.add("scl", lambda s: s.activation(self.tt1[:], self.tt1[:], Act.Sqrt))
            d = p.mark("scl")
            p.wait("vec", d)
            p.add("vec", lambda v: v.reciprocal(self.rho[:], self.tt1[:]))
            p.add("vec", lambda v: v.tensor_scalar_mul(self.mb[:], self.rho[:], 0.0))
            self._ln_dep = p.mark("vec")
            return self._ln_dep
        p.add("vec", lambda v: v.tensor_scalar_mul(self.mu[:], self.sm[:], 1.0 / D))
        p.add("vec", lambda v: v.tensor_scalar_mul(self.tt1[:], self.sq[:], 1.0 / D))
        p.add("vec", lambda v: v.tensor_tensor(self.tt2[:], self.mu[:], self.mu[:], Alu.mult))
        p.add("vec", lambda v: v.tensor_tensor(self.tt1[:], self.tt1[:], self.tt2[:], Alu.subtract))
        p.add("vec", lambda v: v.tensor_scalar_add(self.tt2[:], self.tt1[:], eps))
        d = p.mark("vec")
        p.wait("scl", d)
        p.add("scl", lambda s: s.activation(self.tt2[:], self.tt2[:], Act.Sqrt))
        d = p.mark("scl")
        p.wait("vec", d)
        p.add("vec", lambda v: v.reciprocal(self.tt2[:], self.tt2[:]))
        p.add("vec", lambda v: v.tensor_copy(self.r1s[:], self.tt2[:]))
        p.add("vec", lambda v: v.tensor_tensor(self.mb1s[:], self.mu[:], self.r1s[:], Alu.mult))
        p.add("vec", lambda v: v.tensor_scalar_mul(self.mb1s[:], self.mb1s[:], -1.0))
        if mode == "lnln":
            p.add("vec", lambda v: v.tensor_tensor(self.rho[:], self.tt2[:], self.tt2[:], Alu.mult))
            p.add("vec", lambda v: v.tensor_tensor(self.rho[:], self.rho[:], self.tt1[:], Alu.mult))
            p.add("vec", lambda v: v.tensor_scalar_add(self.rho[:], self.rho[:], eps2))
            d = p.mark("vec")
            p.wait("scl", d)
            p.add("scl", lambda s: s.activation(self.rho[:], self.rho[:], Act.Sqrt))
            d = p.mark("scl")
            p.wait("vec", d)
            p.add("vec", lambda v: v.reciprocal(self.rho[:], self.rho[:]))
            p.add("vec", lambda v: v.tensor_tensor(self.rho[:], self.rho[:], self.tt2[:], Alu.mult))
        elif mode == "ln_rms":
            d = p.mark("vec")
            p.wait("scl", d)
            p.add("scl", lambda s: s.activation(self.tt1[:], self.tt1[:], Act.Sqrt))
            d = p.mark("scl")
            p.wait("vec", d)
            p.add("vec", lambda v: v.tensor_tensor(self.tt1[:], self.tt1[:], self.tt2[:], Alu.mult))
            p.add("vec", lambda v: v.tensor_scalar_add(self.tt1[:], self.tt1[:], 1e-8))
            p.add("vec", lambda v: v.reciprocal(self.tt1[:], self.tt1[:]))
            p.add("vec", lambda v: v.tensor_tensor(self.rho[:], self.tt1[:], self.tt2[:], Alu.mult))
        else:
            p.add("vec", lambda v: v.tensor_copy(self.rho[:], self.tt2[:]))
        p.add("vec", lambda v: v.tensor_tensor(self.mb[:], self.mu[:], self.rho[:], Alu.mult))
        p.add("vec", lambda v: v.tensor_scalar_mul(self.mb[:], self.mb[:], -1.0))
        self._ln_dep = p.mark("vec")
        return self._ln_dep

    def split_transpose_x(self, dst_h, dst_l, gamma, tiles=None, span=S, nchunk=ND,
                          to_sbuf=False):
        p = self.p
        if tiles is None:
            tiles = range(NT)
        tiles = list(tiles)
        t0 = tiles[0]
        last = None
        for t in tiles:
            xs = self.x_sb[:, t * D:(t + 1) * D]
            p.waits("scl", [self.sfree.get("u32"), getattr(self, "_ln_dep", None)])
            p.add("scl", lambda s, xs=xs, t=t: s.activation(
                self.u32[:], xs, Act.Identity, scale=self.rho[:, t:t + 1], bias=self.mb[:, t:t + 1]))
            d = p.mark("scl")
            p.wait("vec", d)
            src = self.u32
            if gamma:
                p.add("vec", lambda v: v.tensor_tensor(self.u32[:], self.u32[:], self.gam_rep[:], Alu.mult))
                p.add("vec", lambda v: v.tensor_tensor(self.u32[:], self.u32[:], self.bet_rep[:], Alu.add))
            p.waits("vec", [self.sfree.get("st")])
            p.add("vec", lambda v, src=src: v.tensor_copy(self.sth[:], src[:]))
            p.add("vec", lambda v, src=src: v.tensor_tensor(self.stl[:], src[:], self.sth[:], Alu.subtract))
            d = p.mark("vec")
            self.sfree["u32"] = d
            p.wait("ten", d)
            p.waits("ten", [self.pfree["T"]])
            for c in range(ND):
                p.add("ten", lambda tn, c=c: tn.transpose(
                    self.T_PS[:, c * 128:(c + 1) * 128], self.sth[:, c * 128:(c + 1) * 128], self.ident_sb[:]))
            for c in range(ND):
                p.add("ten", lambda tn, c=c: tn.transpose(
                    self.T_PS[:, 1024 + c * 128:1024 + (c + 1) * 128],
                    self.stl[:, c * 128:(c + 1) * 128], self.ident_sb[:]))
            d = p.mark("ten")
            self.sfree["st"] = d
            p.wait("vec", d)
            col = 128 * (t - t0)
            nc_cols = nchunk * 128
            if to_sbuf:
                dh_ = dst_h.rearrange("p (c s) -> p c s", c=nchunk)[:, :, col:col + 128]
                dl_ = dst_l.rearrange("p (c s) -> p c s", c=nchunk)[:, :, col:col + 128]
                p.waits("vec", [self.sfree.get("xsl")])
                p.add("vec", lambda v, dh_=dh_: v.tensor_copy(
                    dh_, self.T_PS[:, 0:nc_cols].rearrange("p (c k) -> p c k", c=nchunk)))
                p.add("vec", lambda v, dl_=dl_: v.tensor_copy(
                    dl_, self.T_PS[:, 1024:1024 + nc_cols].rearrange("p (c k) -> p c k", c=nchunk)))
                last = p.mark("vec")
                self.pfree["T"] = last
                continue
            p.waits("vec", [self.sfree.get("xsl")])
            p.add("vec", lambda v: v.tensor_copy(self.xsl_h[:, 0:1024], self.T_PS[:, 0:1024]))
            p.add("vec", lambda v: v.tensor_copy(self.xsl_l[:, 0:1024], self.T_PS[:, 1024:2048]))
            last = p.mark("vec")
            self.pfree["T"] = last
            p.waits("syn", [last])
            dh_ = dst_h.rearrange("p (c s) -> p c s", c=nchunk)[:, :, col:col + 128]
            dl_ = dst_l.rearrange("p (c s) -> p c s", c=nchunk)[:, :, col:col + 128]
            p.dma("syn", out=dh_, in_=self.xsl_h[:, 0:1024].rearrange("p (c k) -> p c k", c=nchunk), q="DSS")
            sd = p.dma("syn", out=dl_, in_=self.xsl_l[:, 0:1024].rearrange("p (c k) -> p c k", c=nchunk), q="DSS")
            self.sfree["xsl"] = sd
            last = sd
        return last

    def load_xslab(self, scr_h, scr_l, tc, span=S, nchunk=ND):
        p = self.p
        p.waits("syn", [self.sfree.get("xsl")])
        sh = scr_h.rearrange("p (c s) -> p c s", c=nchunk)[:, :, tc * 512:(tc + 1) * 512]
        sl = scr_l.rearrange("p (c s) -> p c s", c=nchunk)[:, :, tc * 512:(tc + 1) * 512]
        p.dma("syn", out=self.xsl_h[:].rearrange("p (c s) -> p c s", c=ND), in_=sh, q="DSL")
        d = p.dma("syn", out=self.xsl_l[:].rearrange("p (c s) -> p c s", c=ND), in_=sl, q="DSL")
        return d

    def transpose_x32(self, dst_scr, gamma):
        p = self.p
        TPSF = self.T_PS[:].bitcast(F32)
        last = None
        for t in range(NT):
            xs = self.x_sb[:, t * D:(t + 1) * D]
            p.waits("scl", [self.sfree.get("u32"), getattr(self, "_ln_dep", None)])
            p.add("scl", lambda s, xs=xs, t=t: s.activation(
                self.u32[:], xs, Act.Identity, scale=self.rho[:, t:t + 1], bias=self.mb[:, t:t + 1]))
            d = p.mark("scl")
            if gamma:
                p.wait("vec", d)
                p.add("vec", lambda v: v.tensor_tensor(self.u32[:], self.u32[:], self.gam_rep[:], Alu.mult))
                p.add("vec", lambda v: v.tensor_tensor(self.u32[:], self.u32[:], self.bet_rep[:], Alu.add))
                d = p.mark("vec")
            p.wait("ten", d)
            p.waits("ten", [self.pfree["T"]])
            for c in range(ND):
                p.add("ten", lambda tn, c=c: tn.transpose(
                    TPSF[:, c * 128:(c + 1) * 128], self.u32[:, c * 128:(c + 1) * 128],
                    self.identf_sb[:]))
            dT = p.mark("ten")
            self.sfree["u32"] = dT
            stage = self.xsl_h[:].bitcast(F32)[:, 0:1024]
            p.wait("vec", dT)
            p.waits("vec", [self.sfree.get("xsl")])
            p.add("vec", lambda v, stage=stage: v.tensor_copy(stage, TPSF[:, 0:1024]))
            ved = p.mark("vec")
            self.pfree["T"] = ved
            p.waits("syn", [ved])
            dst = dst_scr.rearrange("p (c s) -> p c s", c=ND)[:, :, t * 128:(t + 1) * 128]
            sd = p.dma("syn", out=dst, in_=stage.rearrange("p (c k) -> p c k", c=ND), q="DSS")
            self.sfree["xsl"] = sd
            last = sd
        return last

    def load_xslab32(self, tc):
        p = self.p
        p.waits("syn", [self.sfree.get("xsl32")])
        src = self.xnT32_scr.rearrange("p (c s) -> p c s", c=ND)[:, :, tc * 512:(tc + 1) * 512]
        d = p.dma("syn", out=self.xsl32.rearrange("p (c s) -> p c s", c=ND), in_=src, q="DSL")
        return d

    def load_weight_rows(self, dram_rows, ncols):
        p = self.p
        p.waits("syn", [self.sfree.get("wld")])
        p.waits("gps", [self.sfree.get("wld")])
        d1 = p.dma("syn", out=self.wld[:, 0:ncols], in_=dram_rows, q="DWT")
        d2 = p.dma("gps", out=self.wld_h[:, 0:ncols], in_=dram_rows, q="DGH")
        p.waits("vec", [d1, d2])
        p.add("vec", lambda v: v.tensor_tensor(
            self.wld_l[:, 0:ncols], self.wld[:, 0:ncols], self.wld_h[:, 0:ncols], Alu.subtract))
        return p.mark("vec")

    def load_weight_cols(self, dram, rows, c0, ncols_src):
        p = self.p
        p.waits("syn", [self.sfree.get("wld")])
        p.waits("gps", [self.sfree.get("wld")])
        nrc = rows // 128
        d1 = d2 = None
        for r in range(nrc):
            src = dram[r * 128:(r + 1) * 128, c0:c0 + ncols_src]
            d1 = p.dma("syn", out=self.wld[:, r * ncols_src:(r + 1) * ncols_src], in_=src, q="DWT")
            d2 = p.dma("gps", out=self.wld_h[:, r * ncols_src:(r + 1) * ncols_src], in_=src, q="DGH")
        p.waits("vec", [d1, d2])
        n = nrc * ncols_src
        p.add("vec", lambda v, n=n: v.tensor_tensor(
            self.wld_l[:, 0:n], self.wld[:, 0:n], self.wld_h[:, 0:n], Alu.subtract))
        return p.mark("vec")

    def transpose_weight(self, dst_off, ncols, dep, extra=()):
        p = self.p
        p.wait("ten", dep)
        p.waits("ten", list(extra))
        p.waits("vec", list(extra))
        p.waits("ten", [self.pfree["T"]])
        nch = ncols // 128
        assert nch <= 8
        for c in range(nch):
            p.add("ten", lambda tn, c=c: tn.transpose(
                self.T_PS[:, c * 128:(c + 1) * 128], self.wld_h[:, c * 128:(c + 1) * 128], self.ident_sb[:]))
        for c in range(nch):
            p.add("ten", lambda tn, c=c: tn.transpose(
                self.T_PS[:, 1024 + c * 128:1024 + (c + 1) * 128],
                self.wld_l[:, c * 128:(c + 1) * 128], self.ident_sb[:]))
        dT = p.mark("ten")
        self.sfree["wld"] = dT
        p.wait("vec", dT)
        p.add("vec", lambda v: v.tensor_copy(self.wT_h[:, dst_off:dst_off + ncols], self.T_PS[:, 0:ncols]))
        p.add("vec", lambda v: v.tensor_copy(self.wT_l[:, dst_off:dst_off + ncols],
                                             self.T_PS[:, 1024:1024 + ncols]))
        d = p.mark("vec")
        self.pfree["T"] = d
        return d

    def mm3(self, bank_ap, lhsT_h, lhsT_l, rhs_h, rhs_l, first, last, passes=None):
        p = self.p
        if passes is None:
            passes = [(lhsT_h, rhs_h), (lhsT_l, rhs_h), (lhsT_h, rhs_l)]
        for i, (a_, b_) in enumerate(passes):
            st = first and i == 0
            sp = last and i == len(passes) - 1
            p.add("ten", lambda tn, a_=a_, b_=b_, st=st, sp=sp: tn.matmul(
                bank_ap, a_, b_, start=st, stop=sp))

    def proj_fm(self, woff, dst_h, dst_l, scale, bias_col, extra=None):
        p = self.p
        deps = []
        for tc in range(4):
            bank = tc % 4
            p.waits("ten", [self.pfree[f"A{bank}"]])
            ps = self.PS_A[:, bank * 512:(bank + 1) * 512]
            for c in range(ND):
                self.mm3(ps, self.wT_h[:, woff + c * 128:woff + (c + 1) * 128],
                         self.wT_l[:, woff + c * 128:woff + (c + 1) * 128],
                         self.xh_v[:, c, tc * 512:(tc + 1) * 512],
                         self.xl_v[:, c, tc * 512:(tc + 1) * 512],
                         first=(c == 0), last=(c == ND - 1))
            d = p.mark("ten")
            p.wait("scl", d)
            p.waits("scl", [self.sfree.get("g32")])
            p.add("scl", lambda s, ps=ps: s.activation(
                self.g32[:], ps, Act.Identity, scale=scale, bias=bias_col))
            d = p.mark("scl")
            self.pfree[f"A{bank}"] = d
            p.wait("vec", d)
            p.add("vec", lambda v, tc=tc: v.tensor_copy(dst_h[:, tc * 512:(tc + 1) * 512], self.g32[:]))
            p.add("vec", lambda v, tc=tc: v.tensor_tensor(
                dst_l[:, tc * 512:(tc + 1) * 512], self.g32[:], dst_h[:, tc * 512:(tc + 1) * 512],
                Alu.subtract))
            d = p.mark("vec")
            self.sfree["g32"] = d
            deps.append(d)
        return deps[-1]

    def l0_attn(self):
        p = self.p
        W = self.W
        self.replicate_row(W["L0_attn_gamma"], self.gam_rep, D)
        self.replicate_row(W["L0_attn_beta"], self.bet_rep, D)
        bd = None
        for i, nm in enumerate(("bq", "bk", "bv", "bo")):
            bd = p.dma("syn", out=self.bias_fm[:, i * ND:(i + 1) * ND],
                       in_=W[f"L0_attn_{nm}"].rearrange("(c p) -> p c", p=128), slow=True, q="DRR")
        p.wait("vec", bd)
        p.add("vec", lambda v: v.tensor_scalar_mul(self.bias_fm[:, 0:ND], self.bias_fm[:, 0:ND], 0.125))
        bqs_dep = p.mark("vec")
        bo_dep = self.split_row(W["L0_attn_bo"], D)

        self.stats_pass()
        self.ln_scalars("lnln")
        xn_dep = self.split_transpose_x(self.xnT_scr_h, self.xnT_scr_l, gamma=True)


        b4_dep = None

        def compute_b4(bb):
            mt = self.p32[0:32, :]
            p.waits("syn", [self.sfree.get("p32"), self.sfree.get("rb_st")])
            mt_dep = p.dma("syn", out=mt, in_=self.mt_c.ap(), q="DB4")
            rb_dep = p.dma("syn", out=self.rb_st[:].rearrange("p (a j) -> p a j", a=4),
                           in_=self.W["L0_attn_rel_bias"].rearrange("h i j -> i h j")[
                               :, bb * 4:(bb + 1) * 4, :], q="DB4")
            p.waits("ten", [self.pfree["C"], mt_dep, rb_dep])
            for a in range(4):
                p.add("ten", lambda tn, a=a: tn.transpose(
                    self.PS_C[0:32, a * 32:(a + 1) * 32], self.rb_st[:, a * 32:(a + 1) * 32],
                    self.identf_sb[0:32, 0:32]))
            d = p.mark("ten")
            self.sfree["rb_st"] = d
            p.wait("vec", d)
            p.waits("vec", [self.sfree.get("relbT")])
            p.add("vec", lambda v: v.tensor_copy(self.relbT[:, :], self.PS_C[0:32, 0:128]))
            d = p.mark("vec")
            self.pfree["C"] = d
            p.wait("ten", d)
            for tc in range(4):
                bank = tc
                p.waits("ten", [self.pfree[f"A{bank}"]])
                p.add("ten", lambda tn, tc=tc, bank=bank, mt=mt: tn.matmul(
                    self.PS_A[:, bank * 512:(bank + 1) * 512], self.relbT[:, :],
                    mt[:, tc * 512:(tc + 1) * 512], start=True, stop=True))
            dT = p.mark("ten")
            self.sfree["relbT"] = dT
            p.wait("vec", dT)
            p.add("vec", lambda v: v.tensor_copy(self.b4_h[:, :], self.PS_A[:, :]))
            p.add("vec", lambda v: v.tensor_tensor(self.b4_l[:, :], self.PS_A[:, :], self.b4_h[:, :],
                                                   Alu.subtract))
            d = p.mark("vec")
            for k in ("A0", "A1", "A2", "A3"):
                self.pfree[k] = d
            return d

        for g in range(8):
            if g % 2 == 0:
                b4_dep = compute_b4(g // 2)
            free = self.sfree.get("qkv_free")
            dd = self.load_weight_rows(W["L0_attn_Wq"][g * 128:(g + 1) * 128, :], 1024)
            wq_dep = self.transpose_weight(0, 1024, dd, extra=(free,))
            dd = self.load_weight_rows(W["L0_attn_Wk"][g * 128:(g + 1) * 128, :], 1024)
            wk_dep = self.transpose_weight(1024, 1024, dd, extra=(free,))
            dd = self.load_weight_rows(W["L0_attn_Wv"][g * 128:(g + 1) * 128, :], 1024)
            wv_dep = self.transpose_weight(2 * 1024, 1024, dd, extra=(free,))

            p.waits("vec", [self.sfree.get("qkv_free")])
            qmm_last = None
            for tc in range(4):
                slab_dep = self.load_xslab(self.xnT_scr_h, self.xnT_scr_l, tc)
                p.waits("ten", [slab_dep, wq_dep, wk_dep, wv_dep, bqs_dep, b4_dep])
                for wi in range(3):
                    bank = wi
                    p.waits("ten", [self.pfree[f"A{bank}"]])
                    ps = self.PS_A[:, bank * 512:(bank + 1) * 512]
                    for c in range(ND):
                        self.mm3(ps, self.wT_h[:, wi * 1024 + c * 128:wi * 1024 + (c + 1) * 128],
                                 self.wT_l[:, wi * 1024 + c * 128:wi * 1024 + (c + 1) * 128],
                                 self.xh_v[:, c, 0:512], self.xl_v[:, c, 0:512],
                                 first=(c == 0), last=(c == ND - 1))
                    d = p.mark("ten")
                    if wi == 0:
                        qmm_last = d
                    if wi == 2:
                        self.sfree["xsl"] = d
                    p.wait("scl", d)
                    p.waits("scl", [self.sfree.get("g32")])
                    scale = 0.125 if wi == 0 else 1.0
                    bcol = self.bias_fm[:, wi * ND + g:wi * ND + g + 1]
                    p.add("scl", lambda s, ps=ps, scale=scale, bcol=bcol: s.activation(
                        self.g32[:], ps, Act.Identity, scale=scale, bias=bcol))
                    d = p.mark("scl")
                    self.pfree[f"A{bank}"] = d
                    p.wait("vec", d)
                    if wi < 2:
                        dst_h, dst_l = (self.qt_h, self.qt_l) if wi == 0 else (self.kt_h, self.kt_l)
                        p.add("vec", lambda v, tc=tc, dst_h=dst_h: v.tensor_copy(
                            dst_h[:, tc * 512:(tc + 1) * 512], self.g32[:]))
                        p.add("vec", lambda v, tc=tc, dst_h=dst_h, dst_l=dst_l: v.tensor_tensor(
                            dst_l[:, tc * 512:(tc + 1) * 512], self.g32[:],
                            dst_h[:, tc * 512:(tc + 1) * 512], Alu.subtract))
                        d = p.mark("vec")
                        self.sfree["g32"] = d
                    else:
                        p.waits("vec", [self.sfree.get("st")])
                        p.add("vec", lambda v: v.tensor_copy(self.sth[:, 0:512], self.g32[:]))
                        p.add("vec", lambda v: v.tensor_tensor(self.stl[:, 0:512], self.g32[:],
                                                               self.sth[:, 0:512], Alu.subtract))
                        d = p.mark("vec")
                        self.sfree["g32"] = d
                        p.wait("ten", d)
                        p.waits("ten", [self.pfree["T"]])
                        for i in range(4):
                            p.add("ten", lambda tn, i=i: tn.transpose(
                                self.T_PS[:, i * 128:(i + 1) * 128],
                                self.sth[:, i * 128:(i + 1) * 128], self.ident_sb[:]))
                            p.add("ten", lambda tn, i=i: tn.transpose(
                                self.T_PS[:, 1024 + i * 128:1024 + (i + 1) * 128],
                                self.stl[:, i * 128:(i + 1) * 128], self.ident_sb[:]))
                        dT = p.mark("ten")
                        self.sfree["st"] = dT
                        p.wait("vec", dT)
                        p.add("vec", lambda v, tc=tc: v.tensor_copy(
                            self.vt_h[:, tc * 512:(tc + 1) * 512], self.T_PS[:, 0:512]))
                        p.add("vec", lambda v, tc=tc: v.tensor_copy(
                            self.vt_l[:, tc * 512:(tc + 1) * 512], self.T_PS[:, 1024:1536]))
                        d = p.mark("vec")
                        self.pfree["T"] = d
            qkv_dep = p.mark("vec")
            dd = self.load_weight_cols(W["L0_attn_Wo"], D, g * 128, 128)
            wo_dep = self.transpose_weight(0, 1024, dd, extra=(qmm_last,))

            for hh in range(2):
                h = 2 * g + hh
                p.waits("syn", [qkv_dep, self.sfree.get("aug")])
                a1 = p.dma("syn", out=self.qaug_h[0:64, :], in_=self.qt_h[64 * hh:64 * hh + 64, :], q="DAU")
                a2 = p.dma("syn", out=self.qaug_l[0:64, :], in_=self.qt_l[64 * hh:64 * hh + 64, :], q="DAU")
                a3 = p.dma("syn", out=self.kaug_h[0:64, :], in_=self.kt_h[64 * hh:64 * hh + 64, :], q="DAU")
                a4 = p.dma("syn", out=self.kaug_l[0:64, :], in_=self.kt_l[64 * hh:64 * hh + 64, :], q="DAU")
                hq = h % 4
                p.waits("syn", [b4_dep])
                a5 = p.dma("syn", out=self.kaug_h[64:96, :], in_=self.b4_h[32 * hq:32 * hq + 32, :], q="DAU")
                a6 = p.dma("syn", out=self.kaug_l[64:96, :], in_=self.b4_l[32 * hq:32 * hq + 32, :], q="DAU")
                p.waits("ten", [a1, a2, a3, a4, a5, a6, self.pfree["C"]])
                self.attn_head(hh, self.qaug_h, self.qaug_l, self.kaug_h, self.kaug_l)
            self.pfree["C"] = self._attn_sc
            sc = self._attn_sc
            self.sfree["og"] = sc
            p.waits("ten", [sc, wo_dep, bo_dep])
            for tq in range(NT):
                for oc in range(2):
                    bank = (2 * tq + oc) % 2
                    p.waits("ten", [self.pfree[f"A{bank}"]])
                    ps = self.PS_A[:, bank * 512:(bank + 1) * 512]
                    ogh_t = self.ogh[:, tq * 128:(tq + 1) * 128]
                    ogl_t = self.ogl[:, tq * 128:(tq + 1) * 128]
                    woh = self.wT_h[:, oc * 512:(oc + 1) * 512]
                    wol = self.wT_l[:, oc * 512:(oc + 1) * 512]
                    self.mm3(ps, ogh_t, ogl_t, woh, wol, first=True, last=(g != 0))
                    if g == 0:
                        p.add("ten", lambda tn, ps=ps, oc=oc: tn.matmul(
                            ps, self.onesb_sb[:, :], self.brow_h[:, oc * 512:(oc + 1) * 512],
                            start=False, stop=False))
                        p.add("ten", lambda tn, ps=ps, oc=oc: tn.matmul(
                            ps, self.onesb_sb[:, :], self.brow_l[:, oc * 512:(oc + 1) * 512],
                            start=False, stop=True))
                    d = p.mark("ten")
                    p.wait("vec", d)
                    xsl = self.x_sb[:, tq * D + oc * 512: tq * D + (oc + 1) * 512]
                    p.add("vec", lambda v, xsl=xsl, ps=ps: v.tensor_tensor(xsl, xsl, ps, Alu.add))
                    self.pfree[f"A{bank}"] = p.mark("vec")
            self.sfree["qkv_free"] = p.mark("ten")


    def attn_head(self, hh, lq_h, lq_l, kk_h, kk_l):
        p = self.p
        if True:
            if True:
                for tq in range(NT):
                    klen = (tq + 1) * 128
                    nkc = (klen + 511) // 512
                    p.waits("ten", [self.pfree["A0"], self.pfree["A1"], self.pfree["A2"],
                                    self.pfree["A3"]])
                    for kc in range(nkc):
                        w_ = min(512, klen - kc * 512)
                        ps = self.PS_A[:, kc * 512:kc * 512 + w_]
                        self.mm3(ps, lq_h[:, tq * 128:(tq + 1) * 128],
                                 lq_l[:, tq * 128:(tq + 1) * 128],
                                 kk_h[:, kc * 512:kc * 512 + w_],
                                 kk_l[:, kc * 512:kc * 512 + w_],
                                 first=True, last=True)
                    d = p.mark("ten")
                    p.wait("vec", d)
                    p.add("vec", lambda v, tq=tq: v.tensor_tensor(
                        self.PS_A[:, tq * 128:(tq + 1) * 128], self.PS_A[:, tq * 128:(tq + 1) * 128],
                        self.mask_sb[:], Alu.add))
                    p.add("vec", lambda v, klen=klen: v.reduce_max(
                        self.mneg[:, 0:1], self.PS_A[:, 0:klen], axis=AX.X, negate=True))
                    d = p.mark("vec")
                    p.wait("scl", d)
                    p.waits("scl", [self.sfree.get("p32")])
                    p.add("scl", lambda s, klen=klen: s.activation(
                        self.p32[:, 0:klen], self.PS_A[:, 0:klen], Act.Exp,
                        bias=self.mneg[:, 0:1], accum_out=self.lsum[:, 0:1]))
                    d = p.mark("scl")
                    for k in ("A0", "A1", "A2", "A3"):
                        self.pfree[k] = d
                    p.wait("vec", d)
                    p.add("vec", lambda v: v.reciprocal(self.linv[:, 0:1], self.lsum[:, 0:1]))
                    d = p.mark("vec")
                    p.wait("scl", d)
                    p.waits("scl", [self.sfree.get("ph")])
                    p.add("scl", lambda s, klen=klen: s.mul(self.ph[:, 0:klen], self.p32[:, 0:klen],
                                                            self.linv[:, 0:1]))
                    d = p.mark("scl")
                    p.wait("vec", d)
                    p.add("vec", lambda v, klen=klen: v.scalar_tensor_tensor(
                        self.pl[:, 0:klen], self.p32[:, 0:klen], self.linv[:, 0:1],
                        self.ph[:, 0:klen], Alu.mult, Alu.subtract))
                    d = p.mark("vec")
                    self.sfree["p32"] = d
                    p.wait("ten", d)
                    oq = self.PS_C[64 * hh:64 * hh + 64, 256 * (tq % 2):256 * (tq % 2) + 128]
                    ntile = tq + 1
                    first_pv = True
                    for w0 in range(0, ntile, 4):
                        nw = min(4, ntile - w0)
                        half = (w0 // 4) % 2
                        p.waits("ten", [self.pfree["T"]])
                        for i in range(nw):
                            p.add("ten", lambda tn, i=i, w0=w0, half=half: tn.transpose(
                                self.T_PS[:, half * 1024 + i * 128:half * 1024 + (i + 1) * 128],
                                self.ph[:, (w0 + i) * 128:(w0 + i + 1) * 128], self.ident_sb[:]))
                            p.add("ten", lambda tn, i=i, w0=w0, half=half: tn.transpose(
                                self.T_PS[:, half * 1024 + 512 + i * 128:half * 1024 + 512 + (i + 1) * 128],
                                self.pl[:, (w0 + i) * 128:(w0 + i + 1) * 128], self.ident_sb[:]))
                        d = p.mark("ten")
                        p.wait("vec", d)
                        p.waits("vec", [self.sfree.get("pt")])
                        pcol = (w0 % 8) * 128
                        p.add("vec", lambda v, pcol=pcol, nw=nw, half=half: v.tensor_copy(
                            self.pth[:, pcol:pcol + nw * 128],
                            self.T_PS[:, half * 1024:half * 1024 + nw * 128]))
                        p.add("vec", lambda v, pcol=pcol, nw=nw, half=half: v.tensor_copy(
                            self.ptl[:, pcol:pcol + nw * 128],
                            self.T_PS[:, half * 1024 + 512:half * 1024 + 512 + nw * 128]))
                        d = p.mark("vec")
                        self.pfree["T"] = d
                        p.wait("ten", d)
                        for i in range(nw):
                            c = w0 + i
                            cc = (c % 8) * 128
                            vh_ = self.vt_h[:, c * 128 + 64 * hh:c * 128 + 64 * hh + 64]
                            vl_ = self.vt_l[:, c * 128 + 64 * hh:c * 128 + 64 * hh + 64]
                            self.mm3(oq, vh_, vl_,
                                     self.pth[:, cc:cc + 128],
                                     self.ptl[:, cc:cc + 128],
                                     first=first_pv and i == 0,
                                     last=(w0 + nw == ntile) and (i == nw - 1))
                        first_pv = False
                    d = p.mark("ten")
                    self.sfree["ph"] = d
                    self.sfree["pt"] = d
                    p.wait("scl", d)
                    p.waits("scl", [self.sfree.get("og")])
                    p.add("scl", lambda s, hh=hh, tq=tq: s.copy(
                        self.ogh[64 * hh:64 * hh + 64, tq * 128:(tq + 1) * 128],
                        self.PS_C[64 * hh:64 * hh + 64, 256 * (tq % 2):256 * (tq % 2) + 128]))
                    d = p.mark("scl")
                    p.wait("vec", d)
                    p.add("vec", lambda v, hh=hh, tq=tq: v.tensor_tensor(
                        self.ogl[64 * hh:64 * hh + 64, tq * 128:(tq + 1) * 128],
                        self.PS_C[64 * hh:64 * hh + 64, 256 * (tq % 2):256 * (tq % 2) + 128],
                        self.ogh[64 * hh:64 * hh + 64, tq * 128:(tq + 1) * 128], Alu.subtract))
                    sc = p.mark("vec")
                    if tq >= 1:
                        p.wait("ten", sc)
                self.sfree["aug"] = p.mark("ten")
                self._attn_sc = sc

    def _ternarize_stream(self, dram, nrows, ncols, tcol_neg, tcol, scr, scr_map):
        p = self.p
        for r in range(nrows // 128):
            for q in range(ncols // 1024):
                sl = dram[r * 128:(r + 1) * 128, q * 1024:(q + 1) * 1024]
                p.waits("syn", [self.sfree.get("wld")])
                d1 = p.dma("syn", out=self.wld[:, 0:1024], in_=sl, q="DWT")
                p.waits("scl", [d1, self.sfree.get("u32")])
                p.add("scl", lambda s: s.activation(self.u32[:], self.wld[:, 0:1024], Act.Abs))
                d = p.mark("scl")
                p.wait("vec", d)
                p.waits("vec", [self.sfree.get("st")])
                p.add("vec", lambda v: v.tensor_scalar(
                    self.sth[:, 0:1024], self.u32[:], tcol, None, Alu.is_gt))
                p.add("vec", lambda v: v.tensor_tensor(
                    self.u32[:], self.wld[:, 0:1024], self.sth[:, 0:1024], Alu.mult))
                d = p.mark("vec")
                self.sfree["wld"] = d
                p.wait("scl", d)
                p.add("scl", lambda s: s.activation(self.wld_h[:, 0:1024], self.u32[:], Act.Sign))
                d = p.mark("scl")
                self.sfree["u32"] = d
                p.wait("ten", d)
                p.waits("ten", [self.pfree["T"]])
                for c in range(8):
                    p.add("ten", lambda tn, c=c: tn.transpose(
                        self.T_PS[:, c * 128:(c + 1) * 128], self.wld_h[:, c * 128:(c + 1) * 128],
                        self.ident_sb[:]))
                dT = p.mark("ten")
                p.wait("vec", dT)
                p.add("vec", lambda v: v.tensor_copy(self.wld_l[:, 0:1024], self.T_PS[:, 0:1024]))
                d = p.mark("vec")
                self.pfree["T"] = d
                p.waits("syn", [d])
                out_ap, in_ap = scr_map(r, q)
                sd = p.dma("syn", out=out_ap, in_=in_ap, q="DSS")
                self.sfree["wld"] = sd

    def l0_ffn(self):
        p = self.p
        W = self.W
        hb_h = self._av(0, 32768)
        hb_l = self._av(32768, 32768)
        hsq = self._av(65536, 2048, F32)
        srow = self._av(67584 + 4096, 2048, F32, parts=1)
        s2col = self._av(73728, 64, F32)
        acell = self._av(73856, 64, F32)
        tcol_up = self._av(74240, 4, F32)
        tcol_dn = self._av(74368, 4, F32)

        self.stats_pass()
        self.ln_scalars("ln_rms")

        def alpha_pass(dram, nchunks, tcol):
            acc = acell[:, 0:1]
            tmp = acell[:, 1:2]
            first = True
            dview = dram.rearrange("(r p) c -> p r c", p=128)
            ncq = dview.shape[2] // 1024
            for i in range(nchunks):
                r, q = i // ncq, i % ncq
                p.waits("syn", [self.sfree.get("wld")])
                d1 = p.dma("syn", out=self.wld[:, 0:1024],
                           in_=dview[:, r, q * 1024:(q + 1) * 1024],
                           q="DWT")
                p.waits("vec", [d1])
                p.add("vec", lambda v, first=first: v.tensor_reduce(
                    tmp if not first else acc, self.wld[:, 0:1024], axis=AX.X, op=Alu.add,
                    apply_absolute_value=True))
                if not first:
                    p.add("vec", lambda v: v.tensor_tensor(acc, acc, tmp, Alu.add))
                d = p.mark("vec")
                self.sfree["wld"] = d
                first = False
            p.waits("ten", [d, self.pfree["C"]])
            p.add("ten", lambda tn: tn.matmul(self.PS_C[0:1, 0:1], acc, self.onescol_sb[:, :],
                                              start=True, stop=True))
            d = p.mark("ten")
            p.wait("vec", d)
            p.add("vec", lambda v: v.tensor_scalar_mul(acell[0:1, 2:3], self.PS_C[0:1, 0:1],
                                                       0.5 / 4194304.0))
            d = p.mark("vec")
            self.pfree["C"] = d
            p.waits("ten", [d, self.pfree["C"]])
            p.add("ten", lambda tn: tn.matmul(self.PS_C[:, 0:1], self.ones_sb[:, :],
                                              acell[0:1, 2:3], start=True, stop=True))
            d = p.mark("ten")
            p.wait("vec", d)
            p.add("vec", lambda v, tcol=tcol: v.tensor_copy(tcol, self.PS_C[:, 0:1]))
            d = p.mark("vec")
            self.pfree["C"] = d
            return d

        dup = alpha_pass(W["L0_ffn_w_up"], 32, tcol_up)
        ddn = alpha_pass(W["L0_ffn_w_down"], 32, tcol_dn)

        wup_v = self.wupT_scr
        self._ternarize_stream(
            W["L0_ffn_w_up"], 4096, 1024, None, tcol_up, self.wupT_scr,
            lambda r, q: (self.wupT_scr[:, r * 1024:(r + 1) * 1024], self.wld_l[:, 0:1024]))
        dn_view = self.wdownT_scr.rearrange("p (fc rr k) -> p fc rr k", fc=32, rr=8)
        self._ternarize_stream(
            W["L0_ffn_w_down"], 1024, 4096, None, tcol_dn, self.wdownT_scr,
            lambda r, q: (dn_view[:, 8 * q:8 * q + 8, r, :],
                          self.wld_l[:, 0:1024].rearrange("p (j k) -> p j k", j=8)))

        xb_h = self.xsl_h[:].rearrange("p (c s) -> p c s", c=ND)
        xb_l = self.xsl_l[:].rearrange("p (c s) -> p c s", c=ND)
        self.sfree["wsl0"] = self.sfree["wsl1"] = self.sfree.get("wld")
        for b in range(4):
            xd = self.split_transpose_x(self.xsl_h[:], self.xsl_l[:], gamma=False,
                                        tiles=range(4 * b, 4 * b + 4), span=512, to_sbuf=True)
            p.waits("ten", [xd])
            stats_started = False
            for fc in range(32):
                wbuf = self.wld_h if fc % 2 == 0 else self.wld_l
                p.waits("syn", [self.sfree.get(f"wsl{fc % 2}")])
                wd = p.dma("syn", out=wbuf[:, 0:1024],
                           in_=self.wupT_scr[:, fc * 1024:(fc + 1) * 1024],
                           q="DSL" if fc % 2 == 0 else "DS2")
                bank = fc % 4
                p.waits("ten", [wd, self.pfree[f"A{bank}"]])
                ps = self.PS_A[:, bank * 512:(bank + 1) * 512]
                for c in range(ND):
                    self.mm3(ps, None, None, None, None, first=(c == 0), last=(c == ND - 1),
                             passes=[(wbuf[:, c * 128:(c + 1) * 128], xb_h[:, c, :]),
                                     (wbuf[:, c * 128:(c + 1) * 128], xb_l[:, c, :])])
                d = p.mark("ten")
                self.sfree[f"wsl{fc % 2}"] = d
                p.wait("scl", d)
                p.waits("scl", [self.sfree.get("g32")])
                p.add("scl", lambda s, ps=ps: s.activation(self.g32[:], ps, Act.Gelu))
                d = p.mark("scl")
                self.pfree[f"A{bank}"] = d
                p.wait("vec", d)
                p.add("vec", lambda v, fc=fc: v.tensor_copy(hb_h[:, fc * 512:(fc + 1) * 512], self.g32[:]))
                p.add("vec", lambda v, fc=fc: v.tensor_tensor(
                    hb_l[:, fc * 512:(fc + 1) * 512], self.g32[:], hb_h[:, fc * 512:(fc + 1) * 512],
                    Alu.subtract))
                p.add("vec", lambda v: v.tensor_tensor(hsq[:, :], self.g32[:], self.g32[:], Alu.mult))
                d = p.mark("vec")
                self.sfree["g32"] = d
                p.waits("ten", [d] + ([self.pfree["D"]] if not stats_started else []))
                p.add("ten", lambda tn, fc=fc: tn.matmul(
                    self.PS_D[0:1, 0:512], self.onescol_sb[:, :],
                    hsq[:, :], start=(fc == 0), stop=(fc == 31)))
                stats_started = True
                p.waits("vec", [p.mark("ten")])
            d = p.mark("ten")
            p.wait("vec", d)
            p.add("vec", lambda v: v.tensor_copy(srow[:, :], self.PS_D[0:1, 0:512]))
            d = p.mark("vec")
            self.pfree["D"] = d
            p.waits("ten", [d, self.pfree["C"]])
            for tt in range(4):
                p.add("ten", lambda tn, tt=tt: tn.transpose(
                    self.PS_C[:, tt:tt + 1], srow[0:1, tt * 128:(tt + 1) * 128],
                    self.identf_sb[0:1, 0:1]))
            d = p.mark("ten")
            p.wait("vec", d)
            p.add("vec", lambda v: v.tensor_copy(s2col[:, 0:4], self.PS_C[:, 0:4]))
            d = p.mark("vec")
            self.pfree["C"] = d
            p.wait("scl", d)
            p.add("scl", lambda s: s.activation(s2col[:, 0:4], s2col[:, 0:4], Act.Sqrt))
            d = p.mark("scl")
            p.wait("vec", d)
            p.add("vec", lambda v: v.tensor_scalar(s2col[:, 0:4], s2col[:, 0:4], 1.0 / 64.0, 1e-8,
                                                   Alu.mult, Alu.add))
            p.add("vec", lambda v: v.reciprocal(s2col[:, 0:4], s2col[:, 0:4]))
            s2dep = p.mark("vec")

            banks = [self.PS_A[:, i * 512:(i + 1) * 512] for i in range(4)] +                     [self.T_PS[:].bitcast(F32)[:, 0:512], self.T_PS[:].bitcast(F32)[:, 512:1024],
                     self.PS_C[:, 0:512], self.PS_D[:, 0:512]]
            p.waits("ten", [self.pfree["A0"], self.pfree["A1"], self.pfree["A2"], self.pfree["A3"],
                            self.pfree["T"], self.pfree["C"], self.pfree["D"]])
            for fc in range(32):
                wbuf = self.wld_h if fc % 2 == 0 else self.wld_l
                p.waits("syn", [self.sfree.get(f"wsl{fc % 2}")])
                wd = p.dma("syn", out=wbuf[:, 0:1024],
                           in_=self.wdownT_scr[:, fc * 1024:(fc + 1) * 1024],
                           q="DSL" if fc % 2 == 0 else "DS2")
                p.waits("ten", [wd])
                for tt in range(4):
                    hh = hb_h[:, fc * 512 + tt * 128:fc * 512 + (tt + 1) * 128]
                    hl = hb_l[:, fc * 512 + tt * 128:fc * 512 + (tt + 1) * 128]
                    for oc in range(2):
                        self.mm3(banks[tt * 2 + oc], None, None, None, None,
                                 first=(fc == 0), last=(fc == 31),
                                 passes=[(hh, wbuf[:, oc * 512:(oc + 1) * 512]),
                                         (hl, wbuf[:, oc * 512:(oc + 1) * 512])])
                d = p.mark("ten")
                self.sfree[f"wsl{fc % 2}"] = d
            d = p.mark("ten")
            p.wait("vec", d)
            p.waits("vec", [s2dep])
            for tt in range(4):
                t_g = 4 * b + tt
                for oc in range(2):
                    xsl_ = self.x_sb[:, t_g * D + oc * 512:t_g * D + (oc + 1) * 512]
                    p.add("vec", lambda v, xsl_=xsl_, tt=tt, oc=oc: v.scalar_tensor_tensor(
                        xsl_, banks[tt * 2 + oc], s2col[:, tt:tt + 1], xsl_, Alu.mult, Alu.add))
            d = p.mark("vec")
            for k in ("A0", "A1", "A2", "A3", "T", "C", "D"):
                self.pfree[k] = d


    def l1_attn(self):
        p = self.p
        W = self.W
        I = 2048
        NC = 16
        BT = 512
        h_in_h = self._av(0, 16512)
        h_in_l = self._av(16512, 16512)
        u_cm = self._av(0, 32768, F32)
        h2_h = self._av(0, 16384)
        h2_l = self._av(16384, 16384)
        h32 = self._av(33024, 32768, F32)
        gate32 = self._av(65792, 2048, F32)
        m1_h = self._av(67840, 8192)
        m1_l = self._av(76032, 8192)
        murep = self._av(67840, 2048, F32)
        rrep = self._av(69888, 2048, F32)
        srow2 = self._av(65792, 2048, F32, parts=1)
        srow2b = self._av(71936, 2048, F32, parts=1)
        halo_h = self._av(84224, 96)
        halo_l = self._av(84320, 96)

        hiv = lambda buf: buf.rearrange("p (c s) -> p c s", c=NC)
        h32v = h32.rearrange("p (c s) -> p c s", c=NC)
        xb_h = self.xsl_h[:].rearrange("p (c s) -> p c s", c=ND)
        xb_l = self.xsl_l[:].rearrange("p (c s) -> p c s", c=ND)

        for off, nm in ((0, "ln_g"), (16, "ln_b"), (32, "conv_b"), (48, "cproj_b")):
            bd = p.dma("syn", out=self.bias2[:, off:off + 16],
                       in_=W[f"L1_attn_{nm}"].rearrange("(c p) -> p c", p=128), slow=True, q="DRR")
        bd = p.dma("syn", out=self.bias2[:, 64:72],
                   in_=W["L1_attn_fc_b"].rearrange("(c p) -> p c", p=128), slow=True, q="DRR")
        for eng in ("vec", "scl"):
            p.wait(eng, bd)

        self.dbg_dump("xl1a", self.x_sb[:])
        vd_, sd_ = self.stats_pass()
        self.dbg_dump("sq1a", self.sq[:], sd_)
        self.dbg_dump("sm1a", self.sm[:], vd_)
        d_ = self.ln_scalars("ln")
        self.dbg_dump("rho1a", self.rho[:], d_)

        p.add("vec", lambda v: v.memset(hiv(h_in_h)[:, :, 0:4], 0.0))
        p.add("vec", lambda v: v.memset(hiv(h_in_l)[:, :, 0:4], 0.0))

        for b in range(4):
            xd = self.split_transpose_x(self.xsl_h[:], self.xsl_l[:], gamma=False,
                                        tiles=range(4 * b, 4 * b + 4), span=BT, to_sbuf=True)
            if b == 3:
                self.dbg_dump("xsl_h", self.xsl_h[:], xd)
                self.dbg_dump("xsl_l", self.xsl_l[:], xd)
            if b > 0:
                p.add("vec", lambda v: v.tensor_copy(hiv(h_in_h)[:, :, 1:4], halo_h[:].rearrange("p (c k) -> p c k", c=NC)))
                p.add("vec", lambda v: v.tensor_copy(hiv(h_in_l)[:, :, 1:4], halo_l[:].rearrange("p (c k) -> p c k", c=NC)))
            p.waits("ten", [xd])

            for c in range(NC):
                slot = c % 2
                free = self.sfree.get("wT")
                dd = self.load_weight_rows(W["L1_attn_in_proj_w"][c * 128:(c + 1) * 128, :], 1024)
                wdep = self.transpose_weight(slot * 1024, 1024, dd, extra=(free,))
                bank = c % 4
                p.waits("ten", [wdep, self.pfree[f"A{bank}"]])
                ps = self.PS_A[:, bank * 512:(bank + 1) * 512]
                for dc in range(ND):
                    self.mm3(ps, self.wT_h[:, slot * 1024 + dc * 128:slot * 1024 + (dc + 1) * 128],
                             self.wT_l[:, slot * 1024 + dc * 128:slot * 1024 + (dc + 1) * 128],
                             xb_h[:, dc, :], xb_l[:, dc, :], first=(dc == 0), last=(dc == ND - 1))
                d = p.mark("ten")
                self.sfree["wT"] = d
                p.wait("scl", d)
                p.waits("scl", [self.sfree.get("g32")])
                p.add("scl", lambda s, ps=ps: s.activation(self.g32[:], ps, Act.Identity))
                d = p.mark("scl")
                self.pfree[f"A{bank}"] = d
                p.wait("vec", d)
                p.waits("vec", [self.sfree.get("hreg")])
                p.add("vec", lambda v, c=c: v.tensor_copy(hiv(h_in_h)[:, c, 4:516], self.g32[:]))
                p.add("vec", lambda v, c=c: v.tensor_tensor(
                    hiv(h_in_l)[:, c, 4:516], self.g32[:], hiv(h_in_h)[:, c, 4:516], Alu.subtract))
                d = p.mark("vec")
                self.sfree["g32"] = d
            hin_dep = p.mark("vec")
            if b == 3:
                self.dbg_dump("hin_h", h_in_h, hin_dep)
                self.dbg_dump("hin_l", h_in_l, hin_dep)

            for g in range(8):
                free = self.sfree.get("wT")
                for rr in range(2):
                    dd = self.load_weight_rows(W["L1_attn_conv_w"].rearrange("o c k -> o (c k)")[
                        g * 256 + rr * 128:g * 256 + rr * 128 + 128, :], 1024)
                    p.wait("ten", dd)
                    p.waits("ten", [free, self.pfree["T"]])
                    p.waits("vec", [free])
                    for ci in range(2):
                        for k in range(4):
                            src_h = self.wld_h[:, 0:1024].rearrange("p (ci2 k2) -> p k2 ci2", k2=4)[:, k, ci * 128:(ci + 1) * 128]
                            src_l = self.wld_l[:, 0:1024].rearrange("p (ci2 k2) -> p k2 ci2", k2=4)[:, k, ci * 128:(ci + 1) * 128]
                            ti = (ci * 4 + k)
                            p.add("ten", lambda tn, src_h=src_h, ti=ti, rr=rr: tn.transpose(
                                self.T_PS[:, ti * 128:(ti + 1) * 128], src_h, self.ident_sb[:]))
                            p.add("ten", lambda tn, src_l=src_l, ti=ti, rr=rr: tn.transpose(
                                self.T_PS[:, 1024 + ti * 128:1024 + (ti + 1) * 128], src_l, self.ident_sb[:]))
                    dT = p.mark("ten")
                    self.sfree["wld"] = dT
                    p.wait("vec", dT)
                    p.add("vec", lambda v, rr=rr: v.tensor_copy(
                        self.wT_h[:, rr * 1024:(rr + 1) * 1024], self.T_PS[:, 0:1024]))
                    p.add("vec", lambda v, rr=rr: v.tensor_copy(
                        self.wT_l[:, rr * 1024:(rr + 1) * 1024], self.T_PS[:, 1024:2048]))
                    d = p.mark("vec")
                    self.pfree["T"] = d
                conv_w_dep = p.mark("vec")
                for co in range(2):
                    cog = 2 * g + co
                    free = self.sfree.get("wT2")
                    dd = self.load_weight_rows(
                        W["L1_attn_in_proj_w"][I + cog * 128:I + (cog + 1) * 128, :], 1024)
                    gdep = self.transpose_weight(2 * 1024, 1024, dd, extra=(free,))
                    p.waits("ten", [gdep, self.pfree["C"]])
                    for dc in range(ND):
                        self.mm3(self.PS_C[:, 0:512],
                                 self.wT_h[:, 2 * 1024 + dc * 128:2 * 1024 + (dc + 1) * 128],
                                 self.wT_l[:, 2 * 1024 + dc * 128:2 * 1024 + (dc + 1) * 128],
                                 xb_h[:, dc, :], xb_l[:, dc, :], first=(dc == 0), last=(dc == ND - 1))
                    d = p.mark("ten")
                    self.sfree["wT2"] = d
                    p.wait("scl", d)
                    p.waits("scl", [self.sfree.get("gate32")])
                    p.add("scl", lambda s: s.activation(gate32[:, :], self.PS_C[:, 0:512], Act.Silu))
                    sil_dep = p.mark("scl")
                    self.pfree["C"] = sil_dep
                    bank = co % 2
                    p.waits("ten", [conv_w_dep, hin_dep, self.pfree[f"A{bank}"]])
                    ps = self.PS_A[:, bank * 512:(bank + 1) * 512]
                    first = True
                    for ci in range(2):
                        cig = 2 * g + ci
                        for k in range(4):
                            ti = (co * 8 + ci * 4 + k) * 128
                            rh = hiv(h_in_h)[:, cig, 1 + k:1 + k + 512]
                            rl = hiv(h_in_l)[:, cig, 1 + k:1 + k + 512]
                            self.mm3(ps, self.wT_h[:, ti:ti + 128], self.wT_l[:, ti:ti + 128],
                                     rh, rl, first=first, last=(ci == 1 and k == 3))
                            first = False
                    d = p.mark("ten")
                    p.waits("vec", [d, sil_dep])
                    p.add("vec", lambda v, cog=cog, ps=ps: v.scalar_tensor_tensor(
                        h32v[:, cog, :], ps, self.bias2[:, 32 + cog:33 + cog],
                        gate32[:, :], Alu.add, Alu.mult))
                    d = p.mark("vec")
                    self.pfree[f"A{bank}"] = d
                    self.sfree["gate32"] = d
            conv_done = p.mark("vec")
            if b == 3:
                self.dbg_dump("h32", h32, conv_done)
            self.sfree["xsl"] = p.mark("ten")

            p.add("vec", lambda v: v.tensor_copy(
                halo_h[:].rearrange("p (c k) -> p c k", c=NC), hiv(h_in_h)[:, :, 513:516]))
            p.add("vec", lambda v: v.tensor_copy(
                halo_l[:].rearrange("p (c k) -> p c k", c=NC), hiv(h_in_l)[:, :, 513:516]))
            halo_dep = p.mark("vec")

            p.waits("ten", [conv_done, self.pfree["C"], self.pfree["D"]])
            for c in range(NC):
                p.add("ten", lambda tn, c=c: tn.matmul(
                    self.PS_D[0:1, 0:512], self.onescol_sb[:, :], h32v[:, c, :],
                    start=(c == 0), stop=(c == NC - 1)))
            for c in range(NC):
                p.waits("vec", [self.sfree.get("gate32")])
                p.add("vec", lambda v, c=c: v.tensor_tensor(gate32[:, :], h32v[:, c, :], h32v[:, c, :],
                                                            Alu.mult))
                d = p.mark("vec")
                p.waits("ten", [d])
                p.add("ten", lambda tn, c=c: tn.matmul(
                    self.PS_C[0:1, 0:512], self.onescol_sb[:, :], gate32[:, :],
                    start=(c == 0), stop=(c == NC - 1)))
                self.sfree["gate32"] = p.mark("ten")
            d = p.mark("ten")
            p.wait("vec", d)
            p.add("vec", lambda v: v.tensor_copy(srow2[:, :], self.PS_D[0:1, 0:512]))
            p.add("vec", lambda v: v.tensor_copy(srow2b[:, :], self.PS_C[0:1, 0:512]))
            d = p.mark("vec")
            self.pfree["C"] = d
            self.pfree["D"] = d
            p.waits("ten", [d, self.pfree["A0"], self.pfree["A1"]])
            p.add("ten", lambda tn: tn.matmul(self.PS_A[:, 0:512], self.ones_sb[:, :], srow2[:, :],
                                              start=True, stop=True))
            p.add("ten", lambda tn: tn.matmul(self.PS_A[:, 512:1024], self.ones_sb[:, :], srow2b[:, :],
                                              start=True, stop=True))
            d = p.mark("ten")
            p.wait("vec", d)
            p.add("vec", lambda v: v.tensor_scalar_mul(murep[:, :], self.PS_A[:, 0:512], 1.0 / I))
            p.add("vec", lambda v: v.tensor_scalar_mul(rrep[:, :], self.PS_A[:, 512:1024], 1.0 / I))
            d = p.mark("vec")
            self.pfree["A0"] = d
            self.pfree["A1"] = d
            p.add("vec", lambda v: v.tensor_tensor(gate32[:, :], murep[:, :], murep[:, :], Alu.mult))
            p.add("vec", lambda v: v.tensor_tensor(rrep[:, :], rrep[:, :], gate32[:, :], Alu.subtract))
            p.add("vec", lambda v: v.tensor_scalar_add(rrep[:, :], rrep[:, :], 1e-5))
            d = p.mark("vec")
            p.wait("scl", d)
            p.add("scl", lambda s: s.activation(rrep[:, :], rrep[:, :], Act.Sqrt))
            d = p.mark("scl")
            p.wait("vec", d)
            p.add("vec", lambda v: v.reciprocal(rrep[:, :], rrep[:, :]))
            stat_dep = p.mark("vec")

            u_cmv = u_cm.rearrange("p (c s) -> p c s", c=NC)
            p.waits("vec", [halo_dep, stat_dep])
            for c in range(NC):
                p.add("vec", lambda v, c=c: v.tensor_tensor(u_cmv[:, c, :], h32v[:, c, :],
                                                            murep[:, :], Alu.subtract))
                p.add("vec", lambda v, c=c: v.tensor_tensor(u_cmv[:, c, :], u_cmv[:, c, :],
                                                            rrep[:, :], Alu.mult))
            ucm_dep = p.mark("vec")

            for half in range(2):
                started = [False] * 4
                for i in range(NC):
                    p.waits("scl", [ucm_dep, self.sfree.get("g32")])
                    p.add("scl", lambda s, i=i: s.activation(
                        self.g32[:], u_cmv[:, i, :], Act.Identity,
                        scale=self.bias2[:, i:i + 1],
                        bias=self.bias2[:, 16 + i:17 + i]))
                    d = p.mark("scl")
                    p.wait("vec", d)
                    p.waits("vec", [self.sfree.get("st")])
                    p.add("vec", lambda v: v.tensor_copy(self.sth[:, 0:512], self.g32[:]))
                    p.add("vec", lambda v: v.tensor_tensor(self.stl[:, 0:512], self.g32[:],
                                                           self.sth[:, 0:512], Alu.subtract))
                    hn_dep = p.mark("vec")
                    self.sfree["g32"] = hn_dep
                    free = self.sfree.get("wT")
                    dd = self.load_weight_cols(W["L1_attn_fc_w"][half * 512:(half + 1) * 512, :],
                                               512, i * 128, 128)
                    wdep = self.transpose_weight(0, 512, dd, extra=(free,))
                    p.waits("ten", [wdep, hn_dep])
                    if i == 0:
                        p.waits("ten", [self.pfree["A0"], self.pfree["A1"], self.pfree["A2"],
                                        self.pfree["A3"]])
                    for oc in range(4):
                        ps = self.PS_A[:, oc * 512:(oc + 1) * 512]
                        self.mm3(ps, None, None, None, None,
                                 first=(i == 0), last=(i == NC - 1),
                                 passes=[(self.wT_h[:, oc * 128:(oc + 1) * 128], self.sth[:, 0:512]),
                                         (self.wT_l[:, oc * 128:(oc + 1) * 128], self.sth[:, 0:512]),
                                         (self.wT_h[:, oc * 128:(oc + 1) * 128], self.stl[:, 0:512])])
                    d = p.mark("ten")
                    self.sfree["wT"] = d
                    self.sfree["st"] = d
                d = p.mark("ten")
                p.wait("scl", d)
                for oc in range(4):
                    og = half * 4 + oc
                    p.waits("scl", [self.sfree.get("g32")])
                    p.add("scl", lambda s, oc=oc, og=og: s.activation(
                        self.g32[:], self.PS_A[:, oc * 512:(oc + 1) * 512], Act.Silu,
                        bias=self.bias2[:, 64 + og:65 + og]))
                    d2 = p.mark("scl")
                    p.wait("vec", d2)
                    p.waits("vec", [self.sfree.get("m1")])
                    p.add("vec", lambda v, og=og: v.tensor_copy(
                        m1_h[:, og * 512:(og + 1) * 512], self.g32[:]))
                    p.add("vec", lambda v, og=og: v.tensor_tensor(
                        m1_l[:, og * 512:(og + 1) * 512], self.g32[:], m1_h[:, og * 512:(og + 1) * 512],
                        Alu.subtract))
                    d2 = p.mark("vec")
                    self.sfree["g32"] = d2
                    self.pfree[f"A{oc}"] = d2
            m1_dep = p.mark("vec")

            for c in range(NC):
                slot = c % 2
                free = self.sfree.get("wT")
                dd = self.load_weight_rows(W["L1_attn_cproj_w"][c * 128:(c + 1) * 128, :], 1024)
                wdep = self.transpose_weight(slot * 1024, 1024, dd, extra=(free,))
                bank = c % 4
                p.waits("ten", [wdep, m1_dep, self.pfree[f"A{bank}"]])
                ps = self.PS_A[:, bank * 512:(bank + 1) * 512]
                for oc in range(ND):
                    self.mm3(ps, self.wT_h[:, slot * 1024 + oc * 128:slot * 1024 + (oc + 1) * 128],
                             self.wT_l[:, slot * 1024 + oc * 128:slot * 1024 + (oc + 1) * 128],
                             m1_h[:, oc * 512:(oc + 1) * 512], m1_l[:, oc * 512:(oc + 1) * 512],
                             first=(oc == 0), last=(oc == ND - 1))
                d = p.mark("ten")
                self.sfree["wT"] = d
                p.wait("vec", d)
                p.waits("vec", [self.sfree.get("g32")])
                p.add("vec", lambda v, c=c, ps=ps: v.scalar_tensor_tensor(
                    self.g32[:], ps, self.bias2[:, 48 + c:49 + c], h32v[:, c, :],
                    Alu.add, Alu.add))
                h2v_h = h2_h.rearrange("p (c s) -> p c s", c=NC)
                h2v_l = h2_l.rearrange("p (c s) -> p c s", c=NC)
                p.add("vec", lambda v, c=c, h2v_h=h2v_h: v.tensor_copy(h2v_h[:, c, :], self.g32[:]))
                p.add("vec", lambda v, c=c, h2v_h=h2v_h, h2v_l=h2v_l: v.tensor_tensor(
                    h2v_l[:, c, :], self.g32[:], h2v_h[:, c, :], Alu.subtract))
                d = p.mark("vec")
                self.sfree["g32"] = d
                self.pfree[f"A{bank}"] = d
            h2_dep = p.mark("vec")
            h2v_h = h2_h.rearrange("p (c s) -> p c s", c=NC)
            h2v_l = h2_l.rearrange("p (c s) -> p c s", c=NC)

            for half in range(2):
                started = False
                for c in range(NC):
                    free = self.sfree.get("wT")
                    dd = self.load_weight_cols(W["L1_attn_out_w"][half * 512:(half + 1) * 512, :],
                                               512, c * 128, 128)
                    wdep = self.transpose_weight(0, 512, dd, extra=(free,))
                    p.waits("ten", [wdep, h2_dep])
                    if not started:
                        p.waits("ten", [self.pfree["A0"], self.pfree["A1"], self.pfree["A2"],
                                        self.pfree["A3"]])
                        started = True
                    for tt in range(4):
                        ps = self.PS_A[:, tt * 512:(tt + 1) * 512]
                        self.mm3(ps, h2v_h[:, c, tt * 128:(tt + 1) * 128],
                                 h2v_l[:, c, tt * 128:(tt + 1) * 128],
                                 self.wT_h[:, 0:512], self.wT_l[:, 0:512],
                                 first=(c == 0), last=(c == NC - 1))
                    d = p.mark("ten")
                    self.sfree["wT"] = d
                d = p.mark("ten")
                p.wait("vec", d)
                for tt in range(4):
                    t_g = 4 * b + tt
                    xsl_ = self.x_sb[:, t_g * D + half * 512:t_g * D + (half + 1) * 512]
                    p.add("vec", lambda v, xsl_=xsl_, tt=tt: v.tensor_tensor(
                        xsl_, xsl_, self.PS_A[:, tt * 512:(tt + 1) * 512], Alu.add))
                d = p.mark("vec")
                for k in ("A0", "A1", "A2", "A3"):
                    self.pfree[k] = d
            self.sfree["hreg"] = p.mark("ten")
            self.sfree["m1"] = self.sfree["hreg"]


    def l1_ffn(self):
        p = self.p
        W = self.W
        cd = None
        for t_, c_ in ((self.irev_sb, self.irev_c), (self.ecol_sb, self.ecol_c),
                       (self.g8_sb, self.g8_c), (self.g8t_sb, self.g8t_c)):
            cd = p.dma("syn", out=t_[:], in_=c_.ap(), q="DRR")

        spill = None
        for t in range(NT):
            spill = p.dma("syn", out=self.x_scr[:, t * D:(t + 1) * D],
                          in_=self.x_sb[:, t * D:(t + 1) * D], q="DSS")
        self.replicate_row(W["L1_ffn_g1"], self.gam_rep, D)
        p.add("vec", lambda v: v.memset(self.bet_rep[:], 0.0))
        self.stats_pass()
        self.ln_scalars("lnln", eps=1e-5, eps2=1e-6)
        xn_dep = self.split_transpose_x(self.xnT_scr_h, self.xnT_scr_l, gamma=True)
        p.waits("scl", [spill, xn_dep])
        zt = None
        for t in range(NT):
            xs = self.x_sb[:, t * D:(t + 1) * D]
            p.add("scl", lambda s, xs=xs, t=t: s.activation(
                xs, xs, Act.Identity, scale=self.r1s[:, t:t + 1], bias=self.mb1s[:, t:t + 1]))
        zt = p.mark("scl")
        p.waits("vec", [zt])
        p.waits("ten", [zt])

        qkv = W["L1_ffn_w_qkv"]
        for g in range(8):
            for wi, coff in ((0, 0), (1, D), (2, 2 * D)):
                p.waits("syn", [self.sfree.get("wld")])
                p.waits("gps", [self.sfree.get("wld")])
                d1 = d2 = None
                for r in range(ND):
                    slc = qkv[r * 128:(r + 1) * 128, coff + g * 128:coff + (g + 1) * 128]
                    d1 = p.dma("syn", out=self.wld[:, r * 128:(r + 1) * 128], in_=slc, q="DWT")
                    d2 = p.dma("gps", out=self.wld_h[:, r * 128:(r + 1) * 128], in_=slc, q="DGH")
                p.waits("vec", [d1, d2])
                p.add("vec", lambda v: v.tensor_tensor(
                    self.wld_l[:, 0:1024], self.wld[:, 0:1024], self.wld_h[:, 0:1024], Alu.subtract))
                wdep = p.mark("vec")
                for tc in range(4):
                    slab_dep = self.load_xslab(self.xnT_scr_h, self.xnT_scr_l, tc)
                    p.waits("ten", [slab_dep, wdep, xn_dep])
                    bank = tc % 4
                    p.waits("ten", [self.pfree[f"A{bank}"]])
                    ps = self.PS_A[:, bank * 512:(bank + 1) * 512]
                    for c in range(ND):
                        self.mm3(ps, self.wld_h[:, c * 128:(c + 1) * 128],
                                 self.wld_l[:, c * 128:(c + 1) * 128],
                                 self.xh_v[:, c, 0:512], self.xl_v[:, c, 0:512],
                                 first=(c == 0), last=(c == ND - 1))
                    d = p.mark("ten")
                    if wi == 2 and tc == 3:
                        self.sfree["wld"] = d
                    self.sfree["xsl"] = d
                    p.wait("scl", d)
                    p.waits("scl", [self.sfree.get("g32")])
                    scale = 0.125 if wi == 0 else 1.0
                    p.add("scl", lambda s, ps=ps, scale=scale: s.activation(
                        self.g32[:], ps, Act.Identity, scale=scale))
                    d = p.mark("scl")
                    self.pfree[f"A{bank}"] = d
                    if wi < 2:
                        qs_buf = self._av(77824, 2048, F32)
                        qr_buf = self._av(79872, 2048, F32)
                        p.waits("syn", [self.sfree.get("u32"), ("DB4", p.ctr["DB4"])])
                        c1d = p.dma("syn", out=self.u32[:, 0:512],
                                    in_=self.rope1_c.ap()[:, tc * 512:(tc + 1) * 512], q="DB4")
                        c2d = p.dma("syn", out=self.u32[:, 512:1024],
                                    in_=self.rope2_c.ap()[:, tc * 512:(tc + 1) * 512], q="DB4")
                        p.waits("vec", [d, c1d, c2d])
                        mask = []
                        for i_ in range(32):
                            mask.append(i_ + 1 if i_ % 2 == 0 else i_ - 1)
                        p.add("vec", lambda v, qs_buf=qs_buf, mask=mask: v.stream_shuffle(
                            qs_buf[:, :], self.g32[:], mask))
                        p.add("vec", lambda v, qr_buf=qr_buf: v.tensor_tensor(
                            qr_buf[:, :], self.g32[:], self.u32[:, 0:512], Alu.mult))
                        p.add("vec", lambda v, qs_buf=qs_buf: v.tensor_tensor(
                            qs_buf[:, :], qs_buf[:, :], self.u32[:, 512:1024], Alu.mult))
                        p.add("vec", lambda v, qr_buf=qr_buf, qs_buf=qs_buf: v.tensor_tensor(
                            qr_buf[:, :], qr_buf[:, :], qs_buf[:, :], Alu.add))
                        ud = p.mark("vec")
                        self.sfree["u32"] = ud
                        dst_h, dst_l = (self.qt_h, self.qt_l) if wi == 0 else (self.kt_h, self.kt_l)
                        p.add("vec", lambda v, tc=tc, dst_h=dst_h, qr_buf=qr_buf: v.tensor_copy(
                            dst_h[:, tc * 512:(tc + 1) * 512], qr_buf[:, :]))
                        p.add("vec", lambda v, tc=tc, dst_h=dst_h, dst_l=dst_l, qr_buf=qr_buf:
                              v.tensor_tensor(dst_l[:, tc * 512:(tc + 1) * 512], qr_buf[:, :],
                                              dst_h[:, tc * 512:(tc + 1) * 512], Alu.subtract))
                        d = p.mark("vec")
                        self.sfree["g32"] = d
                    else:
                        p.wait("vec", d)
                        p.waits("vec", [self.sfree.get("st")])
                        p.add("vec", lambda v: v.tensor_copy(self.sth[:, 0:512], self.g32[:]))
                        p.add("vec", lambda v: v.tensor_tensor(self.stl[:, 0:512], self.g32[:],
                                                               self.sth[:, 0:512], Alu.subtract))
                        d = p.mark("vec")
                        self.sfree["g32"] = d
                        p.wait("ten", d)
                        p.waits("ten", [self.pfree["T"]])
                        for i_ in range(4):
                            p.add("ten", lambda tn, i_=i_: tn.transpose(
                                self.T_PS[:, i_ * 128:(i_ + 1) * 128],
                                self.sth[:, i_ * 128:(i_ + 1) * 128], self.ident_sb[:]))
                            p.add("ten", lambda tn, i_=i_: tn.transpose(
                                self.T_PS[:, 1024 + i_ * 128:1024 + (i_ + 1) * 128],
                                self.stl[:, i_ * 128:(i_ + 1) * 128], self.ident_sb[:]))
                        dT = p.mark("ten")
                        self.sfree["st"] = dT
                        p.wait("vec", dT)
                        p.add("vec", lambda v, tc=tc: v.tensor_copy(
                            self.vt_h[:, tc * 512:(tc + 1) * 512], self.T_PS[:, 0:512]))
                        p.add("vec", lambda v, tc=tc: v.tensor_copy(
                            self.vt_l[:, tc * 512:(tc + 1) * 512], self.T_PS[:, 1024:1536]))
                        d = p.mark("vec")
                        self.pfree["T"] = d
            qkv_dep = p.mark("vec")

            for hh in range(2):
                p.waits("ten", [qkv_dep, self.pfree["C"]])
                self.attn_head(hh, self.qt_h[64 * hh:64 * hh + 64, :],
                               self.qt_l[64 * hh:64 * hh + 64, :],
                               self.kt_h[64 * hh:64 * hh + 64, :],
                               self.kt_l[64 * hh:64 * hh + 64, :])
            self.pfree["C"] = self._attn_sc

            sc = self._attn_sc
            self.sfree["og"] = sc
            p.waits("syn", [self.sfree.get("wld")])
            p.waits("gps", [self.sfree.get("wld")])
            wo = W["L1_ffn_w_o"]
            d1 = p.dma("syn", out=self.wld[:, 0:1024], in_=wo[g * 128:(g + 1) * 128, :], q="DWT")
            d2 = p.dma("gps", out=self.wld_h[:, 0:1024], in_=wo[g * 128:(g + 1) * 128, :], q="DGH")
            p.waits("vec", [d1, d2, sc])
            p.add("vec", lambda v: v.tensor_tensor(
                self.wld_l[:, 0:1024], self.wld[:, 0:1024], self.wld_h[:, 0:1024], Alu.subtract))
            wo_dep = p.mark("vec")
            p.waits("ten", [sc, wo_dep])
            for tq in range(NT):
                for oc in range(2):
                    bank = (2 * tq + oc) % 2
                    p.waits("ten", [self.pfree[f"A{bank}"]])
                    ps = self.PS_A[:, bank * 512:(bank + 1) * 512]
                    self.mm3(ps, self.ogh[:, tq * 128:(tq + 1) * 128],
                             self.ogl[:, tq * 128:(tq + 1) * 128],
                             self.wld_h[:, oc * 512:(oc + 1) * 512],
                             self.wld_l[:, oc * 512:(oc + 1) * 512], first=True, last=True)
                    d = p.mark("ten")
                    if tq == NT - 1 and oc == 1:
                        self.sfree["wld"] = d
                    p.wait("vec", d)
                    xsl_ = self.x_sb[:, tq * D + oc * 512:tq * D + (oc + 1) * 512]
                    p.add("vec", lambda v, xsl_=xsl_, ps=ps: v.tensor_tensor(xsl_, xsl_, ps, Alu.add))
                    self.pfree[f"A{bank}"] = p.mark("vec")
            self.sfree["qkv_free"] = p.mark("ten")

        p.barrier()
        self.dbg_dump("xattn", self.x_sb[:])
        hb32 = self._av(0, 32768, F32)

        gtile = self._av(32768, 8192, F32)
        etile = self._av(40960, 8192, F32)
        lgT = self._av(49152, 8192, F32, parts=8)
        probs = self._av(57344, 512, F32)
        m1c = self._av(57856, 64, F32)
        m2c = self._av(57920, 64, F32)
        e1c = self._av(57984, 64, F32)
        e2c = self._av(58048, 64, F32)
        smax = self._av(58112, 4, F32)
        ssum = self._av(58116, 4, F32)
        locol = self._av(58120, 4, F32, parts=8)
        hicol = self._av(58124, 4, F32, parts=8)
        midcol = self._av(58128, 4, F32, parts=8)
        gecol = self._av(58132, 4, F32, parts=8)
        cntcol = self._av(58136, 4, F32)
        thcol = self._av(58140, 4, F32)
        throw = self._av(58144, 32, F32, parts=1)
        threp = self._av(58176, 32, F32)
        thsel1 = self._av(58240, 64, F32)
        thsel2 = self._av(58304, 64, F32)
        mke1 = self._av(58368, 64, F32)
        mke2 = self._av(58432, 64, F32)
        ce_all = self._av(58496, 512, F32)
        bis_scr = self._av(0, 4096)

        self.replicate_row(W["L1_ffn_g2"], self.gam_rep, D)
        p.add("vec", lambda v: v.memset(self.bet_rep[:], 0.0))
        self.stats_pass()
        self.ln_scalars("rms", eps=1e-6)
        xn_dep = self.transpose_x32(self.xnT32_scr, gamma=True)
        x32v = self.xsl32.rearrange("p (c s) -> p c s", c=ND)

        p.waits("syn", [self.sfree.get("wld")])
        rdep = p.dma("syn", out=self.wld[:, 0:64].rearrange("p (c o) -> p c o", c=8),
                     in_=W["L1_ffn_w_router"].rearrange("(c p) o -> p c o", p=128), q="DWT")
        for tc in range(4):
            slab_dep = self.load_xslab32(tc)
            p.waits("ten", [slab_dep, rdep, xn_dep, self.pfree["C"]])
            for c in range(ND):
                p.add("ten", lambda tn, c=c: tn.matmul(
                    self.PS_C[0:8, 0:512], self.wld[:, c * 8:(c + 1) * 8],
                    x32v[:, c, :], start=(c == 0), stop=(c == ND - 1)))
            d = p.mark("ten")
            self.sfree["xsl32"] = d
            p.wait("vec", d)
            p.add("vec", lambda v, tc=tc: v.tensor_copy(lgT[:, tc * 512:(tc + 1) * 512],
                                                        self.PS_C[0:8, 0:512]))
            self.pfree["C"] = p.mark("vec")
            self.sfree["wld"] = d
        lg_dep = self.pfree["C"]

        p.waits("ten", [lg_dep, self.pfree["T"]])
        TPSF = self.T_PS[:].bitcast(F32)
        prev_d = None
        for t in range(NT):
            p.waits("ten", [prev_d])
            p.add("ten", lambda tn, t=t: tn.transpose(
                TPSF[:, (t % 8) * 8:(t % 8) * 8 + 8],
                lgT[:, t * 128:(t + 1) * 128], self.identf_sb[0:8, 0:8]))
            d = p.mark("ten")
            p.wait("vec", d)
            P8 = probs[:, t * 8:(t + 1) * 8]
            p.add("vec", lambda v, t=t, P8=P8: v.tensor_copy(
                P8, TPSF[:, (t % 8) * 8:(t % 8) * 8 + 8]))
            p.add("vec", lambda v, P8=P8: v.reduce_max(smax[:, 0:1], P8, axis=AX.X, negate=True))
            d = p.mark("vec")
            p.wait("scl", d)
            p.add("scl", lambda s, P8=P8: s.activation(P8, P8, Act.Exp, bias=smax[:, 0:1],
                                                       accum_out=ssum[:, 0:1]))
            d = p.mark("scl")
            p.wait("vec", d)
            p.add("vec", lambda v: v.reciprocal(ssum[:, 0:1], ssum[:, 0:1]))
            p.add("vec", lambda v, P8=P8: v.tensor_scalar_mul(P8, P8, ssum[:, 0:1]))
            u8a = self.u32[:, 0:8]
            u8b = self.u32[:, 8:16]
            u1 = self.u32[:, 16:17]
            p.add("vec", lambda v, P8=P8, t=t: v.reduce_max(m1c[:, t:t + 1], P8, axis=AX.X))
            p.add("vec", lambda v, P8=P8, t=t, u8a=u8a: v.tensor_scalar(
                u8a, P8, m1c[:, t:t + 1], None, Alu.is_equal))
            p.add("vec", lambda v, u8a=u8a, u8b=u8b: v.tensor_tensor(
                u8b, u8a, self.irev_sb[:], Alu.mult))
            p.add("vec", lambda v, u8b=u8b, u1=u1: v.reduce_max(u1, u8b, axis=AX.X))
            p.add("vec", lambda v, t=t, u1=u1: v.tensor_scalar(
                e1c[:, t:t + 1], u1, -1.0, 8.0, Alu.mult, Alu.add))
            p.add("vec", lambda v, u8a=u8a, u1=u1: v.tensor_scalar(
                u8a, self.irev_sb[:], u1, None, Alu.is_equal))
            p.add("vec", lambda v, P8=P8, u8a=u8a, u8b=u8b: v.scalar_tensor_tensor(
                u8b, u8a, -1.0e30, P8, Alu.mult, Alu.add))
            p.add("vec", lambda v, u8b=u8b, t=t: v.reduce_max(m2c[:, t:t + 1], u8b, axis=AX.X))
            p.add("vec", lambda v, u8a=u8a, u8b=u8b, t=t: v.tensor_scalar(
                u8a, u8b, m2c[:, t:t + 1], None, Alu.is_equal))
            p.add("vec", lambda v, u8a=u8a: v.tensor_tensor(
                u8a, u8a, self.irev_sb[:], Alu.mult))
            p.add("vec", lambda v, u8a=u8a, u1=u1: v.reduce_max(u1, u8a, axis=AX.X))
            p.add("vec", lambda v, t=t, u1=u1: v.tensor_scalar(
                e2c[:, t:t + 1], u1, -1.0, 8.0, Alu.mult, Alu.add))
            prev_d = p.mark("vec")
            self.sfree["u32"] = prev_d
        top2_dep = p.mark("vec")

        for k, colsrc in ((0, m1c), (1, m2c)):
            p.dma("syn", out=self.m_loc[k].rearrange("(t q) -> q t", q=128), in_=colsrc[:, 0:16],
                  q="DB4", slow=True)
        for k, colsrc in ((0, e1c), (1, e2c)):
            cd2 = p.dma("syn", out=self.e_loc[k].rearrange("(t q) -> q t", q=128), in_=colsrc[:, 0:16],
                        q="DB4", slow=True)
        p.waits("syn", [top2_dep])
        p.waits("gps", [cd2])
        p.add("gps", lambda g_: g_.collective_compute(
            "AllGather", Alu.bypass, [[0, 1, 2, 3, 4, 5, 6, 7]],
            [self.m_loc[:, :]], [self.m_all[:, :]]))
        agd = p.mark("gps")
        p.add("gps", lambda g_: g_.collective_compute(
            "AllGather", Alu.bypass, [[0, 1, 2, 3, 4, 5, 6, 7]],
            [self.e_loc[:, :]], [self.e_all[:, :]]))
        agd = p.mark("gps")

        p.waits("syn", [agd])
        gd = ed = None
        for e_ in range(8):
            gd = p.dma("syn", out=gtile[16 * e_:16 * e_ + 16, :], in_=self.m_all[:, :], q="DSS")
            ed = p.dma("syn", out=etile[16 * e_:16 * e_ + 16, :], in_=self.e_all[:, :], q="DSS")
        p.waits("vec", [gd, ed])
        p.add("vec", lambda v: v.tensor_scalar(etile[:, :], etile[:, :], self.ecol_sb[:, 0:1],
                                               None, Alu.is_equal))
        p.add("vec", lambda v: v.tensor_tensor(gtile[:, :], gtile[:, :], etile[:, :], Alu.mult))
        p.add("vec", lambda v: v.tensor_scalar(etile[:, :], etile[:, :], 1.0, 1.0e30,
                                               Alu.subtract, Alu.mult))
        p.add("vec", lambda v: v.tensor_tensor(gtile[:, :], gtile[:, :], etile[:, :], Alu.add))
        p.add("vec", lambda v: v.memset(locol[:, 0:1], 0.0))
        p.add("vec", lambda v: v.memset(hicol[:, 0:1], 1.0))
        gt_dep = p.mark("vec")
        for it in range(40):
            p.add("vec", lambda v: v.tensor_tensor(midcol[:, 0:1], locol[:, 0:1], hicol[:, 0:1],
                                                   Alu.add))
            p.add("vec", lambda v: v.tensor_scalar_mul(midcol[:, 0:1], midcol[:, 0:1], 0.5))
            d = p.mark("vec")
            p.waits("ten", [d, self.pfree["C"]])
            p.add("ten", lambda tn: tn.matmul(self.PS_C[:, 0:1], self.g8t_sb[:, :],
                                              midcol[:, 0:1], start=True, stop=True))
            d = p.mark("ten")
            p.wait("vec", d)
            p.add("vec", lambda v: v.tensor_copy(thcol[:, 0:1], self.PS_C[:, 0:1]))
            d = p.mark("vec")
            self.pfree["C"] = d
            p.add("vec", lambda v: v.tensor_scalar(
                bis_scr[:, 0:2048], gtile[:, :], thcol[:, 0:1], 0.0, Alu.is_ge,
                op1=Alu.add, accum_out=cntcol[:, 0:1]))
            d = p.mark("vec")
            p.waits("ten", [d, self.pfree["D"]])
            p.add("ten", lambda tn: tn.matmul(self.PS_D[0:8, 0:1], self.g8_sb[:, :],
                                              cntcol[:, 0:1], start=True, stop=True))
            d = p.mark("ten")
            p.wait("vec", d)
            p.add("vec", lambda v: v.tensor_copy(gecol[:, 0:1], self.PS_D[0:8, 0:1]))
            d = p.mark("vec")
            self.pfree["D"] = d
            p.add("vec", lambda v: v.tensor_single_scalar(gecol[:, 0:1], gecol[:, 0:1], 1280.0,
                                                          Alu.is_ge))
            gnot = self.u32[:, 20:21]
            t1 = self.u32[:, 21:22]
            p.add("vec", lambda v, gnot=gnot: v.tensor_scalar(gnot[0:8, :], gecol[:, 0:1], 1.0, -1.0,
                                                              Alu.subtract, Alu.mult))
            p.add("vec", lambda v, t1=t1: v.tensor_tensor(t1[0:8, :], gecol[:, 0:1], midcol[:, 0:1],
                                                          Alu.mult))
            p.add("vec", lambda v, gnot=gnot: v.tensor_tensor(locol[:, 0:1], gnot[0:8, :],
                                                              locol[:, 0:1], Alu.mult))
            p.add("vec", lambda v, t1=t1: v.tensor_tensor(locol[:, 0:1], locol[:, 0:1], t1[0:8, :],
                                                          Alu.add))
            p.add("vec", lambda v, t1=t1: v.tensor_tensor(t1[0:8, :], gecol[:, 0:1], hicol[:, 0:1],
                                                          Alu.mult))
            p.add("vec", lambda v, gnot=gnot: v.tensor_tensor(hicol[:, 0:1], gnot[0:8, :],
                                                              midcol[:, 0:1], Alu.mult))
            p.add("vec", lambda v, t1=t1: v.tensor_tensor(hicol[:, 0:1], hicol[:, 0:1], t1[0:8, :],
                                                          Alu.add))
            d = p.mark("vec")
        p.waits("ten", [d, self.pfree["C"]])
        p.add("ten", lambda tn: tn.transpose(self.PS_C[0:1, 0:8], locol[:, 0:1],
                                             self.identf_sb[0:8, 0:8]))
        d = p.mark("ten")
        p.wait("vec", d)
        p.add("vec", lambda v: v.tensor_copy(throw[:, 0:8], self.PS_C[0:1, 0:8]))
        d = p.mark("vec")
        self.pfree["C"] = d
        p.waits("ten", [d, self.pfree["D"]])
        p.add("ten", lambda tn: tn.matmul(self.PS_D[:, 0:8], self.ones_sb[:, :], throw[:, 0:8],
                                          start=True, stop=True))
        d = p.mark("ten")
        p.wait("vec", d)
        p.add("vec", lambda v: v.tensor_copy(threp[:, 0:8], self.PS_D[:, 0:8]))
        d = p.mark("vec")
        self.pfree["D"] = d
        for k, (ec, ts_) in enumerate(((e1c, thsel1), (e2c, thsel2))):
            p.add("vec", lambda v, ts_=ts_: v.memset(ts_[:, 0:16], 0.0))
            for e_ in range(8):
                p.add("vec", lambda v, ec=ec, e_=e_: v.tensor_single_scalar(
                    self.u32[:, 0:16], ec[:, 0:16], float(e_), Alu.is_equal))
                p.add("vec", lambda v, ts_=ts_, e_=e_: v.scalar_tensor_tensor(
                    ts_[:, 0:16], self.u32[:, 0:16], threp[:, e_:e_ + 1], ts_[:, 0:16],
                    Alu.mult, Alu.add))
        p.add("vec", lambda v: v.tensor_tensor(mke1[:, 0:16], m1c[:, 0:16], thsel1[:, 0:16],
                                               Alu.is_ge))
        p.add("vec", lambda v: v.tensor_tensor(mke1[:, 0:16], mke1[:, 0:16], m1c[:, 0:16],
                                               Alu.mult))
        p.add("vec", lambda v: v.tensor_tensor(mke2[:, 0:16], m2c[:, 0:16], thsel2[:, 0:16],
                                               Alu.is_ge))
        p.add("vec", lambda v: v.tensor_tensor(mke2[:, 0:16], mke2[:, 0:16], m2c[:, 0:16],
                                               Alu.mult))
        for e_ in range(8):
            ce = ce_all[:, 16 * e_:16 * e_ + 16]
            p.add("vec", lambda v, ce=ce: v.memset(ce, 0.0))
            for ec, mke in ((e1c, mke1), (e2c, mke2)):
                p.add("vec", lambda v, ec=ec, e_=e_: v.tensor_single_scalar(
                    self.u32[:, 0:16], ec[:, 0:16], float(e_), Alu.is_equal))
                p.add("vec", lambda v, mke=mke: v.tensor_tensor(
                    self.u32[:, 16:32], self.u32[:, 0:16], mke[:, 0:16], Alu.mult))
                p.add("vec", lambda v, ce=ce: v.tensor_tensor(ce, ce, self.u32[:, 16:32], Alu.add))
        coeff_dep = p.mark("vec")
        self.sfree["u32"] = coeff_dep
        self.sfree["hb32"] = coeff_dep
        self.dbg_dump("m1c", m1c[:, 0:16], coeff_dep)
        self.dbg_dump("m2c", m2c[:, 0:16], coeff_dep)
        self.dbg_dump("e1c", e1c[:, 0:16], coeff_dep)
        self.dbg_dump("e2c", e2c[:, 0:16], coeff_dep)
        self.dbg_dump("threp", threp[:, 0:8], coeff_dep)
        self.dbg_dump("ce", ce_all[:, 0:128], coeff_dep)
        self.dbg_dump("probs", probs[:, 0:128], coeff_dep)

        for e_ in range(8):
            for b in range(4):
                slab_dep = self.load_xslab32(b)
                for fc in range(16):
                    p.waits("syn", [self.sfree.get("wld")])
                    d1 = None
                    for r in range(ND):
                        slc = W["L1_ffn_w1"][e_, r * 128:(r + 1) * 128, fc * 128:(fc + 1) * 128]
                        d1 = p.dma("syn", out=self.wld[:, r * 128:(r + 1) * 128], in_=slc, q="DWT")
                    bank = fc % 4
                    p.waits("ten", [slab_dep, d1, self.pfree[f"A{bank}"]])
                    ps = self.PS_A[:, bank * 512:(bank + 1) * 512]
                    for c in range(ND):
                        p.add("ten", lambda tn, c=c, ps=ps: tn.matmul(
                            ps, self.wld[:, c * 128:(c + 1) * 128], x32v[:, c, :],
                            start=(c == 0), stop=(c == ND - 1)))
                    d = p.mark("ten")
                    self.sfree["wld"] = d
                    self.sfree["xsl32"] = d
                    p.wait("scl", d)
                    p.waits("scl", [self.sfree.get("hb32")])
                    col = fc * 512
                    p.add("scl", lambda s, ps=ps, col=col: s.activation(
                        hb32[:, col:col + 512], ps, Act.Relu))
                    d = p.mark("scl")
                    self.pfree[f"A{bank}"] = d
                hb_dep = p.mark("scl")
                banks = [self.PS_A[:, i * 512:(i + 1) * 512] for i in range(4)] + \
                        [self.T_PS[:].bitcast(F32)[:, 0:512], self.T_PS[:].bitcast(F32)[:, 512:1024],
                         self.PS_C[:, 0:512], self.PS_D[:, 0:512]]
                p.waits("ten", [hb_dep, self.pfree["A0"], self.pfree["A1"], self.pfree["A2"],
                                self.pfree["A3"], self.pfree["T"], self.pfree["C"], self.pfree["D"]])
                for fc in range(16):
                    p.waits("syn", [self.sfree.get("wld")])
                    slc = W["L1_ffn_w2"][e_, fc * 128:(fc + 1) * 128, :]
                    d1 = p.dma("syn", out=self.wld[:, 0:1024], in_=slc, q="DWT")
                    p.waits("ten", [d1])
                    for tt in range(4):
                        hh_ = hb32[:, fc * 512 + tt * 128:fc * 512 + (tt + 1) * 128]
                        for oc in range(2):
                            p.add("ten", lambda tn, hh_=hh_, oc=oc, fc=fc, bk=banks[tt * 2 + oc]:
                                  tn.matmul(bk, hh_, self.wld[:, oc * 512:(oc + 1) * 512],
                                            start=(fc == 0), stop=(fc == 15)))
                    d = p.mark("ten")
                    self.sfree["wld"] = d
                d = p.mark("ten")
                self.sfree["hb32"] = d
                p.wait("vec", d)
                p.waits("vec", [coeff_dep])
                for tt in range(4):
                    t_g = 4 * b + tt
                    for oc in range(2):
                        xsl_ = self.x_sb[:, t_g * D + oc * 512:t_g * D + (oc + 1) * 512]
                        p.add("vec", lambda v, xsl_=xsl_, tt=tt, oc=oc, e_=e_, t_g=t_g:
                              v.scalar_tensor_tensor(
                                  xsl_, banks[tt * 2 + oc],
                                  ce_all[:, 16 * e_ + t_g:16 * e_ + t_g + 1],
                                  xsl_, Alu.mult, Alu.add))
                d = p.mark("vec")
                for kk in ("A0", "A1", "A2", "A3", "T", "C", "D"):
                    self.pfree[kk] = d
        for t in range(NT):
            p.waits("syn", [self.sfree.get("u32")])
            ld = p.dma("syn", out=self.u32[:], in_=self.x_scr[:, t * D:(t + 1) * D], q="DSL")
            p.waits("vec", [ld])
            xs = self.x_sb[:, t * D:(t + 1) * D]
            p.add("vec", lambda v, xs=xs: v.tensor_tensor(xs, xs, self.u32[:], Alu.add))
            self.sfree["u32"] = p.mark("vec")

    def finalize(self):
        p = self.p
        p.barrier()
        for t in range(NT):
            p.dma("syn", out=self.xout[t * 128:(t + 1) * 128, :],
                  in_=self.x_sb[:, t * D:(t + 1) * D], q="DO")
        p.ops["syn"].append(lambda e: e.wait_ge(p.sem["DO"], p.ctr["DO"]))

        with ExitStack() as sctx:
            for s in Prog.SEMS:
                p.sem[s] = sctx.enter_context(self.nc.semaphore("sem_" + s))
            with self.nc.Block() as block:
                p.emit(block)
        self.ctx.close()
        return self.nc

    def build(self):
        self.init_loads()
        if "l0a" in self.stages:
            self.l0_attn()
        if "l0f" in self.stages:
            self.p.barrier()
            self.l0_ffn()
        if "l1a" in self.stages:
            self.p.barrier()
            self.l1_attn()
        if "l1f" in self.stages:
            self.p.barrier()
            self.l1_ffn()
        return self.finalize()


_NC_CACHE = {}


def _get_program(stages):
    key = tuple(stages)
    if key not in _NC_CACHE:
        _NC_CACHE[key] = Builder(stages).build()
    return _NC_CACHE[key]


def kernel(**inputs):
    nc = _get_program(STAGES)
    x = np.ascontiguousarray(np.asarray(inputs["L0_attn_x"], dtype=np.float32))
    wmap = {k: np.ascontiguousarray(np.asarray(v, dtype=np.float32))
            for k, v in inputs.items() if k != "L0_attn_x"}
    in_maps = []
    for c in range(8):
        m = {"x_in": x[c]}
        m.update(wmap)
        in_maps.append(m)
    res = run_bass_kernel_spmd(nc, in_maps, core_ids=list(range(8)))
    return np.stack([r["x_out"] for r in res.results], axis=0)


# version 06
# speedup vs baseline: 1.2676x; 1.2574x over previous
import numpy as np
import ml_dtypes
from contextlib import ExitStack

import concourse.bass as bass
import concourse.mybir as mybir
from concourse.bass_utils import run_bass_kernel_spmd

F32 = mybir.dt.float32
BF16 = mybir.dt.bfloat16
Alu = mybir.AluOpType
Act = mybir.ActivationFunctionType
AX = mybir.AxisListType
BF = ml_dtypes.bfloat16

S = 2048
D = 1024
NT = S // 128
ND = D // 128
NH = 16
BIAS_S = 32
NEG = -3.0e38

STAGES = ("l0a", "l0f", "l1a", "l1f")


def _split_const(a):
    hi = a.astype(BF)
    lo = (a - hi.astype(np.float32)).astype(BF)
    return hi, lo


def _interp_matrix(n, s):
    pos = np.arange(n, dtype=np.float32) * np.float32((s - 1) / (n - 1))
    i0 = np.floor(pos).astype(np.int32)
    i1 = np.minimum(i0 + 1, s - 1)
    frac = (pos - i0.astype(np.float32)).astype(np.float32)
    M = np.zeros((n, s), dtype=np.float32)
    np.add.at(M, (np.arange(n), i0), (1.0 - frac))
    np.add.at(M, (np.arange(n), i1), frac)
    return M


class Prog:
    ENGS = ("ten", "vec", "scl", "gps", "syn")
    SEMS = ("T", "V", "S", "G", "DIO", "DWT", "DGH", "DAU", "DSL", "DS2", "DSS", "DGR", "DRR", "DB4", "DO")
    DMA_SEMS = frozenset(SEMS[4:])

    def __init__(self):
        self.ops = {e: [] for e in self.ENGS}
        self.ctr = {s: 0 for s in self.SEMS}
        self.waited = {e: {s: 0 for s in self.SEMS} for e in self.ENGS}
        self.own = {"ten": "T", "vec": "V", "scl": "S", "gps": "G", "syn": None}
        self.sem = {}
        self.dirty = {e: False for e in self.ENGS}
        self.last_real = {e: None for e in self.ENGS}
        self.last_mark = {e: None for e in self.ENGS}
        self.chain = {"vec": True, "scl": True, "gps": True}
        self.chain_val = {e: 0 for e in self.ENGS}

    def add(self, eng, fn):
        if self.chain.get(eng):
            sname = self.own[eng]
            prevv = self.chain_val[eng]
            if prevv:
                self.ops[eng].append(lambda e, sn=sname, v=prevv: e.wait_ge(self.sem[sn], v))
            self.ctr[sname] += 1
            val = self.ctr[sname]
            self.ops[eng].append(lambda e, fn=fn, sn=sname: fn(e).then_inc(self.sem[sn], 1))
            self.chain_val[eng] = val
            self.last_mark[eng] = (sname, val)
            self.dirty[eng] = False
            return
        self.ops[eng].append(fn)
        self.last_real[eng] = len(self.ops[eng]) - 1
        self.dirty[eng] = True

    def mark(self, eng):
        if self.chain.get(eng):
            return self.last_mark[eng]
        if not self.dirty[eng]:
            return self.last_mark[eng]
        sname = self.own[eng]
        self.ctr[sname] += 1
        val = self.ctr[sname]
        i = self.last_real[eng]
        prev = self.ops[eng][i]
        self.ops[eng][i] = lambda e, prev=prev, sn=sname: prev(e).then_inc(self.sem[sn], 1)
        self.dirty[eng] = False
        self.last_mark[eng] = (sname, val)
        return (sname, val)

    def wait(self, eng, dep):
        if dep is None:
            return
        sname, val = dep
        if sname in self.DMA_SEMS and val < self.ctr[sname]:
            val = self.ctr[sname]
        if self.own[eng] == sname:
            return
        if self.waited[eng][sname] >= val:
            return
        self.waited[eng][sname] = val
        self.ops[eng].append(lambda e, sn=sname, v=val: e.wait_ge(self.sem[sn], v))

    def waits(self, eng, deps):
        for d in deps:
            self.wait(eng, d)

    def wait_force(self, eng, dep):
        if dep is None:
            return
        sname, val = dep
        self.ops[eng].append(lambda e, sn=sname, v=val: e.wait_ge(self.sem[sn], v))

    def dma(self, eng, out, in_, q=None, slow=False):
        assert q is not None, "specify a DMA stream"
        qname = q
        self.ctr[qname] += 16
        val = self.ctr[qname]
        self.ops[eng].append(
            lambda e, out=out, in_=in_, qn=qname, slow=slow: e.dma_start(
                out=out, in_=in_, allow_slow_non_contiguous=slow).then_inc(self.sem[qn], 16))
        return (qname, val)

    def barrier(self):
        marks = [self.mark(e) for e in ("ten", "vec", "scl", "gps")]
        for e in self.ENGS:
            for m in marks:
                self.wait(e, m)

    def emit(self, block):
        def mk(name):
            def body(eng):
                for fn in self.ops[name]:
                    fn(eng)
            return body
        block.sync(mk("syn"))
        block.tensor(mk("ten"))
        block.vector(mk("vec"))
        block.scalar(mk("scl"))
        block.gpsimd(mk("gps"))


class Builder:
    def __init__(self, stages):
        self.stages = stages
        nc = bass.Bass("TRN2", target_bir_lowering=False)
        self.nc = nc
        self.p = Prog()
        self.ctx = ExitStack()

        self.xin = nc.dram_tensor("x_in", [S, D], F32, kind="ExternalInput").ap()
        self.xout = nc.dram_tensor("x_out", [S, D], F32, kind="ExternalOutput").ap()
        WSHAPES = {
            "L0_attn_Wq": (D, D), "L0_attn_bq": (D,), "L0_attn_Wk": (D, D), "L0_attn_bk": (D,),
            "L0_attn_Wv": (D, D), "L0_attn_bv": (D,), "L0_attn_Wo": (D, D), "L0_attn_bo": (D,),
            "L0_attn_gamma": (D,), "L0_attn_beta": (D,), "L0_attn_rel_bias": (NH, BIAS_S, BIAS_S),
            "L0_ffn_w_up": (4096, D), "L0_ffn_w_down": (D, 4096),
            "L1_attn_in_proj_w": (4096, D), "L1_attn_conv_w": (2048, 256, 4),
            "L1_attn_conv_b": (2048,), "L1_attn_ln_g": (2048,), "L1_attn_ln_b": (2048,),
            "L1_attn_fc_w": (D, 2048), "L1_attn_fc_b": (D,), "L1_attn_cproj_w": (2048, D),
            "L1_attn_cproj_b": (2048,), "L1_attn_out_w": (D, 2048),
            "L1_ffn_g1": (D,), "L1_ffn_w_qkv": (D, 3 * D), "L1_ffn_w_o": (D, D),
            "L1_ffn_g2": (D,), "L1_ffn_w_router": (D, 8),
            "L1_ffn_w1": (8, D, 2048), "L1_ffn_w2": (8, 2048, D),
        }
        self.W = {k: nc.dram_tensor(k, list(v), F32, kind="ExternalInput").ap()
                  for k, v in WSHAPES.items()}

        self.ident_bf_c = nc.inline_tensor(np.eye(128, dtype=np.float32).astype(BF), "c_identbf")
        self.ident_f_c = nc.inline_tensor(np.eye(128, dtype=np.float32), "c_identf")
        self.ones_c = nc.inline_tensor(np.ones((1, 128), np.float32), "c_ones")
        self.onesbf_c = nc.inline_tensor(np.ones((1, 128), np.float32).astype(BF), "c_onesbf")
        self.onescol_c = nc.inline_tensor(np.ones((128, 1), np.float32), "c_onescol")
        self.mask_c = nc.inline_tensor(np.triu(np.full((128, 128), NEG, np.float32), k=1), "c_mask")
        cm = np.zeros((128, 896), np.float32)
        for p_ in range(128):
            cm[p_, :] = np.where(p_ > (np.arange(896) - 384), NEG, 0.0)
        self.cmask_c = nc.inline_tensor(cm, "c_cmask")
        self.ones128_c = nc.inline_tensor(np.ones((128, 128), np.float32), "c_ones128")
        M = _interp_matrix(S, BIAS_S)
        Mq = M.copy()
        Mq[-1, :] = 0.0
        self.mqt32_c = nc.inline_tensor(np.ascontiguousarray(Mq.T), "c_mqt32")
        theta = (10000.0 ** (-2.0 * (np.arange(32) / 64.0))).astype(np.float64)
        tpos = np.arange(S, dtype=np.float64)
        fr = tpos[None, :] * theta[:, None]
        cosd = np.cos(fr).astype(np.float32)
        sind = np.sin(fr).astype(np.float32)
        C1 = np.zeros((128, S), np.float32)
        C2 = np.zeros((128, S), np.float32)
        for p_ in range(128):
            f = (p_ % 64) // 2
            C1[p_] = cosd[f]
            C2[p_] = (-sind[f]) if (p_ % 2 == 0) else sind[f]
        self.rope1_c = nc.inline_tensor(C1, "c_rope1")
        self.rope2_c = nc.inline_tensor(C2, "c_rope2")
        self.irev_c = nc.inline_tensor(
            np.broadcast_to((8.0 - np.arange(8, dtype=np.float32)), (128, 8)).copy(), "c_irev")
        self.ecol_c = nc.inline_tensor((np.arange(128, dtype=np.float32) // 16)[:, None].copy(), "c_ecol")
        G8 = np.zeros((128, 8), np.float32)
        for p_ in range(128):
            G8[p_, p_ // 16] = 1.0
        self.g8_c = nc.inline_tensor(G8, "c_g8")
        self.g8t_c = nc.inline_tensor(np.ascontiguousarray(G8.T), "c_g8t")

        e = self.ctx.enter_context
        sb = lambda nm, shp, dt=BF16: e(nc.sbuf_tensor(nm, shp, dt))
        self.x_sb = sb("x_sb", [128, NT * D], F32)
        self.xsl_h = sb("xsl_h", [128, ND * 512])
        self.xsl_l = sb("xsl_l", [128, ND * 512])
        self.xnT_scr_h = nc.dram_tensor("xnT_scr_h", [128, ND * S], BF16, kind="Internal").ap()
        self.xnT_scr_l = nc.dram_tensor("xnT_scr_l", [128, ND * S], BF16, kind="Internal").ap()
        self.xnT32_scr = nc.dram_tensor("xnT32_scr", [128, ND * S], F32, kind="Internal").ap()
        self.wupT_scr = nc.dram_tensor("wupT_scr", [128, 32 * 1024], BF16, kind="Internal").ap()
        self.wdownT_scr = nc.dram_tensor("wdownT_scr", [128, 32 * 1024], BF16, kind="Internal").ap()
        self.x_scr = nc.dram_tensor("x_scr", [128, NT * D], F32, kind="Internal").ap()
        self.m_loc = nc.dram_tensor("m_loc", [2, S], F32, kind="Internal").ap()
        self.e_loc = nc.dram_tensor("e_loc", [2, S], F32, kind="Internal").ap()
        self.m_all = nc.dram_tensor("m_all", [16, S], F32, kind="Internal").ap()
        self.e_all = nc.dram_tensor("e_all", [16, S], F32, kind="Internal").ap()
        self.u32 = sb("u32", [128, D], F32)
        self.sth = sb("sth", [128, D])
        self.stl = sb("stl", [128, D])
        for nm in ("sm", "sq", "mu", "rho", "mb", "tt1", "tt2", "r1s", "mb1s"):
            setattr(self, nm, sb(nm, [128, NT], F32))
        self.wld = sb("wld", [128, 1024], F32)
        self.wld_h = sb("wld_h", [128, 1024])
        self.wld_l = sb("wld_l", [128, 1024])
        self.wT_h = sb("wT_h", [128, 3 * 1024])
        self.wT_l = sb("wT_l", [128, 3 * 1024])
        self.g32 = sb("g32", [128, 512], F32)
        self.relbT = sb("relbT", [32, 128], F32)
        self.rb_st = sb("rb_st", [32, 128], F32)
        self.mneg = sb("mneg", [128, 1], F32)
        self.lsum = sb("lsum", [128, 1], F32)
        self.linv = sb("linv", [128, 1], F32)
        self.AR = sb("arena", [128, 43008])
        self.qt32 = self._av(0, 8192, F32)
        self.kt32 = self._av(8192, 8192, F32)
        self.vta = self._av(16384, 8320, F32)
        self.og32 = self._av(24704, 8192, F32)
        self.p32a = self._av(32896, 2048, F32)
        self.p32b = self._av(34944, 2048, F32)
        self.w2t = self._av(36992, 8192, F32, parts=32)
        self.lrow64 = self._av(36992, 2048, F32, parts=1, p0=64)
        self.mqt32 = self._av(45184, 8192, F32, parts=32)
        self.brow32 = self._av(45184, 4096, F32, parts=1, p0=64)
        self.wT32 = self._av(53376, 12288, F32)
        self.xsl32a = self._av(65664, 16384, F32)
        self.smalls = self._av(82048, 1920, F32)
        self.xsl32 = self._av(59392, 16384, F32)
        self.ident_sb = sb("ident_sb", [128, 128])
        self.identf_sb = sb("identf_sb", [128, 128], F32)
        self.ones_sb = sb("ones_sb", [1, 128], F32)
        self.onescol_sb = sb("onescol_sb", [128, 1], F32)
        self.onesb_sb = sb("onesb_sb", [1, 128])
        self.mask_sb = sb("mask_sb", [128, 128], F32)
        self.bias_fm = sb("bias_fm", [128, 4 * ND], F32)
        self.gb_fm = sb("gb_fm", [128, 2 * ND], F32)
        self.cmask_sb = sb("cmask_sb", [128, 896], F32)
        self.ones128_sb = sb("ones128_sb", [128, 128], F32)
        self.bias2 = sb("bias2", [128, 72], F32)
        self.irev_sb = sb("irev_sb", [128, 8], F32)
        self.ecol_sb = sb("ecol_sb", [128, 1], F32)
        self.g8_sb = sb("g8_sb", [128, 8], F32)
        self.g8t_sb = sb("g8t_sb", [8, 128], F32)
        self.grow = sb("grow", [1, 512], F32)

        self.PS_A = e(nc.psum_tensor("PS_A", [128, 2048], F32))
        self.T_PS = e(nc.psum_tensor("T_PS", [128, 2048], BF16))
        self.PS_C = e(nc.psum_tensor("PS_C", [128, 512], F32))
        self.PS_D = e(nc.psum_tensor("PS_D", [128, 512], F32))

        self.pfree = {k: None for k in ("A0", "A1", "A2", "A3", "T", "C", "D")}
        self.sfree = {}

        self.xh_v = self.xsl_h[:].rearrange("p (c s) -> p c s", c=ND)
        self.xl_v = self.xsl_l[:].rearrange("p (c s) -> p c s", c=ND)

    DBG = False

    def dbg_dump(self, name, ap, dep=None):
        if not self.DBG:
            return
        self.p.waits("syn", [dep, ("DB4", self.p.ctr["DB4"])])
        t = self.nc.dram_tensor("dbg_" + name, list(ap.shape), ap.dtype, kind="ExternalOutput").ap()
        dd = self.p.dma("syn", out=t, in_=ap, q="DB4")
        for eng in ("vec", "scl", "ten"):
            self.p.wait(eng, dd)

    def _av(self, off_bytes, nbytes, dt=None, parts=128, p0=0):
        v = self.AR[p0:p0 + parts, off_bytes // 2:(off_bytes + nbytes) // 2]
        if dt is F32:
            v = v.bitcast(F32)
        return v

    def init_loads(self):
        p = self.p
        for t in range(NT):
            p.dma("syn", out=self.x_sb[:, t * D:(t + 1) * D], in_=self.xin[t * 128:(t + 1) * 128, :], q="DIO")
        p.dma("syn", out=self.ident_sb[:], q="DIO", in_=self.ident_bf_c.ap())
        p.dma("syn", out=self.identf_sb[:], q="DIO", in_=self.ident_f_c.ap())
        p.dma("syn", out=self.ones_sb[:], q="DIO", in_=self.ones_c.ap())
        p.dma("syn", out=self.onescol_sb[:], q="DIO", in_=self.onescol_c.ap())
        p.dma("syn", out=self.onesb_sb[:], q="DIO", in_=self.onesbf_c.ap())
        p.dma("syn", out=self.cmask_sb[:], q="DIO", in_=self.cmask_c.ap())
        dep = p.dma("syn", out=self.ones128_sb[:], q="DIO", in_=self.ones128_c.ap())
        for eng in ("ten", "vec", "scl", "gps"):
            p.wait(eng, dep)

    def replicate_row(self, dram_vec, dst, n):
        p = self.p
        for c0 in range(0, n, 512):
            w_ = min(512, n - c0)
            p.waits("syn", [self.sfree.get("grow")])
            dd = p.dma("syn", out=self.grow[:, 0:w_],
                       in_=dram_vec[c0:c0 + w_].rearrange("d -> () d"), q="DGR")
            p.waits("ten", [self.pfree["C"], dd])
            p.add("ten", lambda t, w_=w_: t.matmul(
                self.PS_C[:, 0:w_], self.ones_sb[:, :], self.grow[:, 0:w_],
                start=True, stop=True))
            d = p.mark("ten")
            self.sfree["grow"] = d
            p.wait("vec", d)
            p.add("vec", lambda v, c0=c0, w_=w_: v.tensor_copy(dst[:, c0:c0 + w_], self.PS_C[:, 0:w_]))
            self.pfree["C"] = p.mark("vec")

    def split_row(self, dram_vec, n):
        p = self.p
        d = None
        for c0 in range(0, n, 512):
            w_ = min(512, n - c0)
            p.waits("syn", [self.sfree.get("grow")])
            dd = p.dma("syn", out=self.grow[:, 0:w_],
                       in_=dram_vec[c0:c0 + w_].rearrange("d -> () d"), q="DGR")
            p.waits("vec", [dd])
            p.add("vec", lambda v, c0=c0, w_=w_: v.tensor_copy(
                self.brow_h[:, c0:c0 + w_], self.grow[:, 0:w_]))
            p.add("vec", lambda v, c0=c0, w_=w_: v.tensor_tensor(
                self.brow_l[:, c0:c0 + w_], self.grow[:, 0:w_], self.brow_h[:, c0:c0 + w_],
                Alu.subtract))
            d = p.mark("vec")
            self.sfree["grow"] = d
        return d

    def stats_pass(self):
        p = self.p
        sd = None
        for t in range(NT):
            xs = self.x_sb[:, t * D:(t + 1) * D]
            p.add("vec", lambda v, xs=xs, t=t: v.reduce_sum(self.sm[:, t:t + 1], xs, axis=AX.X))
            p.wait_force("scl", sd)
            p.add("scl", lambda s, xs=xs, t=t: s.activation(
                self.u32[:], xs, Act.Square, accum_out=self.sq[:, t:t + 1]))
            sd = p.mark("scl")
        self._stats_dep = sd
        return (p.mark("vec"), sd)

    def ln_scalars(self, mode, eps=1e-5, eps2=1e-5):
        p = self.p
        p.wait("vec", getattr(self, "_stats_dep", None))
        if mode == "rms":
            p.add("vec", lambda v: v.tensor_scalar_mul(self.tt1[:], self.sq[:], 1.0 / D))
            p.add("vec", lambda v: v.tensor_scalar_add(self.tt1[:], self.tt1[:], eps))
            d = p.mark("vec")
            p.wait("scl", d)
            p.add("scl", lambda s: s.activation(self.tt1[:], self.tt1[:], Act.Sqrt))
            d = p.mark("scl")
            p.wait("vec", d)
            p.add("vec", lambda v: v.reciprocal(self.rho[:], self.tt1[:]))
            p.add("vec", lambda v: v.tensor_scalar_mul(self.mb[:], self.rho[:], 0.0))
            self._ln_dep = p.mark("vec")
            return self._ln_dep
        p.add("vec", lambda v: v.tensor_scalar_mul(self.mu[:], self.sm[:], 1.0 / D))
        p.add("vec", lambda v: v.tensor_scalar_mul(self.tt1[:], self.sq[:], 1.0 / D))
        p.add("vec", lambda v: v.tensor_tensor(self.tt2[:], self.mu[:], self.mu[:], Alu.mult))
        p.add("vec", lambda v: v.tensor_tensor(self.tt1[:], self.tt1[:], self.tt2[:], Alu.subtract))
        p.add("vec", lambda v: v.tensor_scalar_add(self.tt2[:], self.tt1[:], eps))
        d = p.mark("vec")
        p.wait("scl", d)
        p.add("scl", lambda s: s.activation(self.tt2[:], self.tt2[:], Act.Sqrt))
        d = p.mark("scl")
        p.wait("vec", d)
        p.add("vec", lambda v: v.reciprocal(self.tt2[:], self.tt2[:]))
        p.add("vec", lambda v: v.tensor_copy(self.r1s[:], self.tt2[:]))
        p.add("vec", lambda v: v.tensor_tensor(self.mb1s[:], self.mu[:], self.r1s[:], Alu.mult))
        p.add("vec", lambda v: v.tensor_scalar_mul(self.mb1s[:], self.mb1s[:], -1.0))
        if mode == "lnln":
            p.add("vec", lambda v: v.tensor_tensor(self.rho[:], self.tt2[:], self.tt2[:], Alu.mult))
            p.add("vec", lambda v: v.tensor_tensor(self.rho[:], self.rho[:], self.tt1[:], Alu.mult))
            p.add("vec", lambda v: v.tensor_scalar_add(self.rho[:], self.rho[:], eps2))
            d = p.mark("vec")
            p.wait("scl", d)
            p.add("scl", lambda s: s.activation(self.rho[:], self.rho[:], Act.Sqrt))
            d = p.mark("scl")
            p.wait("vec", d)
            p.add("vec", lambda v: v.reciprocal(self.rho[:], self.rho[:]))
            p.add("vec", lambda v: v.tensor_tensor(self.rho[:], self.rho[:], self.tt2[:], Alu.mult))
        elif mode == "ln_rms":
            d = p.mark("vec")
            p.wait("scl", d)
            p.add("scl", lambda s: s.activation(self.tt1[:], self.tt1[:], Act.Sqrt))
            d = p.mark("scl")
            p.wait("vec", d)
            p.add("vec", lambda v: v.tensor_tensor(self.tt1[:], self.tt1[:], self.tt2[:], Alu.mult))
            p.add("vec", lambda v: v.tensor_scalar_add(self.tt1[:], self.tt1[:], 1e-8))
            p.add("vec", lambda v: v.reciprocal(self.tt1[:], self.tt1[:]))
            p.add("vec", lambda v: v.tensor_tensor(self.rho[:], self.tt1[:], self.tt2[:], Alu.mult))
        else:
            p.add("vec", lambda v: v.tensor_copy(self.rho[:], self.tt2[:]))
        p.add("vec", lambda v: v.tensor_tensor(self.mb[:], self.mu[:], self.rho[:], Alu.mult))
        p.add("vec", lambda v: v.tensor_scalar_mul(self.mb[:], self.mb[:], -1.0))
        self._ln_dep = p.mark("vec")
        return self._ln_dep

    def split_transpose_x(self, dst_h, dst_l, gamma, tiles=None, span=S, nchunk=ND,
                          to_sbuf=False):
        p = self.p
        if tiles is None:
            tiles = range(NT)
        tiles = list(tiles)
        t0 = tiles[0]
        last = None
        for t in tiles:
            xs = self.x_sb[:, t * D:(t + 1) * D]
            p.waits("scl", [self.sfree.get("u32"), getattr(self, "_ln_dep", None)])
            p.add("scl", lambda s, xs=xs, t=t: s.activation(
                self.u32[:], xs, Act.Identity, scale=self.rho[:, t:t + 1], bias=self.mb[:, t:t + 1]))
            d = p.mark("scl")
            p.wait("vec", d)
            src = self.u32
            if gamma:
                p.add("vec", lambda v: v.tensor_tensor(self.u32[:], self.u32[:], self.gam_rep[:], Alu.mult))
                p.add("vec", lambda v: v.tensor_tensor(self.u32[:], self.u32[:], self.bet_rep[:], Alu.add))
            p.waits("vec", [self.sfree.get("st")])
            p.add("vec", lambda v, src=src: v.tensor_copy(self.sth[:], src[:]))
            p.add("vec", lambda v, src=src: v.tensor_tensor(self.stl[:], src[:], self.sth[:], Alu.subtract))
            d = p.mark("vec")
            self.sfree["u32"] = d
            p.wait("ten", d)
            p.waits("ten", [self.pfree["T"]])
            for c in range(ND):
                p.add("ten", lambda tn, c=c: tn.transpose(
                    self.T_PS[:, c * 128:(c + 1) * 128], self.sth[:, c * 128:(c + 1) * 128], self.ident_sb[:]))
            for c in range(ND):
                p.add("ten", lambda tn, c=c: tn.transpose(
                    self.T_PS[:, 1024 + c * 128:1024 + (c + 1) * 128],
                    self.stl[:, c * 128:(c + 1) * 128], self.ident_sb[:]))
            d = p.mark("ten")
            self.sfree["st"] = d
            p.wait("vec", d)
            col = 128 * (t - t0)
            nc_cols = nchunk * 128
            if to_sbuf:
                dh_ = dst_h.rearrange("p (c s) -> p c s", c=nchunk)[:, :, col:col + 128]
                dl_ = dst_l.rearrange("p (c s) -> p c s", c=nchunk)[:, :, col:col + 128]
                p.waits("vec", [self.sfree.get("xsl")])
                p.add("vec", lambda v, dh_=dh_: v.tensor_copy(
                    dh_, self.T_PS[:, 0:nc_cols].rearrange("p (c k) -> p c k", c=nchunk)))
                p.add("vec", lambda v, dl_=dl_: v.tensor_copy(
                    dl_, self.T_PS[:, 1024:1024 + nc_cols].rearrange("p (c k) -> p c k", c=nchunk)))
                last = p.mark("vec")
                self.pfree["T"] = last
                continue
            p.waits("vec", [self.sfree.get("xsl")])
            p.add("vec", lambda v: v.tensor_copy(self.xsl_h[:, 0:1024], self.T_PS[:, 0:1024]))
            p.add("vec", lambda v: v.tensor_copy(self.xsl_l[:, 0:1024], self.T_PS[:, 1024:2048]))
            last = p.mark("vec")
            self.pfree["T"] = last
            p.waits("syn", [last])
            dh_ = dst_h.rearrange("p (c s) -> p c s", c=nchunk)[:, :, col:col + 128]
            dl_ = dst_l.rearrange("p (c s) -> p c s", c=nchunk)[:, :, col:col + 128]
            p.dma("syn", out=dh_, in_=self.xsl_h[:, 0:1024].rearrange("p (c k) -> p c k", c=nchunk), q="DSS")
            sd = p.dma("syn", out=dl_, in_=self.xsl_l[:, 0:1024].rearrange("p (c k) -> p c k", c=nchunk), q="DSS")
            self.sfree["xsl"] = sd
            last = sd
        return last

    def load_xslab(self, scr_h, scr_l, tc, span=S, nchunk=ND):
        p = self.p
        p.waits("syn", [self.sfree.get("xsl")])
        sh = scr_h.rearrange("p (c s) -> p c s", c=nchunk)[:, :, tc * 512:(tc + 1) * 512]
        sl = scr_l.rearrange("p (c s) -> p c s", c=nchunk)[:, :, tc * 512:(tc + 1) * 512]
        p.dma("syn", out=self.xsl_h[:].rearrange("p (c s) -> p c s", c=ND), in_=sh, q="DSL")
        d = p.dma("syn", out=self.xsl_l[:].rearrange("p (c s) -> p c s", c=ND), in_=sl, q="DSL")
        return d

    def transpose_x32(self, dst_scr, gfm=None, bfm=None):
        p = self.p
        TPSF = self.T_PS[:].bitcast(F32)
        last = None
        for t in range(NT):
            xs = self.x_sb[:, t * D:(t + 1) * D]
            p.waits("scl", [self.sfree.get("u32"), getattr(self, "_ln_dep", None)])
            p.add("scl", lambda s, xs=xs, t=t: s.activation(
                self.u32[:], xs, Act.Identity, scale=self.rho[:, t:t + 1], bias=self.mb[:, t:t + 1]))
            d = p.mark("scl")
            p.wait("ten", d)
            p.waits("ten", [self.pfree["T"]])
            for c in range(ND):
                p.add("ten", lambda tn, c=c: tn.transpose(
                    TPSF[:, c * 128:(c + 1) * 128], self.u32[:, c * 128:(c + 1) * 128],
                    self.identf_sb[:]))
            dT = p.mark("ten")
            self.sfree["u32"] = dT
            stage = self.xsl_h[:].bitcast(F32)[:, 0:1024]
            if gfm is not None:
                p.wait("scl", dT)
                p.waits("scl", [self.sfree.get("xsl")])
                for c in range(ND):
                    bias = bfm[:, c:c + 1] if bfm is not None else 0.0
                    p.add("scl", lambda s, c=c, bias=bias, stage=stage: s.activation(
                        stage[:, c * 128:(c + 1) * 128], TPSF[:, c * 128:(c + 1) * 128],
                        Act.Identity, scale=gfm[:, c:c + 1], bias=bias))
                ved = p.mark("scl")
            else:
                p.wait("vec", dT)
                p.waits("vec", [self.sfree.get("xsl")])
                p.add("vec", lambda v, stage=stage: v.tensor_copy(stage, TPSF[:, 0:1024]))
                ved = p.mark("vec")
            self.pfree["T"] = ved
            p.waits("syn", [ved])
            dst = dst_scr.rearrange("p (c s) -> p c s", c=ND)[:, :, t * 128:(t + 1) * 128]
            sd = p.dma("syn", out=dst, in_=stage.rearrange("p (c k) -> p c k", c=ND), q="DSS")
            self.sfree["xsl"] = sd
            last = sd
        return last

    def load_xslab32(self, tc):
        p = self.p
        p.waits("syn", [self.sfree.get("xsl32")])
        src = self.xnT32_scr.rearrange("p (c s) -> p c s", c=ND)[:, :, tc * 512:(tc + 1) * 512]
        d = p.dma("syn", out=self.xsl32.rearrange("p (c s) -> p c s", c=ND), in_=src, q="DSL")
        return d

    def load_xslab32a(self, tc):
        p = self.p
        p.waits("syn", [self.sfree.get("xsl32")])
        src = self.xnT32_scr.rearrange("p (c s) -> p c s", c=ND)[:, :, tc * 512:(tc + 1) * 512]
        d = p.dma("syn", out=self.xsl32a.rearrange("p (c s) -> p c s", c=ND), in_=src, q="DSL")
        return d

    def load_w_rows32(self, dram_rows, ncols):
        p = self.p
        p.waits("syn", [self.sfree.get("wld")])
        return p.dma("syn", out=self.wld[:, 0:ncols], in_=dram_rows, q="DWT")

    def load_w_cols32(self, dram, rows, c0, ncols_src):
        p = self.p
        p.waits("syn", [self.sfree.get("wld")])
        d1 = None
        for r in range(rows // 128):
            src = dram[r * 128:(r + 1) * 128, c0:c0 + ncols_src]
            d1 = p.dma("syn", out=self.wld[:, r * ncols_src:(r + 1) * ncols_src], in_=src, q="DWT")
        return d1

    def transpose_weight32(self, slot, dep, extra=()):
        p = self.p
        TPSF = self.T_PS[:].bitcast(F32)
        p.wait("ten", dep)
        p.waits("ten", [d for d in extra if d] + [self.pfree["T"]])
        for c in range(ND):
            p.add("ten", lambda tn, c=c: tn.transpose(
                TPSF[:, c * 128:(c + 1) * 128], self.wld[:, c * 128:(c + 1) * 128],
                self.identf_sb[:]))
        dT = p.mark("ten")
        self.sfree["wld"] = dT
        p.wait("vec", dT)
        p.waits("vec", [d for d in extra if d])
        p.add("vec", lambda v, slot=slot: v.tensor_copy(
            self.wT32[:, slot * 1024:(slot + 1) * 1024], TPSF[:, 0:1024]))
        d = p.mark("vec")
        self.pfree["T"] = d
        return d

    def negb_head(self, hh, rmax_cell=None, qkv_dep=None):
        p = self.p
        sm = self.smalls
        b0 = 64 * hh
        red_d = None
        for j, src in enumerate((self.qt32, self.kt32)):
            dten = None
            for tc in range(4):
                p.waits("vec", [self.sfree.get("g32")] +
                        ([qkv_dep] if tc == 0 and j == 0 else []) + [red_d])
                p.add("vec", lambda v, src=src, tc=tc, b0=b0: v.tensor_tensor(
                    self.g32[b0:b0 + 64, :], src[b0:b0 + 64, tc * 512:(tc + 1) * 512],
                    src[b0:b0 + 64, tc * 512:(tc + 1) * 512], Alu.mult))
                d = p.mark("vec")
                p.waits("ten", [d] + ([self.pfree["D"]] if tc == 0 and j == 0 else []))
                p.add("ten", lambda tn, tc=tc, b0=b0: tn.matmul(
                    self.PS_D[0:1, 0:512], self.ones128_sb[b0:b0 + 64, 0:1],
                    self.g32[b0:b0 + 64, :], start=(tc == 0), stop=(tc == 3)))
                dten = p.mark("ten")
                self.sfree["g32"] = dten
            p.wait("vec", dten)
            p.add("vec", lambda v, j=j: v.reduce_max(sm[0:1, 1 + j:2 + j],
                                                     self.PS_D[0:1, 0:512], axis=AX.X))
            red_d = p.mark("vec")
        p.add("vec", lambda v: v.tensor_tensor(sm[0:1, 3:4], sm[0:1, 1:2], sm[0:1, 2:3], Alu.mult))
        d = p.mark("vec")
        p.wait("scl", d)
        p.add("scl", lambda s: s.activation(sm[0:1, 3:4], sm[0:1, 3:4], Act.Sqrt))
        d = p.mark("scl")
        p.wait("vec", d)
        if rmax_cell is not None:
            p.add("vec", lambda v: v.tensor_tensor(sm[0:1, 3:4], sm[0:1, 3:4], rmax_cell, Alu.add))
        p.add("vec", lambda v: v.tensor_scalar_mul(sm[0:1, 3:4], sm[0:1, 3:4], -1.0))
        d = p.mark("vec")
        p.waits("ten", [d])
        p.add("ten", lambda tn: tn.matmul(self.PS_D[:, 0:1], self.ones_sb[:, :], sm[0:1, 3:4],
                                          start=True, stop=True))
        d = p.mark("ten")
        p.waits("vec", [d, self.sfree.get("negb")])
        p.add("vec", lambda v: v.tensor_copy(sm[:, 0:1], self.PS_D[:, 0:1]))
        d = p.mark("vec")
        self.pfree["D"] = d
        return d

    def attn_head32(self, hh, use_b4, qkv_dep, negb_dep):
        p = self.p
        b0 = 64 * hh
        negb = self.smalls[:, 0:1]
        ptiles = (self.p32a, self.p32b)
        vtav = self.vta.rearrange("p (t k) -> p t k", k=130)
        if hh == 0:
            self._og_dmas = []
        pvd = None
        for qb in range(4):
            nkt = 4 * qb + 4
            exp_marks = [None] * nkt
            for kt in range(nkt + 1):
                if kt < nkt:
                    bank = kt % 2
                    p.waits("ten", [qkv_dep, self.pfree[f"A{bank}"]])
                    ps = self.PS_A[:, bank * 512:(bank + 1) * 512]
                    p.add("ten", lambda tn, ps=ps, kt=kt, qb=qb, b0=b0: tn.matmul(
                        ps, self.kt32[b0:b0 + 64, kt * 128:(kt + 1) * 128],
                        self.qt32[b0:b0 + 64, qb * 512:(qb + 1) * 512],
                        start=True, stop=not use_b4))
                    if use_b4:
                        p.add("ten", lambda tn, ps=ps, kt=kt, qb=qb: tn.matmul(
                            ps, self.w2t[:, kt * 128:(kt + 1) * 128],
                            self.mqt32[:, qb * 512:(qb + 1) * 512],
                            start=False, stop=True))
                    dep = p.mark("ten")
                    if kt >= 4 * qb:
                        i = kt - 4 * qb
                        p.wait("vec", dep)
                        p.add("vec", lambda v, ps=ps, i=i: v.tensor_tensor(
                            ps, ps, self.cmask_sb[:, 384 - 128 * i:384 - 128 * i + 512],
                            Alu.add))
                        dep = p.mark("vec")
                    pt = ptiles[kt % 2]
                    p.waits("scl", [dep, negb_dep, self.sfree.get(f"pt{kt % 2}")])
                    p.add("scl", lambda s, pt=pt, ps=ps: s.activation(pt[:], ps, Act.Exp,
                                                                      bias=negb))
                    em = p.mark("scl")
                    exp_marks[kt] = em
                    self.pfree[f"A{bank}"] = em
                if kt >= 1:
                    k2 = kt - 1
                    p.waits("ten", [exp_marks[k2]] + ([self.pfree["C"]] if k2 == 0 else []))
                    p.add("ten", lambda tn, k2=k2, hh=hh, nkt=nkt, vtav=vtav: tn.matmul(
                        self.PS_C[0:65, 0:512], vtav[:, k2, hh * 65:hh * 65 + 65],
                        ptiles[k2 % 2][:], start=(k2 == 0), stop=(k2 == nkt - 1)))
                    pvd = p.mark("ten")
                    self.sfree[f"pt{k2 % 2}"] = pvd
            p.wait("vec", pvd)
            p.add("vec", lambda v: v.reciprocal(self.lrow64[:, 0:512], self.PS_C[64:65, 0:512]))
            d = p.mark("vec")
            p.waits("ten", [d, self.pfree["D"]])
            p.add("ten", lambda tn: tn.matmul(
                self.PS_D[0:64, 0:512], self.ones128_sb[64:65, 0:64], self.lrow64[:, 0:512],
                start=True, stop=True))
            d2 = p.mark("ten")
            p.waits("scl", [pvd, self.sfree.get("g32")])
            p.add("scl", lambda s: s.copy(self.g32[0:64, 0:512], self.PS_C[0:64, 0:512]))
            dc = p.mark("scl")
            self.pfree["C"] = dc
            p.waits("vec", [d2, dc, self.sfree.get("og")])
            if hh == 0:
                p.add("vec", lambda v, qb=qb: v.tensor_tensor(
                    self.og32[0:64, qb * 512:(qb + 1) * 512], self.g32[0:64, 0:512],
                    self.PS_D[0:64, 0:512], Alu.mult))
                d = p.mark("vec")
                self.sfree["g32"] = d
            else:
                p.add("vec", lambda v: v.tensor_tensor(
                    self.g32[0:64, 0:512], self.g32[0:64, 0:512],
                    self.PS_D[0:64, 0:512], Alu.mult))
                d = p.mark("vec")
                p.waits("syn", [d, self.sfree.get("og")])
                dd = p.dma("syn", out=self.og32[64:128, qb * 512:(qb + 1) * 512],
                           in_=self.g32[0:64, 0:512], q="DAU")
                self._og_dmas.append(dd)
                self.sfree["g32"] = dd
            self.pfree["D"] = d
        self.sfree["negb"] = p.mark("scl")
        self._attn_done = pvd
        return pvd

    def load_weight_rows(self, dram_rows, ncols):
        p = self.p
        p.waits("syn", [self.sfree.get("wld")])
        p.waits("gps", [self.sfree.get("wld")])
        d1 = p.dma("syn", out=self.wld[:, 0:ncols], in_=dram_rows, q="DWT")
        d2 = p.dma("gps", out=self.wld_h[:, 0:ncols], in_=dram_rows, q="DGH")
        p.waits("vec", [d1, d2])
        p.add("vec", lambda v: v.tensor_tensor(
            self.wld_l[:, 0:ncols], self.wld[:, 0:ncols], self.wld_h[:, 0:ncols], Alu.subtract))
        return p.mark("vec")

    def load_weight_cols(self, dram, rows, c0, ncols_src):
        p = self.p
        p.waits("syn", [self.sfree.get("wld")])
        p.waits("gps", [self.sfree.get("wld")])
        nrc = rows // 128
        d1 = d2 = None
        for r in range(nrc):
            src = dram[r * 128:(r + 1) * 128, c0:c0 + ncols_src]
            d1 = p.dma("syn", out=self.wld[:, r * ncols_src:(r + 1) * ncols_src], in_=src, q="DWT")
            d2 = p.dma("gps", out=self.wld_h[:, r * ncols_src:(r + 1) * ncols_src], in_=src, q="DGH")
        p.waits("vec", [d1, d2])
        n = nrc * ncols_src
        p.add("vec", lambda v, n=n: v.tensor_tensor(
            self.wld_l[:, 0:n], self.wld[:, 0:n], self.wld_h[:, 0:n], Alu.subtract))
        return p.mark("vec")

    def transpose_weight(self, dst_off, ncols, dep, extra=()):
        p = self.p
        p.wait("ten", dep)
        p.waits("ten", list(extra))
        p.waits("vec", list(extra))
        p.waits("ten", [self.pfree["T"]])
        nch = ncols // 128
        assert nch <= 8
        for c in range(nch):
            p.add("ten", lambda tn, c=c: tn.transpose(
                self.T_PS[:, c * 128:(c + 1) * 128], self.wld_h[:, c * 128:(c + 1) * 128], self.ident_sb[:]))
        for c in range(nch):
            p.add("ten", lambda tn, c=c: tn.transpose(
                self.T_PS[:, 1024 + c * 128:1024 + (c + 1) * 128],
                self.wld_l[:, c * 128:(c + 1) * 128], self.ident_sb[:]))
        dT = p.mark("ten")
        self.sfree["wld"] = dT
        p.wait("vec", dT)
        p.add("vec", lambda v: v.tensor_copy(self.wT_h[:, dst_off:dst_off + ncols], self.T_PS[:, 0:ncols]))
        p.add("vec", lambda v: v.tensor_copy(self.wT_l[:, dst_off:dst_off + ncols],
                                             self.T_PS[:, 1024:1024 + ncols]))
        d = p.mark("vec")
        self.pfree["T"] = d
        return d

    def mm3(self, bank_ap, lhsT_h, lhsT_l, rhs_h, rhs_l, first, last, passes=None):
        p = self.p
        if passes is None:
            passes = [(lhsT_h, rhs_h), (lhsT_l, rhs_h), (lhsT_h, rhs_l)]
        for i, (a_, b_) in enumerate(passes):
            st = first and i == 0
            sp = last and i == len(passes) - 1
            p.add("ten", lambda tn, a_=a_, b_=b_, st=st, sp=sp: tn.matmul(
                bank_ap, a_, b_, start=st, stop=sp))

    def proj_fm(self, woff, dst_h, dst_l, scale, bias_col, extra=None):
        p = self.p
        deps = []
        for tc in range(4):
            bank = tc % 4
            p.waits("ten", [self.pfree[f"A{bank}"]])
            ps = self.PS_A[:, bank * 512:(bank + 1) * 512]
            for c in range(ND):
                self.mm3(ps, self.wT_h[:, woff + c * 128:woff + (c + 1) * 128],
                         self.wT_l[:, woff + c * 128:woff + (c + 1) * 128],
                         self.xh_v[:, c, tc * 512:(tc + 1) * 512],
                         self.xl_v[:, c, tc * 512:(tc + 1) * 512],
                         first=(c == 0), last=(c == ND - 1))
            d = p.mark("ten")
            p.wait("scl", d)
            p.waits("scl", [self.sfree.get("g32")])
            p.add("scl", lambda s, ps=ps: s.activation(
                self.g32[:], ps, Act.Identity, scale=scale, bias=bias_col))
            d = p.mark("scl")
            self.pfree[f"A{bank}"] = d
            p.wait("vec", d)
            p.add("vec", lambda v, tc=tc: v.tensor_copy(dst_h[:, tc * 512:(tc + 1) * 512], self.g32[:]))
            p.add("vec", lambda v, tc=tc: v.tensor_tensor(
                dst_l[:, tc * 512:(tc + 1) * 512], self.g32[:], dst_h[:, tc * 512:(tc + 1) * 512],
                Alu.subtract))
            d = p.mark("vec")
            self.sfree["g32"] = d
            deps.append(d)
        return deps[-1]

    def l0_attn(self):
        p = self.p
        W = self.W
        sm = self.smalls
        p.dma("syn", out=self.gb_fm[:, 0:ND],
              in_=W["L0_attn_gamma"].rearrange("(c p) -> p c", p=128), slow=True, q="DRR")
        p.dma("syn", out=self.gb_fm[:, ND:2 * ND],
              in_=W["L0_attn_beta"].rearrange("(c p) -> p c", p=128), slow=True, q="DRR")
        bd = None
        for i, nm in enumerate(("bq", "bk", "bv", "bo")):
            bd = p.dma("syn", out=self.bias_fm[:, i * ND:(i + 1) * ND],
                       in_=W[f"L0_attn_{nm}"].rearrange("(c p) -> p c", p=128), slow=True, q="DRR")
        p.wait("scl", bd)
        p.wait("vec", bd)
        p.add("vec", lambda v: v.tensor_scalar_mul(self.bias_fm[:, 0:ND], self.bias_fm[:, 0:ND], 0.125))
        bqs_dep = p.mark("vec")
        p.wait("scl", bqs_dep)
        bo_dep = p.dma("syn", out=self.brow32[0:1, :],
                       in_=W["L0_attn_bo"].rearrange("d -> () d"), q="DGR")
        mq_dep = p.dma("syn", out=self.mqt32[:, :], in_=self.mqt32_c.ap(), q="DB4")
        p.add("vec", lambda v: v.memset(
            self.vta.rearrange("p (t h c) -> p t h c", h=2, c=65)[:, :, :, 64:65], 1.0))
        vta1 = p.mark("vec")
        rb_d = p.dma("syn", out=self.g32[0:32, 0:512].rearrange("p (h j) -> p h j", h=NH),
                     in_=W["L0_attn_rel_bias"].rearrange("h i j -> i h j"), q="DB4")
        p.waits("vec", [rb_d])
        for h in range(NH):
            p.add("vec", lambda v, h=h: v.tensor_reduce(
                sm[0:32, 4 + h:5 + h], self.g32[0:32, h * 32:(h + 1) * 32], axis=AX.X,
                op=Alu.max, apply_absolute_value=True))
        d = p.mark("vec")
        p.waits("ten", [d, self.pfree["C"]])
        p.add("ten", lambda tn: tn.transpose(self.PS_C[0:16, 0:32], sm[0:32, 4:20],
                                             self.identf_sb[0:32, 0:32]))
        d = p.mark("ten")
        p.wait("vec", d)
        p.add("vec", lambda v: v.tensor_reduce(sm[0:16, 4:5], self.PS_C[0:16, 0:32],
                                               axis=AX.X, op=Alu.max))
        d = p.mark("vec")
        self.pfree["C"] = d
        p.waits("ten", [d, self.pfree["D"]])
        p.add("ten", lambda tn: tn.transpose(self.PS_D[0:1, 0:16], sm[0:16, 4:5],
                                             self.identf_sb[0:16, 0:16]))
        d = p.mark("ten")
        p.wait("vec", d)
        p.add("vec", lambda v: v.tensor_copy(sm[0:1, 16:32], self.PS_D[0:1, 0:16]))
        rmax_dep = p.mark("vec")
        self.pfree["D"] = rmax_dep
        self.sfree["g32"] = rmax_dep

        self.stats_pass()
        self.ln_scalars("lnln")
        xn_dep = self.transpose_x32(self.xnT32_scr, gfm=self.gb_fm[:, 0:ND],
                                    bfm=self.gb_fm[:, ND:2 * ND])
        x32a = self.xsl32a.rearrange("p (c s) -> p c s", c=ND)

        rb_dep = None
        for g in range(8):
            if g % 2 == 0:
                p.waits("syn", [self.sfree.get("rb_st")])
                rb_dep = p.dma("syn", out=self.rb_st[:].rearrange("p (a j) -> p a j", a=4),
                               in_=W["L0_attn_rel_bias"].rearrange("h i j -> j h i")[
                                   :, (g // 2) * 4:(g // 2) * 4 + 4, :], q="DB4", slow=True)
            free = self.sfree.get("qkv_free")
            wq_dep = self.transpose_weight32(
                0, self.load_w_rows32(W["L0_attn_Wq"][g * 128:(g + 1) * 128, :], 1024), (free,))
            wk_dep = self.transpose_weight32(
                1, self.load_w_rows32(W["L0_attn_Wk"][g * 128:(g + 1) * 128, :], 1024), (free,))
            wv_dep = self.transpose_weight32(
                2, self.load_w_rows32(W["L0_attn_Wv"][g * 128:(g + 1) * 128, :], 1024), (free,))

            qmm_last = None
            for tc in range(4):
                slab_dep = self.load_xslab32a(tc)
                p.waits("ten", [slab_dep, wq_dep, wk_dep, wv_dep, xn_dep])
                for wi in range(3):
                    bank = wi
                    p.waits("ten", [self.pfree[f"A{bank}"]])
                    ps = self.PS_A[:, bank * 512:(bank + 1) * 512]
                    for c in range(ND):
                        p.add("ten", lambda tn, ps=ps, wi=wi, c=c, tc=tc: tn.matmul(
                            ps, self.wT32[:, wi * 1024 + c * 128:wi * 1024 + (c + 1) * 128],
                            x32a[:, c, :], start=(c == 0), stop=(c == ND - 1)))
                    d = p.mark("ten")
                    if wi == 0:
                        qmm_last = d
                    if wi == 2:
                        self.sfree["xsl32"] = d
                    p.wait("scl", d)
                    p.waits("scl", [self.sfree.get("qkv_free")])
                    scale = 0.125 if wi == 0 else 1.0
                    bcol = self.bias_fm[:, wi * ND + g:wi * ND + g + 1]
                    if wi < 2:
                        dst = self.qt32 if wi == 0 else self.kt32
                        p.add("scl", lambda s, ps=ps, scale=scale, bcol=bcol, dst=dst, tc=tc:
                              s.activation(dst[:, tc * 512:(tc + 1) * 512], ps, Act.Identity,
                                           scale=scale, bias=bcol))
                        d = p.mark("scl")
                        self.pfree[f"A{bank}"] = d
                    else:
                        p.waits("scl", [self.sfree.get("g32")])
                        p.add("scl", lambda s, ps=ps, bcol=bcol: s.activation(
                            self.g32[:], ps, Act.Identity, bias=bcol))
                        d = p.mark("scl")
                        self.pfree[f"A{bank}"] = d
                        self.v_transpose_to_vta(tc, d, vta1)
            qkv_dep = p.mark("vec")
            qk_dep = p.mark("scl")
            wo_ld = self.load_w_cols32(W["L0_attn_Wo"], D, g * 128, 128)

            att_d = None
            for hh in range(2):
                h = 2 * g + hh
                hq = h % 4
                p.waits("ten", [rb_dep, mq_dep, self.pfree["D"], self.sfree.get("w2t")])
                for tc in range(4):
                    p.waits("ten", [self.pfree["D"]])
                    p.add("ten", lambda tn, tc=tc, hq=hq: tn.matmul(
                        self.PS_D[0:32, 0:512], self.rb_st[:, hq * 32:(hq + 1) * 32],
                        self.mqt32[:, tc * 512:(tc + 1) * 512], start=True, stop=True))
                    d = p.mark("ten")
                    p.wait("vec", d)
                    p.add("vec", lambda v, tc=tc: v.tensor_copy(
                        self.w2t[:, tc * 512:(tc + 1) * 512], self.PS_D[0:32, 0:512]))
                    d = p.mark("vec")
                    self.pfree["D"] = d
                w2t_dep = d
                nb = self.negb_head(hh, rmax_cell=sm[0:1, 16 + h:17 + h],
                                    qkv_dep=(qk_dep if hh == 0 else None))
                p.waits("ten", [qkv_dep, qk_dep, w2t_dep, rmax_dep, nb])
                att_d = self.attn_head32(hh, True, qkv_dep, nb)
                self.sfree["w2t"] = att_d
            self.sfree["rb_st"] = att_d
            self.sfree["qkv_free"] = att_d

            wo_dep = self.transpose_weight32(0, wo_ld, (qmm_last,))
            p.waits("ten", [att_d, wo_dep, bo_dep] + self._og_dmas)
            for tq in range(NT):
                for oc in range(2):
                    bank = (2 * tq + oc) % 2
                    p.waits("ten", [self.pfree[f"A{bank}"]])
                    ps = self.PS_A[:, bank * 512:(bank + 1) * 512]
                    p.add("ten", lambda tn, ps=ps, tq=tq, oc=oc, g=g: tn.matmul(
                        ps, self.og32[:, tq * 128:(tq + 1) * 128],
                        self.wT32[:, oc * 512:(oc + 1) * 512],
                        start=True, stop=(g != 0)))
                    if g == 0:
                        p.add("ten", lambda tn, ps=ps, oc=oc: tn.matmul(
                            ps, self.ones128_sb[64:65, 0:128],
                            self.brow32[0:1, oc * 512:(oc + 1) * 512],
                            start=False, stop=True))
                    d = p.mark("ten")
                    p.wait("vec", d)
                    xsl = self.x_sb[:, tq * D + oc * 512: tq * D + (oc + 1) * 512]
                    p.add("vec", lambda v, xsl=xsl, ps=ps: v.tensor_tensor(xsl, xsl, ps, Alu.add))
                    self.pfree[f"A{bank}"] = p.mark("vec")
            self.sfree["og"] = p.mark("ten")

    def v_transpose_to_vta(self, tc, g32_dep, vta1):
        p = self.p
        TPSF = self.T_PS[:].bitcast(F32)
        vtav = self.vta.rearrange("p (t k) -> p t k", k=130)
        p.wait("ten", g32_dep)
        p.waits("ten", [self.pfree["T"]])
        for i in range(4):
            p.add("ten", lambda tn, i=i: tn.transpose(
                TPSF[:, i * 128:(i + 1) * 128], self.g32[:, i * 128:(i + 1) * 128],
                self.identf_sb[:]))
        dT = p.mark("ten")
        self.sfree["g32"] = dT
        p.wait("vec", dT)
        p.waits("vec", [vta1, self.sfree.get("qkv_free")])
        for i in range(4):
            kt = tc * 4 + i
            for hh in range(2):
                p.add("vec", lambda v, kt=kt, hh=hh, i=i, vtav=vtav: v.tensor_copy(
                    vtav[:, kt, hh * 65:hh * 65 + 64],
                    TPSF[:, i * 128 + hh * 64:i * 128 + hh * 64 + 64]))
        d = p.mark("vec")
        self.pfree["T"] = d
        return d

    def _ternarize_stream(self, dram, nrows, ncols, tcol_neg, tcol, scr, scr_map):
        p = self.p
        for r in range(nrows // 128):
            for q in range(ncols // 1024):
                sl = dram[r * 128:(r + 1) * 128, q * 1024:(q + 1) * 1024]
                p.waits("syn", [self.sfree.get("wld")])
                d1 = p.dma("syn", out=self.wld[:, 0:1024], in_=sl, q="DWT")
                p.waits("scl", [d1, self.sfree.get("u32")])
                p.add("scl", lambda s: s.activation(self.u32[:], self.wld[:, 0:1024], Act.Abs))
                d = p.mark("scl")
                p.wait("vec", d)
                p.waits("vec", [self.sfree.get("st")])
                p.add("vec", lambda v: v.tensor_scalar(
                    self.sth[:, 0:1024], self.u32[:], tcol, None, Alu.is_gt))
                p.add("vec", lambda v: v.tensor_tensor(
                    self.u32[:], self.wld[:, 0:1024], self.sth[:, 0:1024], Alu.mult))
                d = p.mark("vec")
                self.sfree["wld"] = d
                p.wait("scl", d)
                p.add("scl", lambda s: s.activation(self.wld_h[:, 0:1024], self.u32[:], Act.Sign))
                d = p.mark("scl")
                self.sfree["u32"] = d
                p.wait("ten", d)
                p.waits("ten", [self.pfree["T"]])
                for c in range(8):
                    p.add("ten", lambda tn, c=c: tn.transpose(
                        self.T_PS[:, c * 128:(c + 1) * 128], self.wld_h[:, c * 128:(c + 1) * 128],
                        self.ident_sb[:]))
                dT = p.mark("ten")
                p.wait("vec", dT)
                p.add("vec", lambda v: v.tensor_copy(self.wld_l[:, 0:1024], self.T_PS[:, 0:1024]))
                d = p.mark("vec")
                self.pfree["T"] = d
                p.waits("syn", [d])
                out_ap, in_ap = scr_map(r, q)
                sd = p.dma("syn", out=out_ap, in_=in_ap, q="DSS")
                self.sfree["wld"] = sd

    def l0_ffn(self):
        p = self.p
        W = self.W
        hb_h = self._av(0, 32768)
        hb_l = self._av(32768, 32768)
        hsq = self._av(65536, 2048, F32)
        srow = self._av(67584 + 4096, 2048, F32, parts=1)
        s2col = self._av(73728, 64, F32)
        acell = self._av(73856, 64, F32)
        tcol_up = self._av(74240, 4, F32)
        tcol_dn = self._av(74368, 4, F32)

        self.stats_pass()
        self.ln_scalars("ln_rms")

        def alpha_pass(dram, nchunks, tcol):
            acc = acell[:, 0:1]
            tmp = acell[:, 1:2]
            first = True
            dview = dram.rearrange("(r p) c -> p r c", p=128)
            ncq = dview.shape[2] // 1024
            for i in range(nchunks):
                r, q = i // ncq, i % ncq
                p.waits("syn", [self.sfree.get("wld")])
                d1 = p.dma("syn", out=self.wld[:, 0:1024],
                           in_=dview[:, r, q * 1024:(q + 1) * 1024],
                           q="DWT")
                p.waits("vec", [d1])
                p.add("vec", lambda v, first=first: v.tensor_reduce(
                    tmp if not first else acc, self.wld[:, 0:1024], axis=AX.X, op=Alu.add,
                    apply_absolute_value=True))
                if not first:
                    p.add("vec", lambda v: v.tensor_tensor(acc, acc, tmp, Alu.add))
                d = p.mark("vec")
                self.sfree["wld"] = d
                first = False
            p.waits("ten", [d, self.pfree["C"]])
            p.add("ten", lambda tn: tn.matmul(self.PS_C[0:1, 0:1], acc, self.onescol_sb[:, :],
                                              start=True, stop=True))
            d = p.mark("ten")
            p.wait("vec", d)
            p.add("vec", lambda v: v.tensor_scalar_mul(acell[0:1, 2:3], self.PS_C[0:1, 0:1],
                                                       0.5 / 4194304.0))
            d = p.mark("vec")
            self.pfree["C"] = d
            p.waits("ten", [d, self.pfree["C"]])
            p.add("ten", lambda tn: tn.matmul(self.PS_C[:, 0:1], self.ones_sb[:, :],
                                              acell[0:1, 2:3], start=True, stop=True))
            d = p.mark("ten")
            p.wait("vec", d)
            p.add("vec", lambda v, tcol=tcol: v.tensor_copy(tcol, self.PS_C[:, 0:1]))
            d = p.mark("vec")
            self.pfree["C"] = d
            return d

        dup = alpha_pass(W["L0_ffn_w_up"], 32, tcol_up)
        ddn = alpha_pass(W["L0_ffn_w_down"], 32, tcol_dn)

        wup_v = self.wupT_scr
        self._ternarize_stream(
            W["L0_ffn_w_up"], 4096, 1024, None, tcol_up, self.wupT_scr,
            lambda r, q: (self.wupT_scr[:, r * 1024:(r + 1) * 1024], self.wld_l[:, 0:1024]))
        dn_view = self.wdownT_scr.rearrange("p (fc rr k) -> p fc rr k", fc=32, rr=8)
        self._ternarize_stream(
            W["L0_ffn_w_down"], 1024, 4096, None, tcol_dn, self.wdownT_scr,
            lambda r, q: (dn_view[:, 8 * q:8 * q + 8, r, :],
                          self.wld_l[:, 0:1024].rearrange("p (j k) -> p j k", j=8)))

        xb_h = self.xsl_h[:].rearrange("p (c s) -> p c s", c=ND)
        xb_l = self.xsl_l[:].rearrange("p (c s) -> p c s", c=ND)
        self.sfree["wsl0"] = self.sfree["wsl1"] = self.sfree.get("wld")
        for b in range(4):
            xd = self.split_transpose_x(self.xsl_h[:], self.xsl_l[:], gamma=False,
                                        tiles=range(4 * b, 4 * b + 4), span=512, to_sbuf=True)
            p.waits("ten", [xd])
            stats_started = False
            for fc in range(32):
                wbuf = self.wld_h if fc % 2 == 0 else self.wld_l
                p.waits("syn", [self.sfree.get(f"wsl{fc % 2}")])
                wd = p.dma("syn", out=wbuf[:, 0:1024],
                           in_=self.wupT_scr[:, fc * 1024:(fc + 1) * 1024],
                           q="DSL" if fc % 2 == 0 else "DS2")
                bank = fc % 4
                p.waits("ten", [wd, self.pfree[f"A{bank}"]])
                ps = self.PS_A[:, bank * 512:(bank + 1) * 512]
                for c in range(ND):
                    self.mm3(ps, None, None, None, None, first=(c == 0), last=(c == ND - 1),
                             passes=[(wbuf[:, c * 128:(c + 1) * 128], xb_h[:, c, :]),
                                     (wbuf[:, c * 128:(c + 1) * 128], xb_l[:, c, :])])
                d = p.mark("ten")
                self.sfree[f"wsl{fc % 2}"] = d
                p.wait("scl", d)
                p.waits("scl", [self.sfree.get("g32")])
                p.add("scl", lambda s, ps=ps: s.activation(self.g32[:], ps, Act.Gelu))
                d = p.mark("scl")
                self.pfree[f"A{bank}"] = d
                p.wait("vec", d)
                p.add("vec", lambda v, fc=fc: v.tensor_copy(hb_h[:, fc * 512:(fc + 1) * 512], self.g32[:]))
                p.add("vec", lambda v, fc=fc: v.tensor_tensor(
                    hb_l[:, fc * 512:(fc + 1) * 512], self.g32[:], hb_h[:, fc * 512:(fc + 1) * 512],
                    Alu.subtract))
                p.add("vec", lambda v: v.tensor_tensor(hsq[:, :], self.g32[:], self.g32[:], Alu.mult))
                d = p.mark("vec")
                self.sfree["g32"] = d
                p.waits("ten", [d] + ([self.pfree["D"]] if not stats_started else []))
                p.add("ten", lambda tn, fc=fc: tn.matmul(
                    self.PS_D[0:1, 0:512], self.onescol_sb[:, :],
                    hsq[:, :], start=(fc == 0), stop=(fc == 31)))
                stats_started = True
                p.waits("vec", [p.mark("ten")])
            d = p.mark("ten")
            p.wait("vec", d)
            p.add("vec", lambda v: v.tensor_copy(srow[:, :], self.PS_D[0:1, 0:512]))
            d = p.mark("vec")
            self.pfree["D"] = d
            p.waits("ten", [d, self.pfree["C"]])
            for tt in range(4):
                p.add("ten", lambda tn, tt=tt: tn.transpose(
                    self.PS_C[:, tt:tt + 1], srow[0:1, tt * 128:(tt + 1) * 128],
                    self.identf_sb[0:1, 0:1]))
            d = p.mark("ten")
            p.wait("vec", d)
            p.add("vec", lambda v: v.tensor_copy(s2col[:, 0:4], self.PS_C[:, 0:4]))
            d = p.mark("vec")
            self.pfree["C"] = d
            p.wait("scl", d)
            p.add("scl", lambda s: s.activation(s2col[:, 0:4], s2col[:, 0:4], Act.Sqrt))
            d = p.mark("scl")
            p.wait("vec", d)
            p.add("vec", lambda v: v.tensor_scalar(s2col[:, 0:4], s2col[:, 0:4], 1.0 / 64.0, 1e-8,
                                                   Alu.mult, Alu.add))
            p.add("vec", lambda v: v.reciprocal(s2col[:, 0:4], s2col[:, 0:4]))
            s2dep = p.mark("vec")

            banks = [self.PS_A[:, i * 512:(i + 1) * 512] for i in range(4)] +                     [self.T_PS[:].bitcast(F32)[:, 0:512], self.T_PS[:].bitcast(F32)[:, 512:1024],
                     self.PS_C[:, 0:512], self.PS_D[:, 0:512]]
            p.waits("ten", [self.pfree["A0"], self.pfree["A1"], self.pfree["A2"], self.pfree["A3"],
                            self.pfree["T"], self.pfree["C"], self.pfree["D"]])
            for fc in range(32):
                wbuf = self.wld_h if fc % 2 == 0 else self.wld_l
                p.waits("syn", [self.sfree.get(f"wsl{fc % 2}")])
                wd = p.dma("syn", out=wbuf[:, 0:1024],
                           in_=self.wdownT_scr[:, fc * 1024:(fc + 1) * 1024],
                           q="DSL" if fc % 2 == 0 else "DS2")
                p.waits("ten", [wd])
                for tt in range(4):
                    hh = hb_h[:, fc * 512 + tt * 128:fc * 512 + (tt + 1) * 128]
                    hl = hb_l[:, fc * 512 + tt * 128:fc * 512 + (tt + 1) * 128]
                    for oc in range(2):
                        self.mm3(banks[tt * 2 + oc], None, None, None, None,
                                 first=(fc == 0), last=(fc == 31),
                                 passes=[(hh, wbuf[:, oc * 512:(oc + 1) * 512]),
                                         (hl, wbuf[:, oc * 512:(oc + 1) * 512])])
                d = p.mark("ten")
                self.sfree[f"wsl{fc % 2}"] = d
            d = p.mark("ten")
            p.wait("vec", d)
            p.waits("vec", [s2dep])
            for tt in range(4):
                t_g = 4 * b + tt
                for oc in range(2):
                    xsl_ = self.x_sb[:, t_g * D + oc * 512:t_g * D + (oc + 1) * 512]
                    p.add("vec", lambda v, xsl_=xsl_, tt=tt, oc=oc: v.scalar_tensor_tensor(
                        xsl_, banks[tt * 2 + oc], s2col[:, tt:tt + 1], xsl_, Alu.mult, Alu.add))
            d = p.mark("vec")
            for k in ("A0", "A1", "A2", "A3", "T", "C", "D"):
                self.pfree[k] = d


    def l1_attn(self):
        p = self.p
        W = self.W
        I = 2048
        NC = 16
        BT = 512
        h_in_h = self._av(0, 16512)
        h_in_l = self._av(16512, 16512)
        u_cm = self._av(0, 32768, F32)
        h2_h = self._av(0, 16384)
        h2_l = self._av(16384, 16384)
        h32 = self._av(33024, 32768, F32)
        gate32 = self._av(65792, 2048, F32)
        m1_h = self._av(67840, 8192)
        m1_l = self._av(76032, 8192)
        murep = self._av(67840, 2048, F32)
        rrep = self._av(69888, 2048, F32)
        srow2 = self._av(65792, 2048, F32, parts=1)
        srow2b = self._av(71936, 2048, F32, parts=1)
        halo_h = self._av(84224, 96)
        halo_l = self._av(84320, 96)

        hiv = lambda buf: buf.rearrange("p (c s) -> p c s", c=NC)
        h32v = h32.rearrange("p (c s) -> p c s", c=NC)
        xb_h = self.xsl_h[:].rearrange("p (c s) -> p c s", c=ND)
        xb_l = self.xsl_l[:].rearrange("p (c s) -> p c s", c=ND)

        for off, nm in ((0, "ln_g"), (16, "ln_b"), (32, "conv_b"), (48, "cproj_b")):
            bd = p.dma("syn", out=self.bias2[:, off:off + 16],
                       in_=W[f"L1_attn_{nm}"].rearrange("(c p) -> p c", p=128), slow=True, q="DRR")
        bd = p.dma("syn", out=self.bias2[:, 64:72],
                   in_=W["L1_attn_fc_b"].rearrange("(c p) -> p c", p=128), slow=True, q="DRR")
        for eng in ("vec", "scl"):
            p.wait(eng, bd)

        self.dbg_dump("xl1a", self.x_sb[:])
        vd_, sd_ = self.stats_pass()
        self.dbg_dump("sq1a", self.sq[:], sd_)
        self.dbg_dump("sm1a", self.sm[:], vd_)
        d_ = self.ln_scalars("ln")
        self.dbg_dump("rho1a", self.rho[:], d_)

        p.add("vec", lambda v: v.memset(hiv(h_in_h)[:, :, 0:4], 0.0))
        p.add("vec", lambda v: v.memset(hiv(h_in_l)[:, :, 0:4], 0.0))

        for b in range(4):
            xd = self.split_transpose_x(self.xsl_h[:], self.xsl_l[:], gamma=False,
                                        tiles=range(4 * b, 4 * b + 4), span=BT, to_sbuf=True)
            if b == 3:
                self.dbg_dump("xsl_h", self.xsl_h[:], xd)
                self.dbg_dump("xsl_l", self.xsl_l[:], xd)
            if b > 0:
                p.add("vec", lambda v: v.tensor_copy(hiv(h_in_h)[:, :, 1:4], halo_h[:].rearrange("p (c k) -> p c k", c=NC)))
                p.add("vec", lambda v: v.tensor_copy(hiv(h_in_l)[:, :, 1:4], halo_l[:].rearrange("p (c k) -> p c k", c=NC)))
            p.waits("ten", [xd])

            for c in range(NC):
                slot = c % 2
                free = self.sfree.get("wT")
                dd = self.load_weight_rows(W["L1_attn_in_proj_w"][c * 128:(c + 1) * 128, :], 1024)
                wdep = self.transpose_weight(slot * 1024, 1024, dd, extra=(free,))
                bank = c % 4
                p.waits("ten", [wdep, self.pfree[f"A{bank}"]])
                ps = self.PS_A[:, bank * 512:(bank + 1) * 512]
                for dc in range(ND):
                    self.mm3(ps, self.wT_h[:, slot * 1024 + dc * 128:slot * 1024 + (dc + 1) * 128],
                             self.wT_l[:, slot * 1024 + dc * 128:slot * 1024 + (dc + 1) * 128],
                             xb_h[:, dc, :], xb_l[:, dc, :], first=(dc == 0), last=(dc == ND - 1))
                d = p.mark("ten")
                self.sfree["wT"] = d
                p.wait("scl", d)
                p.waits("scl", [self.sfree.get("g32")])
                p.add("scl", lambda s, ps=ps: s.activation(self.g32[:], ps, Act.Identity))
                d = p.mark("scl")
                self.pfree[f"A{bank}"] = d
                p.wait("vec", d)
                p.waits("vec", [self.sfree.get("hreg")])
                p.add("vec", lambda v, c=c: v.tensor_copy(hiv(h_in_h)[:, c, 4:516], self.g32[:]))
                p.add("vec", lambda v, c=c: v.tensor_tensor(
                    hiv(h_in_l)[:, c, 4:516], self.g32[:], hiv(h_in_h)[:, c, 4:516], Alu.subtract))
                d = p.mark("vec")
                self.sfree["g32"] = d
            hin_dep = p.mark("vec")
            if b == 3:
                self.dbg_dump("hin_h", h_in_h, hin_dep)
                self.dbg_dump("hin_l", h_in_l, hin_dep)

            for g in range(8):
                free = self.sfree.get("wT")
                for rr in range(2):
                    dd = self.load_weight_rows(W["L1_attn_conv_w"].rearrange("o c k -> o (c k)")[
                        g * 256 + rr * 128:g * 256 + rr * 128 + 128, :], 1024)
                    p.wait("ten", dd)
                    p.waits("ten", [free, self.pfree["T"]])
                    p.waits("vec", [free])
                    for ci in range(2):
                        for k in range(4):
                            src_h = self.wld_h[:, 0:1024].rearrange("p (ci2 k2) -> p k2 ci2", k2=4)[:, k, ci * 128:(ci + 1) * 128]
                            src_l = self.wld_l[:, 0:1024].rearrange("p (ci2 k2) -> p k2 ci2", k2=4)[:, k, ci * 128:(ci + 1) * 128]
                            ti = (ci * 4 + k)
                            p.add("ten", lambda tn, src_h=src_h, ti=ti, rr=rr: tn.transpose(
                                self.T_PS[:, ti * 128:(ti + 1) * 128], src_h, self.ident_sb[:]))
                            p.add("ten", lambda tn, src_l=src_l, ti=ti, rr=rr: tn.transpose(
                                self.T_PS[:, 1024 + ti * 128:1024 + (ti + 1) * 128], src_l, self.ident_sb[:]))
                    dT = p.mark("ten")
                    self.sfree["wld"] = dT
                    p.wait("vec", dT)
                    p.add("vec", lambda v, rr=rr: v.tensor_copy(
                        self.wT_h[:, rr * 1024:(rr + 1) * 1024], self.T_PS[:, 0:1024]))
                    p.add("vec", lambda v, rr=rr: v.tensor_copy(
                        self.wT_l[:, rr * 1024:(rr + 1) * 1024], self.T_PS[:, 1024:2048]))
                    d = p.mark("vec")
                    self.pfree["T"] = d
                conv_w_dep = p.mark("vec")
                for co in range(2):
                    cog = 2 * g + co
                    free = self.sfree.get("wT2")
                    dd = self.load_weight_rows(
                        W["L1_attn_in_proj_w"][I + cog * 128:I + (cog + 1) * 128, :], 1024)
                    gdep = self.transpose_weight(2 * 1024, 1024, dd, extra=(free,))
                    p.waits("ten", [gdep, self.pfree["C"]])
                    for dc in range(ND):
                        self.mm3(self.PS_C[:, 0:512],
                                 self.wT_h[:, 2 * 1024 + dc * 128:2 * 1024 + (dc + 1) * 128],
                                 self.wT_l[:, 2 * 1024 + dc * 128:2 * 1024 + (dc + 1) * 128],
                                 xb_h[:, dc, :], xb_l[:, dc, :], first=(dc == 0), last=(dc == ND - 1))
                    d = p.mark("ten")
                    self.sfree["wT2"] = d
                    p.wait("scl", d)
                    p.waits("scl", [self.sfree.get("gate32")])
                    p.add("scl", lambda s: s.activation(gate32[:, :], self.PS_C[:, 0:512], Act.Silu))
                    sil_dep = p.mark("scl")
                    self.pfree["C"] = sil_dep
                    bank = co % 2
                    p.waits("ten", [conv_w_dep, hin_dep, self.pfree[f"A{bank}"]])
                    ps = self.PS_A[:, bank * 512:(bank + 1) * 512]
                    first = True
                    for ci in range(2):
                        cig = 2 * g + ci
                        for k in range(4):
                            ti = (co * 8 + ci * 4 + k) * 128
                            rh = hiv(h_in_h)[:, cig, 1 + k:1 + k + 512]
                            rl = hiv(h_in_l)[:, cig, 1 + k:1 + k + 512]
                            self.mm3(ps, self.wT_h[:, ti:ti + 128], self.wT_l[:, ti:ti + 128],
                                     rh, rl, first=first, last=(ci == 1 and k == 3))
                            first = False
                    d = p.mark("ten")
                    p.waits("vec", [d, sil_dep])
                    p.add("vec", lambda v, cog=cog, ps=ps: v.scalar_tensor_tensor(
                        h32v[:, cog, :], ps, self.bias2[:, 32 + cog:33 + cog],
                        gate32[:, :], Alu.add, Alu.mult))
                    d = p.mark("vec")
                    self.pfree[f"A{bank}"] = d
                    self.sfree["gate32"] = d
            conv_done = p.mark("vec")
            if b == 3:
                self.dbg_dump("h32", h32, conv_done)
            self.sfree["xsl"] = p.mark("ten")

            p.add("vec", lambda v: v.tensor_copy(
                halo_h[:].rearrange("p (c k) -> p c k", c=NC), hiv(h_in_h)[:, :, 513:516]))
            p.add("vec", lambda v: v.tensor_copy(
                halo_l[:].rearrange("p (c k) -> p c k", c=NC), hiv(h_in_l)[:, :, 513:516]))
            halo_dep = p.mark("vec")

            p.waits("ten", [conv_done, self.pfree["C"], self.pfree["D"]])
            for c in range(NC):
                p.add("ten", lambda tn, c=c: tn.matmul(
                    self.PS_D[0:1, 0:512], self.onescol_sb[:, :], h32v[:, c, :],
                    start=(c == 0), stop=(c == NC - 1)))
            for c in range(NC):
                p.waits("vec", [self.sfree.get("gate32")])
                p.add("vec", lambda v, c=c: v.tensor_tensor(gate32[:, :], h32v[:, c, :], h32v[:, c, :],
                                                            Alu.mult))
                d = p.mark("vec")
                p.waits("ten", [d])
                p.add("ten", lambda tn, c=c: tn.matmul(
                    self.PS_C[0:1, 0:512], self.onescol_sb[:, :], gate32[:, :],
                    start=(c == 0), stop=(c == NC - 1)))
                self.sfree["gate32"] = p.mark("ten")
            d = p.mark("ten")
            p.wait("vec", d)
            p.add("vec", lambda v: v.tensor_copy(srow2[:, :], self.PS_D[0:1, 0:512]))
            p.add("vec", lambda v: v.tensor_copy(srow2b[:, :], self.PS_C[0:1, 0:512]))
            d = p.mark("vec")
            self.pfree["C"] = d
            self.pfree["D"] = d
            p.waits("ten", [d, self.pfree["A0"], self.pfree["A1"]])
            p.add("ten", lambda tn: tn.matmul(self.PS_A[:, 0:512], self.ones_sb[:, :], srow2[:, :],
                                              start=True, stop=True))
            p.add("ten", lambda tn: tn.matmul(self.PS_A[:, 512:1024], self.ones_sb[:, :], srow2b[:, :],
                                              start=True, stop=True))
            d = p.mark("ten")
            p.wait("vec", d)
            p.add("vec", lambda v: v.tensor_scalar_mul(murep[:, :], self.PS_A[:, 0:512], 1.0 / I))
            p.add("vec", lambda v: v.tensor_scalar_mul(rrep[:, :], self.PS_A[:, 512:1024], 1.0 / I))
            d = p.mark("vec")
            self.pfree["A0"] = d
            self.pfree["A1"] = d
            p.add("vec", lambda v: v.tensor_tensor(gate32[:, :], murep[:, :], murep[:, :], Alu.mult))
            p.add("vec", lambda v: v.tensor_tensor(rrep[:, :], rrep[:, :], gate32[:, :], Alu.subtract))
            p.add("vec", lambda v: v.tensor_scalar_add(rrep[:, :], rrep[:, :], 1e-5))
            d = p.mark("vec")
            p.wait("scl", d)
            p.add("scl", lambda s: s.activation(rrep[:, :], rrep[:, :], Act.Sqrt))
            d = p.mark("scl")
            p.wait("vec", d)
            p.add("vec", lambda v: v.reciprocal(rrep[:, :], rrep[:, :]))
            stat_dep = p.mark("vec")

            u_cmv = u_cm.rearrange("p (c s) -> p c s", c=NC)
            p.waits("vec", [halo_dep, stat_dep])
            for c in range(NC):
                p.add("vec", lambda v, c=c: v.tensor_tensor(u_cmv[:, c, :], h32v[:, c, :],
                                                            murep[:, :], Alu.subtract))
                p.add("vec", lambda v, c=c: v.tensor_tensor(u_cmv[:, c, :], u_cmv[:, c, :],
                                                            rrep[:, :], Alu.mult))
            ucm_dep = p.mark("vec")

            for half in range(2):
                started = [False] * 4
                for i in range(NC):
                    p.waits("scl", [ucm_dep, self.sfree.get("g32")])
                    p.add("scl", lambda s, i=i: s.activation(
                        self.g32[:], u_cmv[:, i, :], Act.Identity,
                        scale=self.bias2[:, i:i + 1],
                        bias=self.bias2[:, 16 + i:17 + i]))
                    d = p.mark("scl")
                    p.wait("vec", d)
                    p.waits("vec", [self.sfree.get("st")])
                    p.add("vec", lambda v: v.tensor_copy(self.sth[:, 0:512], self.g32[:]))
                    p.add("vec", lambda v: v.tensor_tensor(self.stl[:, 0:512], self.g32[:],
                                                           self.sth[:, 0:512], Alu.subtract))
                    hn_dep = p.mark("vec")
                    self.sfree["g32"] = hn_dep
                    free = self.sfree.get("wT")
                    dd = self.load_weight_cols(W["L1_attn_fc_w"][half * 512:(half + 1) * 512, :],
                                               512, i * 128, 128)
                    wdep = self.transpose_weight(0, 512, dd, extra=(free,))
                    p.waits("ten", [wdep, hn_dep])
                    if i == 0:
                        p.waits("ten", [self.pfree["A0"], self.pfree["A1"], self.pfree["A2"],
                                        self.pfree["A3"]])
                    for oc in range(4):
                        ps = self.PS_A[:, oc * 512:(oc + 1) * 512]
                        self.mm3(ps, None, None, None, None,
                                 first=(i == 0), last=(i == NC - 1),
                                 passes=[(self.wT_h[:, oc * 128:(oc + 1) * 128], self.sth[:, 0:512]),
                                         (self.wT_l[:, oc * 128:(oc + 1) * 128], self.sth[:, 0:512]),
                                         (self.wT_h[:, oc * 128:(oc + 1) * 128], self.stl[:, 0:512])])
                    d = p.mark("ten")
                    self.sfree["wT"] = d
                    self.sfree["st"] = d
                d = p.mark("ten")
                p.wait("scl", d)
                for oc in range(4):
                    og = half * 4 + oc
                    p.waits("scl", [self.sfree.get("g32")])
                    p.add("scl", lambda s, oc=oc, og=og: s.activation(
                        self.g32[:], self.PS_A[:, oc * 512:(oc + 1) * 512], Act.Silu,
                        bias=self.bias2[:, 64 + og:65 + og]))
                    d2 = p.mark("scl")
                    p.wait("vec", d2)
                    p.waits("vec", [self.sfree.get("m1")])
                    p.add("vec", lambda v, og=og: v.tensor_copy(
                        m1_h[:, og * 512:(og + 1) * 512], self.g32[:]))
                    p.add("vec", lambda v, og=og: v.tensor_tensor(
                        m1_l[:, og * 512:(og + 1) * 512], self.g32[:], m1_h[:, og * 512:(og + 1) * 512],
                        Alu.subtract))
                    d2 = p.mark("vec")
                    self.sfree["g32"] = d2
                    self.pfree[f"A{oc}"] = d2
            m1_dep = p.mark("vec")

            for c in range(NC):
                slot = c % 2
                free = self.sfree.get("wT")
                dd = self.load_weight_rows(W["L1_attn_cproj_w"][c * 128:(c + 1) * 128, :], 1024)
                wdep = self.transpose_weight(slot * 1024, 1024, dd, extra=(free,))
                bank = c % 4
                p.waits("ten", [wdep, m1_dep, self.pfree[f"A{bank}"]])
                ps = self.PS_A[:, bank * 512:(bank + 1) * 512]
                for oc in range(ND):
                    self.mm3(ps, self.wT_h[:, slot * 1024 + oc * 128:slot * 1024 + (oc + 1) * 128],
                             self.wT_l[:, slot * 1024 + oc * 128:slot * 1024 + (oc + 1) * 128],
                             m1_h[:, oc * 512:(oc + 1) * 512], m1_l[:, oc * 512:(oc + 1) * 512],
                             first=(oc == 0), last=(oc == ND - 1))
                d = p.mark("ten")
                self.sfree["wT"] = d
                p.wait("vec", d)
                p.waits("vec", [self.sfree.get("g32")])
                p.add("vec", lambda v, c=c, ps=ps: v.scalar_tensor_tensor(
                    self.g32[:], ps, self.bias2[:, 48 + c:49 + c], h32v[:, c, :],
                    Alu.add, Alu.add))
                h2v_h = h2_h.rearrange("p (c s) -> p c s", c=NC)
                h2v_l = h2_l.rearrange("p (c s) -> p c s", c=NC)
                p.add("vec", lambda v, c=c, h2v_h=h2v_h: v.tensor_copy(h2v_h[:, c, :], self.g32[:]))
                p.add("vec", lambda v, c=c, h2v_h=h2v_h, h2v_l=h2v_l: v.tensor_tensor(
                    h2v_l[:, c, :], self.g32[:], h2v_h[:, c, :], Alu.subtract))
                d = p.mark("vec")
                self.sfree["g32"] = d
                self.pfree[f"A{bank}"] = d
            h2_dep = p.mark("vec")
            h2v_h = h2_h.rearrange("p (c s) -> p c s", c=NC)
            h2v_l = h2_l.rearrange("p (c s) -> p c s", c=NC)

            for half in range(2):
                started = False
                for c in range(NC):
                    free = self.sfree.get("wT")
                    dd = self.load_weight_cols(W["L1_attn_out_w"][half * 512:(half + 1) * 512, :],
                                               512, c * 128, 128)
                    wdep = self.transpose_weight(0, 512, dd, extra=(free,))
                    p.waits("ten", [wdep, h2_dep])
                    if not started:
                        p.waits("ten", [self.pfree["A0"], self.pfree["A1"], self.pfree["A2"],
                                        self.pfree["A3"]])
                        started = True
                    for tt in range(4):
                        ps = self.PS_A[:, tt * 512:(tt + 1) * 512]
                        self.mm3(ps, h2v_h[:, c, tt * 128:(tt + 1) * 128],
                                 h2v_l[:, c, tt * 128:(tt + 1) * 128],
                                 self.wT_h[:, 0:512], self.wT_l[:, 0:512],
                                 first=(c == 0), last=(c == NC - 1))
                    d = p.mark("ten")
                    self.sfree["wT"] = d
                d = p.mark("ten")
                p.wait("vec", d)
                for tt in range(4):
                    t_g = 4 * b + tt
                    xsl_ = self.x_sb[:, t_g * D + half * 512:t_g * D + (half + 1) * 512]
                    p.add("vec", lambda v, xsl_=xsl_, tt=tt: v.tensor_tensor(
                        xsl_, xsl_, self.PS_A[:, tt * 512:(tt + 1) * 512], Alu.add))
                d = p.mark("vec")
                for k in ("A0", "A1", "A2", "A3"):
                    self.pfree[k] = d
            self.sfree["hreg"] = p.mark("ten")
            self.sfree["m1"] = self.sfree["hreg"]


    def l1_ffn(self):
        p = self.p
        W = self.W
        cd = None
        for t_, c_ in ((self.irev_sb, self.irev_c), (self.ecol_sb, self.ecol_c),
                       (self.g8_sb, self.g8_c), (self.g8t_sb, self.g8t_c)):
            cd = p.dma("syn", out=t_[:], in_=c_.ap(), q="DRR")

        spill = None
        for t in range(NT):
            spill = p.dma("syn", out=self.x_scr[:, t * D:(t + 1) * D],
                          in_=self.x_sb[:, t * D:(t + 1) * D], q="DSS")
        gd = p.dma("syn", out=self.gb_fm[:, 0:ND],
                   in_=W["L1_ffn_g1"].rearrange("(c p) -> p c", p=128), slow=True, q="DRR")
        p.wait("scl", gd)
        self.stats_pass()
        self.ln_scalars("lnln", eps=1e-5, eps2=1e-6)
        xn_dep = self.transpose_x32(self.xnT32_scr, gfm=self.gb_fm[:, 0:ND])
        x32a = self.xsl32a.rearrange("p (c s) -> p c s", c=ND)
        p.waits("scl", [spill, xn_dep])
        for t in range(NT):
            xs = self.x_sb[:, t * D:(t + 1) * D]
            p.add("scl", lambda s, xs=xs, t=t: s.activation(
                xs, xs, Act.Identity, scale=self.r1s[:, t:t + 1], bias=self.mb1s[:, t:t + 1]))
        zt = p.mark("scl")
        p.waits("vec", [zt])
        p.waits("ten", [zt])
        p.add("vec", lambda v: v.memset(
            self.vta.rearrange("p (t h c) -> p t h c", h=2, c=65)[:, :, :, 64:65], 1.0))
        vta1 = p.mark("vec")

        rope_mask = [i_ + 1 if i_ % 2 == 0 else i_ - 1 for i_ in range(32)]
        qkv = W["L1_ffn_w_qkv"]
        for g in range(8):
            for wi, coff in ((0, 0), (1, D), (2, 2 * D)):
                p.waits("syn", [self.sfree.get("wld")])
                d1 = None
                for r in range(ND):
                    slc = qkv[r * 128:(r + 1) * 128, coff + g * 128:coff + (g + 1) * 128]
                    d1 = p.dma("syn", out=self.wld[:, r * 128:(r + 1) * 128], in_=slc, q="DWT")
                for tc in range(4):
                    slab_dep = self.load_xslab32a(tc)
                    bank = tc % 4
                    p.waits("ten", [slab_dep, d1, xn_dep, self.pfree[f"A{bank}"]])
                    ps = self.PS_A[:, bank * 512:(bank + 1) * 512]
                    for c in range(ND):
                        p.add("ten", lambda tn, ps=ps, c=c: tn.matmul(
                            ps, self.wld[:, c * 128:(c + 1) * 128], x32a[:, c, :],
                            start=(c == 0), stop=(c == ND - 1)))
                    d = p.mark("ten")
                    if wi == 2 and tc == 3:
                        self.sfree["wld"] = d
                    self.sfree["xsl32"] = d
                    p.wait("scl", d)
                    p.waits("scl", [self.sfree.get("g32"), self.sfree.get("qkv_free")])
                    scale = 0.125 if wi == 0 else 1.0
                    p.add("scl", lambda s, ps=ps, scale=scale: s.activation(
                        self.g32[:], ps, Act.Identity, scale=scale))
                    d = p.mark("scl")
                    self.pfree[f"A{bank}"] = d
                    if wi < 2:
                        p.waits("syn", [self.sfree.get("u32"), ("DB4", p.ctr["DB4"])])
                        c1d = p.dma("syn", out=self.u32[:, 0:512],
                                    in_=self.rope1_c.ap()[:, tc * 512:(tc + 1) * 512], q="DB4")
                        c2d = p.dma("syn", out=self.u32[:, 512:1024],
                                    in_=self.rope2_c.ap()[:, tc * 512:(tc + 1) * 512], q="DB4")
                        p.waits("vec", [d, c1d, c2d, self.sfree.get("pt0"),
                                        self.sfree.get("pt1")])
                        dst = self.qt32 if wi == 0 else self.kt32
                        dsl = dst[:, tc * 512:(tc + 1) * 512]
                        p.add("vec", lambda v: v.stream_shuffle(
                            self.p32a[:, :], self.g32[:], rope_mask))
                        p.add("vec", lambda v: v.tensor_tensor(
                            self.p32b[:, :], self.g32[:], self.u32[:, 0:512], Alu.mult))
                        p.add("vec", lambda v: v.tensor_tensor(
                            self.p32a[:, :], self.p32a[:, :], self.u32[:, 512:1024], Alu.mult))
                        p.add("vec", lambda v, dsl=dsl: v.tensor_tensor(
                            dsl, self.p32b[:, :], self.p32a[:, :], Alu.add))
                        d = p.mark("vec")
                        self.sfree["u32"] = d
                        self.sfree["g32"] = d
                    else:
                        self.v_transpose_to_vta(tc, d, vta1)
            qkv_dep = p.mark("vec")
            qk_dep = p.mark("scl")

            att_d = None
            for hh in range(2):
                nb = self.negb_head(hh, qkv_dep=(qk_dep if hh == 0 else None))
                p.waits("ten", [qkv_dep, qk_dep, nb])
                att_d = self.attn_head32(hh, False, qkv_dep, nb)
            self.sfree["qkv_free"] = att_d
            p.waits("syn", [self.sfree.get("wld")])
            wo = W["L1_ffn_w_o"]
            d1 = p.dma("syn", out=self.wld[:, 0:1024], in_=wo[g * 128:(g + 1) * 128, :], q="DWT")
            p.waits("ten", [att_d, d1] + self._og_dmas)
            for tq in range(NT):
                for oc in range(2):
                    bank = (2 * tq + oc) % 2
                    p.waits("ten", [self.pfree[f"A{bank}"]])
                    ps = self.PS_A[:, bank * 512:(bank + 1) * 512]
                    p.add("ten", lambda tn, ps=ps, tq=tq, oc=oc: tn.matmul(
                        ps, self.og32[:, tq * 128:(tq + 1) * 128],
                        self.wld[:, oc * 512:(oc + 1) * 512], start=True, stop=True))
                    d = p.mark("ten")
                    if tq == NT - 1 and oc == 1:
                        self.sfree["wld"] = d
                    p.wait("vec", d)
                    xsl_ = self.x_sb[:, tq * D + oc * 512:tq * D + (oc + 1) * 512]
                    p.add("vec", lambda v, xsl_=xsl_, ps=ps: v.tensor_tensor(xsl_, xsl_, ps, Alu.add))
                    self.pfree[f"A{bank}"] = p.mark("vec")
            self.sfree["og"] = p.mark("ten")

        p.barrier()
        self.dbg_dump("xattn", self.x_sb[:])
        hb32 = self._av(0, 32768, F32)
        wld2 = self._av(75776, 4096, F32)

        gtile = self._av(32768, 8192, F32)
        etile = self._av(40960, 8192, F32)
        lgT = self._av(49152, 8192, F32, parts=8)
        probs = self._av(57344, 512, F32)
        m1c = self._av(57856, 64, F32)
        m2c = self._av(57920, 64, F32)
        e1c = self._av(57984, 64, F32)
        e2c = self._av(58048, 64, F32)
        smax = self._av(58112, 4, F32)
        ssum = self._av(58116, 4, F32)
        locol = self._av(58120, 4, F32, parts=8)
        hicol = self._av(58124, 4, F32, parts=8)
        midcol = self._av(58128, 4, F32, parts=8)
        gecol = self._av(58132, 4, F32, parts=8)
        cntcol = self._av(58136, 4, F32)
        thcol = self._av(58140, 4, F32)
        throw = self._av(58144, 32, F32, parts=1)
        threp = self._av(58176, 32, F32)
        thsel1 = self._av(58240, 64, F32)
        thsel2 = self._av(58304, 64, F32)
        mke1 = self._av(58368, 64, F32)
        mke2 = self._av(58432, 64, F32)
        ce_all = self._av(58496, 512, F32)
        bis_scr = self._av(0, 4096)

        gd = p.dma("syn", out=self.gb_fm[:, 0:ND],
                   in_=W["L1_ffn_g2"].rearrange("(c p) -> p c", p=128), slow=True, q="DRR")
        p.wait("scl", gd)
        self.stats_pass()
        self.ln_scalars("rms", eps=1e-6)
        xn_dep = self.transpose_x32(self.xnT32_scr, gfm=self.gb_fm[:, 0:ND])
        x32v = self.xsl32.rearrange("p (c s) -> p c s", c=ND)

        p.waits("syn", [self.sfree.get("wld")])
        rdep = p.dma("syn", out=self.wld[:, 0:64].rearrange("p (c o) -> p c o", c=8),
                     in_=W["L1_ffn_w_router"].rearrange("(c p) o -> p c o", p=128), q="DWT")
        for tc in range(4):
            slab_dep = self.load_xslab32(tc)
            p.waits("ten", [slab_dep, rdep, xn_dep, self.pfree["C"]])
            for c in range(ND):
                p.add("ten", lambda tn, c=c: tn.matmul(
                    self.PS_C[0:8, 0:512], self.wld[:, c * 8:(c + 1) * 8],
                    x32v[:, c, :], start=(c == 0), stop=(c == ND - 1)))
            d = p.mark("ten")
            self.sfree["xsl32"] = d
            p.wait("vec", d)
            p.add("vec", lambda v, tc=tc: v.tensor_copy(lgT[:, tc * 512:(tc + 1) * 512],
                                                        self.PS_C[0:8, 0:512]))
            self.pfree["C"] = p.mark("vec")
            self.sfree["wld"] = d
        lg_dep = self.pfree["C"]

        p.waits("ten", [lg_dep, self.pfree["T"]])
        TPSF = self.T_PS[:].bitcast(F32)
        prev_d = None
        for t in range(NT):
            p.waits("ten", [prev_d])
            p.add("ten", lambda tn, t=t: tn.transpose(
                TPSF[:, (t % 8) * 8:(t % 8) * 8 + 8],
                lgT[:, t * 128:(t + 1) * 128], self.identf_sb[0:8, 0:8]))
            d = p.mark("ten")
            p.wait("vec", d)
            P8 = probs[:, t * 8:(t + 1) * 8]
            p.add("vec", lambda v, t=t, P8=P8: v.tensor_copy(
                P8, TPSF[:, (t % 8) * 8:(t % 8) * 8 + 8]))
            p.add("vec", lambda v, P8=P8: v.reduce_max(smax[:, 0:1], P8, axis=AX.X, negate=True))
            d = p.mark("vec")
            p.wait("scl", d)
            p.add("scl", lambda s, P8=P8: s.activation(P8, P8, Act.Exp, bias=smax[:, 0:1],
                                                       accum_out=ssum[:, 0:1]))
            d = p.mark("scl")
            p.wait("vec", d)
            p.add("vec", lambda v: v.reciprocal(ssum[:, 0:1], ssum[:, 0:1]))
            p.add("vec", lambda v, P8=P8: v.tensor_scalar_mul(P8, P8, ssum[:, 0:1]))
            u8a = self.u32[:, 0:8]
            u8b = self.u32[:, 8:16]
            u1 = self.u32[:, 16:17]
            p.add("vec", lambda v, P8=P8, t=t: v.reduce_max(m1c[:, t:t + 1], P8, axis=AX.X))
            p.add("vec", lambda v, P8=P8, t=t, u8a=u8a: v.tensor_scalar(
                u8a, P8, m1c[:, t:t + 1], None, Alu.is_equal))
            p.add("vec", lambda v, u8a=u8a, u8b=u8b: v.tensor_tensor(
                u8b, u8a, self.irev_sb[:], Alu.mult))
            p.add("vec", lambda v, u8b=u8b, u1=u1: v.reduce_max(u1, u8b, axis=AX.X))
            p.add("vec", lambda v, t=t, u1=u1: v.tensor_scalar(
                e1c[:, t:t + 1], u1, -1.0, 8.0, Alu.mult, Alu.add))
            p.add("vec", lambda v, u8a=u8a, u1=u1: v.tensor_scalar(
                u8a, self.irev_sb[:], u1, None, Alu.is_equal))
            p.add("vec", lambda v, P8=P8, u8a=u8a, u8b=u8b: v.scalar_tensor_tensor(
                u8b, u8a, -1.0e30, P8, Alu.mult, Alu.add))
            p.add("vec", lambda v, u8b=u8b, t=t: v.reduce_max(m2c[:, t:t + 1], u8b, axis=AX.X))
            p.add("vec", lambda v, u8a=u8a, u8b=u8b, t=t: v.tensor_scalar(
                u8a, u8b, m2c[:, t:t + 1], None, Alu.is_equal))
            p.add("vec", lambda v, u8a=u8a: v.tensor_tensor(
                u8a, u8a, self.irev_sb[:], Alu.mult))
            p.add("vec", lambda v, u8a=u8a, u1=u1: v.reduce_max(u1, u8a, axis=AX.X))
            p.add("vec", lambda v, t=t, u1=u1: v.tensor_scalar(
                e2c[:, t:t + 1], u1, -1.0, 8.0, Alu.mult, Alu.add))
            prev_d = p.mark("vec")
            self.sfree["u32"] = prev_d
        top2_dep = p.mark("vec")

        for k, colsrc in ((0, m1c), (1, m2c)):
            p.dma("syn", out=self.m_loc[k].rearrange("(t q) -> q t", q=128), in_=colsrc[:, 0:16],
                  q="DB4", slow=True)
        for k, colsrc in ((0, e1c), (1, e2c)):
            cd2 = p.dma("syn", out=self.e_loc[k].rearrange("(t q) -> q t", q=128), in_=colsrc[:, 0:16],
                        q="DB4", slow=True)
        p.waits("syn", [top2_dep])
        p.waits("gps", [cd2])
        p.add("gps", lambda g_: g_.collective_compute(
            "AllGather", Alu.bypass, [[0, 1, 2, 3, 4, 5, 6, 7]],
            [self.m_loc[:, :]], [self.m_all[:, :]]))
        agd = p.mark("gps")
        p.add("gps", lambda g_: g_.collective_compute(
            "AllGather", Alu.bypass, [[0, 1, 2, 3, 4, 5, 6, 7]],
            [self.e_loc[:, :]], [self.e_all[:, :]]))
        agd = p.mark("gps")

        p.waits("syn", [agd])
        gd = ed = None
        for e_ in range(8):
            gd = p.dma("syn", out=gtile[16 * e_:16 * e_ + 16, :], in_=self.m_all[:, :], q="DSS")
            ed = p.dma("syn", out=etile[16 * e_:16 * e_ + 16, :], in_=self.e_all[:, :], q="DSS")
        p.waits("vec", [gd, ed])
        p.add("vec", lambda v: v.tensor_scalar(etile[:, :], etile[:, :], self.ecol_sb[:, 0:1],
                                               None, Alu.is_equal))
        p.add("vec", lambda v: v.tensor_tensor(gtile[:, :], gtile[:, :], etile[:, :], Alu.mult))
        p.add("vec", lambda v: v.tensor_scalar(etile[:, :], etile[:, :], 1.0, 1.0e30,
                                               Alu.subtract, Alu.mult))
        p.add("vec", lambda v: v.tensor_tensor(gtile[:, :], gtile[:, :], etile[:, :], Alu.add))
        p.add("vec", lambda v: v.memset(locol[:, 0:1], 0.0))
        p.add("vec", lambda v: v.memset(hicol[:, 0:1], 1.0))
        gt_dep = p.mark("vec")
        for it in range(40):
            p.add("vec", lambda v: v.tensor_tensor(midcol[:, 0:1], locol[:, 0:1], hicol[:, 0:1],
                                                   Alu.add))
            p.add("vec", lambda v: v.tensor_scalar_mul(midcol[:, 0:1], midcol[:, 0:1], 0.5))
            d = p.mark("vec")
            p.waits("ten", [d, self.pfree["C"]])
            p.add("ten", lambda tn: tn.matmul(self.PS_C[:, 0:1], self.g8t_sb[:, :],
                                              midcol[:, 0:1], start=True, stop=True))
            d = p.mark("ten")
            p.wait("vec", d)
            p.add("vec", lambda v: v.tensor_copy(thcol[:, 0:1], self.PS_C[:, 0:1]))
            d = p.mark("vec")
            self.pfree["C"] = d
            p.add("vec", lambda v: v.tensor_scalar(
                bis_scr[:, 0:2048], gtile[:, :], thcol[:, 0:1], 0.0, Alu.is_ge,
                op1=Alu.add, accum_out=cntcol[:, 0:1]))
            d = p.mark("vec")
            p.waits("ten", [d, self.pfree["D"]])
            p.add("ten", lambda tn: tn.matmul(self.PS_D[0:8, 0:1], self.g8_sb[:, :],
                                              cntcol[:, 0:1], start=True, stop=True))
            d = p.mark("ten")
            p.wait("vec", d)
            p.add("vec", lambda v: v.tensor_copy(gecol[:, 0:1], self.PS_D[0:8, 0:1]))
            d = p.mark("vec")
            self.pfree["D"] = d
            p.add("vec", lambda v: v.tensor_single_scalar(gecol[:, 0:1], gecol[:, 0:1], 1280.0,
                                                          Alu.is_ge))
            gnot = self.u32[:, 20:21]
            t1 = self.u32[:, 21:22]
            p.add("vec", lambda v, gnot=gnot: v.tensor_scalar(gnot[0:8, :], gecol[:, 0:1], 1.0, -1.0,
                                                              Alu.subtract, Alu.mult))
            p.add("vec", lambda v, t1=t1: v.tensor_tensor(t1[0:8, :], gecol[:, 0:1], midcol[:, 0:1],
                                                          Alu.mult))
            p.add("vec", lambda v, gnot=gnot: v.tensor_tensor(locol[:, 0:1], gnot[0:8, :],
                                                              locol[:, 0:1], Alu.mult))
            p.add("vec", lambda v, t1=t1: v.tensor_tensor(locol[:, 0:1], locol[:, 0:1], t1[0:8, :],
                                                          Alu.add))
            p.add("vec", lambda v, t1=t1: v.tensor_tensor(t1[0:8, :], gecol[:, 0:1], hicol[:, 0:1],
                                                          Alu.mult))
            p.add("vec", lambda v, gnot=gnot: v.tensor_tensor(hicol[:, 0:1], gnot[0:8, :],
                                                              midcol[:, 0:1], Alu.mult))
            p.add("vec", lambda v, t1=t1: v.tensor_tensor(hicol[:, 0:1], hicol[:, 0:1], t1[0:8, :],
                                                          Alu.add))
            d = p.mark("vec")
        p.waits("ten", [d, self.pfree["C"]])
        p.add("ten", lambda tn: tn.transpose(self.PS_C[0:1, 0:8], locol[:, 0:1],
                                             self.identf_sb[0:8, 0:8]))
        d = p.mark("ten")
        p.wait("vec", d)
        p.add("vec", lambda v: v.tensor_copy(throw[:, 0:8], self.PS_C[0:1, 0:8]))
        d = p.mark("vec")
        self.pfree["C"] = d
        p.waits("ten", [d, self.pfree["D"]])
        p.add("ten", lambda tn: tn.matmul(self.PS_D[:, 0:8], self.ones_sb[:, :], throw[:, 0:8],
                                          start=True, stop=True))
        d = p.mark("ten")
        p.wait("vec", d)
        p.add("vec", lambda v: v.tensor_copy(threp[:, 0:8], self.PS_D[:, 0:8]))
        d = p.mark("vec")
        self.pfree["D"] = d
        for k, (ec, ts_) in enumerate(((e1c, thsel1), (e2c, thsel2))):
            p.add("vec", lambda v, ts_=ts_: v.memset(ts_[:, 0:16], 0.0))
            for e_ in range(8):
                p.add("vec", lambda v, ec=ec, e_=e_: v.tensor_single_scalar(
                    self.u32[:, 0:16], ec[:, 0:16], float(e_), Alu.is_equal))
                p.add("vec", lambda v, ts_=ts_, e_=e_: v.scalar_tensor_tensor(
                    ts_[:, 0:16], self.u32[:, 0:16], threp[:, e_:e_ + 1], ts_[:, 0:16],
                    Alu.mult, Alu.add))
        p.add("vec", lambda v: v.tensor_tensor(mke1[:, 0:16], m1c[:, 0:16], thsel1[:, 0:16],
                                               Alu.is_ge))
        p.add("vec", lambda v: v.tensor_tensor(mke1[:, 0:16], mke1[:, 0:16], m1c[:, 0:16],
                                               Alu.mult))
        p.add("vec", lambda v: v.tensor_tensor(mke2[:, 0:16], m2c[:, 0:16], thsel2[:, 0:16],
                                               Alu.is_ge))
        p.add("vec", lambda v: v.tensor_tensor(mke2[:, 0:16], mke2[:, 0:16], m2c[:, 0:16],
                                               Alu.mult))
        for e_ in range(8):
            ce = ce_all[:, 16 * e_:16 * e_ + 16]
            p.add("vec", lambda v, ce=ce: v.memset(ce, 0.0))
            for ec, mke in ((e1c, mke1), (e2c, mke2)):
                p.add("vec", lambda v, ec=ec, e_=e_: v.tensor_single_scalar(
                    self.u32[:, 0:16], ec[:, 0:16], float(e_), Alu.is_equal))
                p.add("vec", lambda v, mke=mke: v.tensor_tensor(
                    self.u32[:, 16:32], self.u32[:, 0:16], mke[:, 0:16], Alu.mult))
                p.add("vec", lambda v, ce=ce: v.tensor_tensor(ce, ce, self.u32[:, 16:32], Alu.add))
        coeff_dep = p.mark("vec")
        self.sfree["u32"] = coeff_dep
        self.sfree["hb32"] = coeff_dep
        self.dbg_dump("m1c", m1c[:, 0:16], coeff_dep)
        self.dbg_dump("m2c", m2c[:, 0:16], coeff_dep)
        self.dbg_dump("e1c", e1c[:, 0:16], coeff_dep)
        self.dbg_dump("e2c", e2c[:, 0:16], coeff_dep)
        self.dbg_dump("threp", threp[:, 0:8], coeff_dep)
        self.dbg_dump("ce", ce_all[:, 0:128], coeff_dep)
        self.dbg_dump("probs", probs[:, 0:128], coeff_dep)

        self.sfree["wldA"] = self.sfree.get("wld")
        self.sfree["wldB"] = self.sfree.get("wld")
        wbufs = (self.wld, wld2)
        for e_ in range(8):
            for b in range(4):
                slab_dep = self.load_xslab32(b)
                for fc in range(16):
                    par = fc % 2
                    wbuf = wbufs[par]
                    key = "wldA" if par == 0 else "wldB"
                    qn = "DWT" if par == 0 else "DGH"
                    eng = "syn" if par == 0 else "gps"
                    p.waits(eng, [self.sfree.get(key)])
                    d1 = None
                    for r in range(ND):
                        slc = W["L1_ffn_w1"][e_, r * 128:(r + 1) * 128, fc * 128:(fc + 1) * 128]
                        d1 = p.dma(eng, out=wbuf[:, r * 128:(r + 1) * 128], in_=slc, q=qn)
                    bank = fc % 4
                    p.waits("ten", [slab_dep, d1, self.pfree[f"A{bank}"]])
                    ps = self.PS_A[:, bank * 512:(bank + 1) * 512]
                    for c in range(ND):
                        p.add("ten", lambda tn, c=c, ps=ps, wbuf=wbuf: tn.matmul(
                            ps, wbuf[:, c * 128:(c + 1) * 128], x32v[:, c, :],
                            start=(c == 0), stop=(c == ND - 1)))
                    d = p.mark("ten")
                    self.sfree[key] = d
                    self.sfree["xsl32"] = d
                    p.wait("scl", d)
                    p.waits("scl", [self.sfree.get("hb32")])
                    col = fc * 512
                    p.add("scl", lambda s, ps=ps, col=col: s.activation(
                        hb32[:, col:col + 512], ps, Act.Relu))
                    d = p.mark("scl")
                    self.pfree[f"A{bank}"] = d
                hb_dep = p.mark("scl")
                banks = [self.PS_A[:, i * 512:(i + 1) * 512] for i in range(4)] + \
                        [self.T_PS[:].bitcast(F32)[:, 0:512], self.T_PS[:].bitcast(F32)[:, 512:1024],
                         self.PS_C[:, 0:512], self.PS_D[:, 0:512]]
                p.waits("ten", [hb_dep, self.pfree["A0"], self.pfree["A1"], self.pfree["A2"],
                                self.pfree["A3"], self.pfree["T"], self.pfree["C"], self.pfree["D"]])
                for fc in range(16):
                    par = fc % 2
                    wbuf = wbufs[par]
                    key = "wldA" if par == 0 else "wldB"
                    qn = "DWT" if par == 0 else "DGH"
                    eng = "syn" if par == 0 else "gps"
                    p.waits(eng, [self.sfree.get(key)])
                    slc = W["L1_ffn_w2"][e_, fc * 128:(fc + 1) * 128, :]
                    d1 = p.dma(eng, out=wbuf[:, 0:1024], in_=slc, q=qn)
                    p.waits("ten", [d1])
                    for tt in range(4):
                        hh_ = hb32[:, fc * 512 + tt * 128:fc * 512 + (tt + 1) * 128]
                        for oc in range(2):
                            p.add("ten", lambda tn, hh_=hh_, oc=oc, fc=fc, wbuf=wbuf,
                                  bk=banks[tt * 2 + oc]:
                                  tn.matmul(bk, hh_, wbuf[:, oc * 512:(oc + 1) * 512],
                                            start=(fc == 0), stop=(fc == 15)))
                    d = p.mark("ten")
                    self.sfree[key] = d
                d = p.mark("ten")
                self.sfree["hb32"] = d
                p.wait("vec", d)
                p.waits("vec", [coeff_dep])
                for tt in range(4):
                    t_g = 4 * b + tt
                    for oc in range(2):
                        xsl_ = self.x_sb[:, t_g * D + oc * 512:t_g * D + (oc + 1) * 512]
                        p.add("vec", lambda v, xsl_=xsl_, tt=tt, oc=oc, e_=e_, t_g=t_g:
                              v.scalar_tensor_tensor(
                                  xsl_, banks[tt * 2 + oc],
                                  ce_all[:, 16 * e_ + t_g:16 * e_ + t_g + 1],
                                  xsl_, Alu.mult, Alu.add))
                d = p.mark("vec")
                for kk in ("A0", "A1", "A2", "A3", "T", "C", "D"):
                    self.pfree[kk] = d
        for t in range(NT):
            p.waits("syn", [self.sfree.get("u32")])
            ld = p.dma("syn", out=self.u32[:], in_=self.x_scr[:, t * D:(t + 1) * D], q="DSL")
            p.waits("vec", [ld])
            xs = self.x_sb[:, t * D:(t + 1) * D]
            p.add("vec", lambda v, xs=xs: v.tensor_tensor(xs, xs, self.u32[:], Alu.add))
            self.sfree["u32"] = p.mark("vec")

    def finalize(self):
        p = self.p
        p.barrier()
        for t in range(NT):
            p.dma("syn", out=self.xout[t * 128:(t + 1) * 128, :],
                  in_=self.x_sb[:, t * D:(t + 1) * D], q="DO")
        p.ops["syn"].append(lambda e: e.wait_ge(p.sem["DO"], p.ctr["DO"]))

        with ExitStack() as sctx:
            for s in Prog.SEMS:
                p.sem[s] = sctx.enter_context(self.nc.semaphore("sem_" + s))
            with self.nc.Block() as block:
                p.emit(block)
        self.ctx.close()
        return self.nc

    def build(self):
        self.init_loads()
        if "l0a" in self.stages:
            self.l0_attn()
        if "l0f" in self.stages:
            self.p.barrier()
            self.l0_ffn()
        if "l1a" in self.stages:
            self.p.barrier()
            self.l1_attn()
        if "l1f" in self.stages:
            self.p.barrier()
            self.l1_ffn()
        return self.finalize()


_NC_CACHE = {}


def _get_program(stages):
    key = tuple(stages)
    if key not in _NC_CACHE:
        _NC_CACHE[key] = Builder(stages).build()
    return _NC_CACHE[key]


def kernel(**inputs):
    nc = _get_program(STAGES)
    x = np.ascontiguousarray(np.asarray(inputs["L0_attn_x"], dtype=np.float32))
    wmap = {k: np.ascontiguousarray(np.asarray(v, dtype=np.float32))
            for k, v in inputs.items() if k != "L0_attn_x"}
    in_maps = []
    for c in range(8):
        m = {"x_in": x[c]}
        m.update(wmap)
        in_maps.append(m)
    res = run_bass_kernel_spmd(nc, in_maps, core_ids=list(range(8)))
    return np.stack([r["x_out"] for r in res.results], axis=0)
